# version 04
# baseline (speedup 1.0000x reference)
import numpy as np
import ml_dtypes
import concourse.bass as bass
import concourse.mybir as mybir
from concourse.bass_utils import run_bass_kernel_spmd

F32 = mybir.dt.float32
F32R = mybir.dt.float32r
BF16 = mybir.dt.bfloat16
AF = mybir.ActivationFunctionType
ALU = mybir.AluOpType
AX = mybir.AxisListType

ENGINES = ("tensor", "vector", "scalar", "gpsimd", "sync")
OPT_NO_H2D = False
OPT_SER = ""
OPT_NONCE = 0
OPT_P1_STOP = 99
OPT_NG = 99
FCONST_VALS = [1e-5, 1e-6, -0.5, -1.0, 0.5, 1.0 / 128.0, 1.0 / 1024.0,
               -1.0e30, 2.0, 3.0, 0.25, -2.0, 8.0, 16.0, 1.5, -1.5]
D = 1024
NDJ = 8
NH = 8
NE = 16
CHK = 128


class Prog:
    def __init__(self, nc):
        self.nc = nc
        self.streams = {e: [] for e in ENGINES}
        names = ["pe", "ve", "se", "ge", "ldy", "ldg", "st", "sga", "sgb", "ldh"]
        self.sem = {n: nc.alloc_semaphore("sem_" + n) for n in names}
        self.cnt = {n: 0 for n in names}
        self.engsem = {"tensor": "pe", "vector": "ve", "scalar": "se", "gpsimd": "ge"}

    def do(self, eng, fn, waits=(), tick=False):
        name = self.engsem.get(eng)
        if getattr(self, "serialize", False) and name is not None:
            waits = list(waits) + [(k, self.cnt[k]) for k in
                                   ("pe", "ve", "se", "ge") if k != name]
            tick = True
        tn = None
        if tick:
            self.cnt[name] += 1
            tn = name
        self.streams[eng].append((fn, [w for w in waits if w and w[1] > 0],
                                  self.sem[tn] if tn else None, 1))
        return self.cnt[name] if tick else None

    def dma(self, eng, out, in_, waits=(), sem=None):
        if sem is None:
            sem = "ldg" if eng == "gpsimd" else "ldy"
        self.cnt[sem] += 16
        self.streams[eng].append(
            ((lambda e, o=out, i=in_: e.dma_start(out=o, in_=i)),
             [w for w in waits if w and w[1] > 0], self.sem[sem], 16))
        return self.cnt[sem]

    def wait(self, eng, waits):
        self.streams[eng].append((None, [w for w in waits if w and w[1] > 0], None, 0))

    def barrier(self):
        snap = dict(self.cnt)
        for eng in ENGINES:
            own = self.engsem.get(eng)
            ws = [(k, v) for k, v in snap.items() if v > 0 and k != own]
            self.wait(eng, ws)

    def emit(self, block):
        for eng in ENGINES:
            stream = self.streams[eng]
            if not stream:
                continue

            def make(stream):
                def body(e):
                    for fn, waits, sem, n in stream:
                        for (wn, wv) in waits:
                            e.wait_ge(self.sem[wn], wv)
                        if fn is not None:
                            ins = fn(e)
                            if sem is not None:
                                ins.then_inc(sem, n)
                return body
            getattr(block, eng)(make(stream))


def build_program(T=2048, phases=(1, 1, 1, 1), dbg=False):
    CNC = T // CHK
    NHALF = 2 if T >= 1024 else 1
    TH = T // NHALF
    TCW = min(512, TH)
    NTCH = TH // TCW
    NTT = TH // 128
    GRP = 2
    NW1 = max(1, TH // 512)
    W1W = min(512, TH)
    P2W = 256 if T >= 512 else 128
    NP2 = T // P2W

    nc = bass.Bass("TRN2", target_bir_lowering=False, debug=False,
                   dynamic_dma_scratch_size=2048,
                   detect_race_conditions=False)
    P = Prog(nc)

    dr = {}
    def dram_in(name, shape, dt=F32):
        dr[name] = nc.dram_tensor(name, shape, dt, kind="ExternalInput").ap()
        return dr[name]

    dram_in("x", [128, NDJ * T])
    for w in ["wm", "w1t", "w2r", "wo", "qw", "kw", "vw", "gw", "ow"]:
        dram_in(w, [128, 8192])
    dram_in("fcw", [128, NDJ * 4096])
    dram_in("pw", [128, 32 * 1024])
    dram_in("bwaw", [128, 128])
    dram_in("emb", [128, NE])
    dram_in("b2", [16, 128])
    dram_in("b1c", [128, 64])
    dram_in("bmc", [128, 8])
    dram_in("boc", [128, 8])
    dram_in("ln1c", [128, 8])
    dram_in("ln2c", [128, 8])
    dram_in("alog", [1, 8])
    dram_in("dtb", [1, 8])
    dram_in("onormr", [1, 128])
    dram_in("identc", [128, 128])
    dram_in("suc", [128, 128])
    dram_in("uic", [128, 128])
    dram_in("msc", [128, 128])
    dram_in("mstc", [128, 128])
    dram_in("onesc", [128, 1])
    dram_in("onesr", [128, 128])
    dram_in("fconst", [128, 16])

    y = nc.dram_tensor("y", [128, NDJ * T], F32, kind="ExternalOutput").ap()
    h2d = nc.dram_tensor("h2d", [128, NDJ * T], BF16, kind="Internal").ap()

    RES = nc.alloc_sbuf_tensor("RES", [128, NDJ * T], F32)
    WBF = nc.alloc_sbuf_tensor("WBF", [128, 32768], BF16)
    AB = nc.alloc_sbuf_tensor("AB", [128, 24576], BF16)
    AFa = nc.alloc_sbuf_tensor("AFa", [128, 3904], F32)
    STG = nc.alloc_sbuf_tensor("STG", [128, 2048], F32)
    IDB = nc.alloc_sbuf_tensor("IDB", [128, 128], BF16)
    SU = nc.alloc_sbuf_tensor("SUc", [128, 128], F32)
    UI = nc.alloc_sbuf_tensor("UIc", [128, 128], F32)
    MSK = nc.alloc_sbuf_tensor("MSc", [128, 128], F32)
    MST = nc.alloc_sbuf_tensor("MSTc", [128, 128], F32)
    ONESC = nc.alloc_sbuf_tensor("ONESCc", [128, 1], F32)
    ONESR_R = nc.alloc_sbuf_tensor("ONESR_R", [128, 128], F32R)
    CSM = nc.alloc_sbuf_tensor("CSM", [128, 128], F32)
    SMALL = nc.alloc_sbuf_tensor("SMALL", [128, 16], F32)
    ONB = nc.alloc_sbuf_tensor("ONB", [128, 128], F32)
    B2B = nc.alloc_sbuf_tensor("B2B", [16, 128], BF16)
    EMB = nc.alloc_sbuf_tensor("EMBB", [128, NE], BF16)
    ROW8 = nc.alloc_sbuf_tensor("ROW8", [1, 64], F32)
    ROW8R = nc.alloc_sbuf_tensor("ROW8R", [1, 64], F32R)
    RWS = nc.alloc_sbuf_tensor("RWS", [128, 512], F32)
    RWR = nc.alloc_sbuf_tensor("RWR", [128, 512], F32R)
    TOTR = nc.alloc_sbuf_tensor("TOTR", [1, 128], F32)
    TOTRR = nc.alloc_sbuf_tensor("TOTRR", [1, 128], F32R)
    FCON = nc.alloc_sbuf_tensor("FCON", [128, 16], F32)
    SEPT = nc.alloc_sbuf_tensor("SEPT", [128, 2], F32)
    S0 = nc.alloc_sbuf_tensor("S0", [128, NH * 128], F32)
    SC8 = nc.alloc_sbuf_tensor("SC8", [128, GRP * 80 + 16], F32)
    WTT = nc.alloc_sbuf_tensor("WTT", [128, NTT * 128], F32)
    WTB = nc.alloc_sbuf_tensor("WTB", [128, NTT * 128], BF16)
    WFMH = nc.alloc_sbuf_tensor("WFMH", [16, NTT * 128], BF16)

    A0 = dict(ZBF=0, HBF=NDJ * TCW, HMID=NDJ * TCW + NDJ * TH,
              CMBF=NDJ * TCW + NDJ * TH + 8 * TH)
    def cmbf_col(h):
        return h * TH if h < 4 else A0["CMBF"] + (h - 4) * TH
    assert A0["CMBF"] + 4 * TH <= 24576
    assert 4 * TH <= NDJ * TCW
    gq = GRP * 1024
    A1 = dict(OWB=0, BWA=8192, HB1=8320, QN=8320 + gq, KN=8320 + 2 * gq,
              VN=8320 + 3 * gq, GN=8320 + 4 * gq, SCN=8320 + 5 * gq)
    assert A1["SCN"] + 33 * 128 <= 24576
    A2 = dict(FFA=0, H2L=32 * P2W, FCR=32 * P2W + NDJ * P2W)
    assert A2["FCR"] + 4 * 1024 <= 24576
    F0 = dict(SCR=64, CMBT=64 + 2 * TCW)
    assert F0["CMBT"] + NTT * 128 <= 3904
    F1 = dict(QS=64, XET=64 + 2048, OTM=64 + 2048 + 128, X1P=64 + 2048 + 256,
              SCR2=64 + 2048 + 256 + 1024)
    assert F1["SCR2"] + 512 <= 3904

    PS01 = nc.alloc_psum_tensor("PS01", [128, 1024], F32)
    PS23 = nc.alloc_psum_tensor("PS23", [128, 1024], F32)
    PS4 = nc.alloc_psum_tensor("PS4", [128, 512], F32)
    PS5 = nc.alloc_psum_tensor("PS5", [128, 512], F32)
    PS6 = nc.alloc_psum_tensor("PS6", [128, 512], F32)
    PS7B = nc.alloc_psum_tensor("PS7B", [128, 1024], BF16)

    def MM(out, lhsT, rhs, start=True, stop=True, waits=(), tick=False):
        return P.do("tensor",
                    lambda t, o=out, l=lhsT, r=rhs, s1=start, s2=stop:
                    t.matmul(o, l, r, start=s1, stop=s2), waits, tick)

    def TR(out, in_, ident=None, waits=(), tick=False):
        idn = IDB if ident is None else ident
        return P.do("tensor", lambda t, o=out, i=in_, d=idn: t.transpose(o, i, d[:, :]),
                    waits, tick)

    def ACT(out, in_, func, bias=0.0, scale=1.0, accum=None, waits=(), tick=False):
        return P.do("scalar",
                    lambda s, o=out, i=in_, f=func, b=bias, sc=scale, a=accum:
                    s.activation(o, i, f, bias=b, scale=sc, accum_out=a), waits, tick)

    def VTT(out, a, b, op, waits=(), tick=False):
        return P.do("vector", lambda v, o=out, x=a, yy=b, p=op:
                    v.tensor_tensor(o, x, yy, p), waits, tick)

    def VTS(out, in0, s1, s2, op0, op1=ALU.bypass, waits=(), tick=False):
        return P.do("vector",
                    lambda v, o=out, i=in0, a=s1, b=s2, p0=op0, p1=op1:
                    v.tensor_scalar(o, i, a, b, p0, p1), waits, tick)

    def VSTT(out, in0, s, in1, op0, op1, accum=None, waits=(), tick=False):
        return P.do("vector",
                    lambda v, o=out, i0=in0, sc=s, i1=in1, p0=op0, p1=op1, a=accum:
                    v.scalar_tensor_tensor(o, i0, sc, i1, p0, p1, accum_out=a),
                    waits, tick)

    def VCP(out, in_, waits=(), tick=False):
        return P.do("vector", lambda v, o=out, i=in_: v.tensor_copy(o, i), waits, tick)

    def VRD(out, in_, op, waits=(), tick=False):
        return P.do("vector", lambda v, o=out, i=in_, p=op:
                    v.tensor_reduce(o, i, AX.X, p), waits, tick)

    def VSEP(tick=False):
        return P.do("vector", lambda v: v.tensor_copy(SEPT[:, 0:1], ONESC[:, :]),
                    (), tick)

    for _ in range(OPT_NONCE):
        P.do("vector", lambda v: v.tensor_copy(SEPT[:, 1:2], ONESC[:, :]))
    if dbg:
        P.do("vector", lambda v: v.memset(AB[:, :], 0.0))
        P.do("vector", lambda v: v.memset(WTT[:, :], 0.0))
        P.do("vector", lambda v: v.memset(S0[:, :], 0.0))
        P.do("vector", lambda v: v.memset(SC8[:, :], 0.0))
        P.do("vector", lambda v: v.memset(STG[:, :], 0.0))
        P.do("vector", lambda v: v.memset(WBF[:, :], 0.0))
        P.do("vector", lambda v: v.memset(AFa[:, :], 0.0), tick=True)
        P.wait("sync", [("ve", P.cnt["ve"])])
        P.wait("tensor", [("ve", P.cnt["ve"])])
        P.wait("scalar", [("ve", P.cnt["ve"])])
    for dj in range(NDJ):
        P.dma("sync", RES[:, dj * T:(dj + 1) * T], dr["x"][:, dj * T:(dj + 1) * T])
    P.dma("sync", SU[:, :], dr["suc"])
    P.dma("sync", UI[:, :], dr["uic"])
    P.dma("sync", MSK[:, :], dr["msc"])
    P.dma("sync", MST[:, :], dr["mstc"])
    P.dma("sync", ONESC[:, :], dr["onesc"])
    P.dma("sync", CSM[:, 0:8], dr["bmc"])
    P.dma("sync", CSM[:, 8:16], dr["boc"])
    P.dma("sync", CSM[:, 16:24], dr["ln1c"])
    P.dma("sync", CSM[:, 24:32], dr["ln2c"])
    P.dma("sync", CSM[:, 32:96], dr["b1c"])
    P.dma("sync", ROW8[:, 0:8], dr["alog"])
    P.dma("sync", ROW8[:, 8:16], dr["dtb"])
    P.dma("sync", STG[:, 640:768], dr["identc"])
    P.dma("sync", STG[:, 128:144], dr["emb"])
    P.dma("sync", STG[0:16, 256:384], dr["b2"])
    P.dma("sync", STG[0:1, 384:512], dr["onormr"])
    P.dma("sync", FCON[:, :], dr["fconst"])
    ld0 = P.dma("sync", STG[:, 512:640], dr["onesr"])
    for j, val in enumerate(FCONST_VALS):
        nc.const_aps.register(F32, val, FCON[:, j:j + 1])
    VCP(IDB[:, :], STG[:, 640:768], waits=[("ldy", ld0)])
    VCP(EMB[:, :], STG[:, 128:144])
    VCP(B2B[:, :], STG[0:16, 256:384])
    VCP(ONESR_R[:, :], STG[:, 512:640])
    ACT(SEPT[:, 0:1], ONESC[:, :], AF.Silu, waits=[("ldy", ld0)])
    ACT(SEPT[:, 0:1], ONESC[:, :], AF.Gelu)
    ACT(SEPT[:, 0:1], ONESC[:, :], AF.Ln)
    ACT(ROW8[:, 16:24], ROW8[:, 0:8], AF.Exp, tick=True)
    se0 = P.cnt["se"]
    VTS(ROW8[:, 24:32], ROW8[:, 16:24], -1.0, None, ALU.mult, waits=[("se", se0)])
    VSEP()
    VCP(ROW8R[:, 0:8], ROW8[:, 24:32])
    VCP(ROW8R[:, 8:16], ROW8[:, 8:16])
    v0 = VCP(TOTRR[:, 0:128], STG[0:1, 384:512], tick=True)
    MM(PS6[:, 0:8], ONESR_R[0:1, :], ROW8R[:, 0:8], waits=[("ve", v0)])
    MM(PS6[:, 8:16], ONESR_R[0:1, :], ROW8R[:, 8:16])
    m0 = MM(PS6[:, 128:256], ONESR_R[0:1, :], TOTRR[:, 0:128], tick=True)
    VCP(SMALL[:, 0:16], PS6[:, 0:16], waits=[("pe", m0)])
    v1 = VCP(ONB[:, :], PS6[:, 128:256], tick=True)
    P.wait("tensor", [("ve", v1)])

    stg = {"par": 0, "cast": [0, 0]}

    def stage(dst, dst_off, src_ap, ncols, extra_wait=None):
        last = 0
        done = 0
        while done < ncols:
            w = min(1024, ncols - done)
            par = stg["par"]
            stg["par"] ^= 1
            sem = "sga" if par == 0 else "sgb"
            ws = [("ve", stg["cast"][par])]
            if extra_wait:
                ws.append(extra_wait)
                extra_wait = None
            ldv = P.dma("sync", STG[:, par * 1024: par * 1024 + w],
                        src_ap[:, done:done + w], waits=ws, sem=sem)
            cv = VCP(dst[:, dst_off + done: dst_off + done + w],
                     STG[:, par * 1024: par * 1024 + w],
                     waits=[(sem, ldv)], tick=True)
            stg["cast"][par] = cv
            last = cv
            done += w
        return last

    def ln_stats(w0, W, scr_off, sq_ps=None):
        if sq_ps is None:
            sq_ps = PS5
        P.wait("tensor", [("ve", P.cnt["ve"]), ("se", P.cnt["se"])])
        P.wait("scalar", [("pe", P.cnt["pe"]), ("ve", P.cnt["ve"])])
        sqv = [0, 0]
        smv = [0] * NDJ
        for dj in range(NDJ):
            xs = RES[:, dj * T + w0: dj * T + w0 + W]
            sqv[dj % 2] = ACT(AFa[:, scr_off + (dj % 2) * W: scr_off + (dj % 2) * W + W],
                              xs, AF.Square,
                              waits=[("pe", smv[dj - 2])] if dj >= 2 else (),
                              tick=True)
            MM(PS6[0:1, 0:W], ONESC[:, :], xs,
               start=(dj == 0), stop=(dj == NDJ - 1))
            smv[dj] = MM(sq_ps[0:1, 0:W], ONESC[:, :],
                         AFa[:, scr_off + (dj % 2) * W: scr_off + (dj % 2) * W + W],
                         start=(dj == 0), stop=(dj == NDJ - 1),
                         waits=[("se", sqv[dj % 2])], tick=True)
        return smv[NDJ - 1]

    def emit_ffn():
        P.barrier()
        stage(WBF, 0, dr["wm"], 8192)
        stage(WBF, 8192, dr["w1t"], 8192)
        stage(WBF, 16384, dr["w2r"], 8192)
        wv = stage(WBF, 24576, dr["wo"], 8192)
        P.wait("tensor", [("ve", wv)])
        WMB = lambda dj, do: WBF[:, dj * 1024 + do * 128: dj * 1024 + (do + 1) * 128]
        W1B = lambda e, it: WBF[:, 8192 + e * 512 + it * 128: 8192 + e * 512 + (it + 1) * 128]
        W2B = lambda e, it: WBF[:, 16384 + (e * 4 + it) * 128: 16384 + (e * 4 + it) * 128 + 128]
        WOB = lambda h, do: WBF[:, 24576 + h * 1024 + do * 128: 24576 + h * 1024 + (do + 1) * 128]
        ZBF = lambda dj: AB[:, A0["ZBF"] + dj * TCW: A0["ZBF"] + (dj + 1) * TCW]
        HBFs = lambda h, a, w: AB[:, A0["HBF"] + h * TH + a: A0["HBF"] + h * TH + a + w]
        HMIDs = lambda par, it, a, w: AB[:, A0["HMID"] + par * 4 * TH + it * TH + a:
                                         A0["HMID"] + par * 4 * TH + it * TH + a + w]
        CMBFs = lambda h, a, w: AB[:, cmbf_col(h) + a: cmbf_col(h) + a + w]
        SCRf = lambda k, w=TCW: AFa[:, F0["SCR"] + k * TCW: F0["SCR"] + k * TCW + w]
        CMBT = lambda tt: AFa[:, F0["CMBT"] + tt * 128: F0["CMBT"] + (tt + 1) * 128]

        for half in range(NHALF):
            t0 = half * TH
            hv = 0
            for tc in range(NTCH):
                w0 = t0 + tc * TCW
                pemm = ln_stats(w0, TCW, F0["SCR"])
                VTS(RWS[0:1, 0:TCW], PS6[0:1, 0:TCW], 1.0 / D, None, ALU.mult,
                    waits=[("pe", pemm)])
                VTT(RWS[32:33, 0:TCW], RWS[0:1, 0:TCW], RWS[0:1, 0:TCW], ALU.mult)
                vs = VSTT(RWS[32:33, 0:TCW], PS5[0:1, 0:TCW], 1.0 / D,
                          RWS[32:33, 0:TCW], ALU.mult, ALU.subtract, tick=True)
                ACT(RWS[64:65, 0:TCW], RWS[32:33, 0:TCW], AF.Ln, bias=1e-5,
                    waits=[("ve", vs)])
                ACT(RWS[64:65, 0:TCW], RWS[64:65, 0:TCW], AF.Exp, scale=-0.5, tick=True)
                sr = P.cnt["se"]
                VCP(RWR[0:1, 0:TCW], RWS[0:1, 0:TCW], waits=[("se", sr)])
                vr = VCP(RWR[32:33, 0:TCW], RWS[64:65, 0:TCW], tick=True)
                MM(PS4[:, 0:TCW], ONESR_R[0:1, :], RWR[0:1, 0:TCW], waits=[("ve", vr)])
                bc = MM(PS5[:, 0:TCW], ONESR_R[32:33, :], RWR[32:33, 0:TCW], tick=True)
                for dj in range(NDJ):
                    xs = RES[:, dj * T + w0: dj * T + w0 + TCW]
                    VTT(SCRf(0), xs, PS4[:, 0:TCW], ALU.subtract,
                        waits=[("pe", bc)] if dj == 0 else ())
                    VTT(ZBF(dj), SCRf(0), PS5[:, 0:TCW], ALU.mult,
                        tick=(dj == NDJ - 1))
                zv = P.cnt["ve"]
                evs = [hv, hv]
                for do in range(NDJ):
                    pd = PS01 if do % 2 == 0 else PS23
                    for dj in range(NDJ):
                        ws = []
                        if dj == 0 and do == 0:
                            ws.append(("ve", zv))
                        if dj == 0:
                            ws.append(("se", evs[do % 2]))
                        MM(pd[:, 0:TCW], WMB(dj, do), ZBF(dj),
                           start=(dj == 0), stop=(dj == NDJ - 1),
                           waits=ws, tick=(dj == NDJ - 1))
                    mmt = P.cnt["pe"]
                    evs[do % 2] = ACT(HBFs(do, tc * TCW, TCW), pd[:, 0:TCW],
                                      AF.Identity, bias=CSM[:, do:do + 1],
                                      waits=[("pe", mmt)], tick=True)
                hv = max(evs)
            P.wait("tensor", [("se", hv)])

            wfmv = 0
            rt_prev = 0
            for tt in range(NTT):
                for h in range(NH):
                    MM(PS6[:, h * 16:(h + 1) * 16], HBFs(h, tt * 128, 128), EMB[:, :],
                       waits=[("se", rt_prev)] if h == 0 else (),
                       tick=(h == NH - 1))
                lv = P.cnt["pe"]
                VRD(AFa[:, 0:8], PS6[:, 0:128].rearrange("p (a b) -> p a b", b=16),
                    ALU.max, waits=[("pe", lv)])
                VSEP()
                nv = VTS(AFa[:, 8:16], AFa[:, 0:8], -1.0, None, ALU.mult, tick=True)
                for h in range(NH):
                    rt_prev = ACT(SCRf(0, 128)[:, h * 16:(h + 1) * 16],
                                  PS6[:, h * 16:(h + 1) * 16], AF.Exp,
                                  bias=AFa[:, 8 + h:9 + h],
                                  waits=[("ve", nv)] if h == 0 else (),
                                  tick=(h == NH - 1))
                VRD(AFa[:, 16:24], SCRf(0, 128).rearrange("p (a b) -> p a b", b=16),
                    ALU.add, waits=[("se", rt_prev)])
                VSEP()
                P.do("vector", lambda v: v.reciprocal(AFa[:, 24:32], AFa[:, 16:24]))
                VSEP()
                for h in range(NH):
                    VTS(WTT[:, tt * 128 + h * 16: tt * 128 + (h + 1) * 16],
                        SCRf(0, 128)[:, h * 16:(h + 1) * 16],
                        AFa[:, 24 + h:25 + h], None, ALU.mult)
                VSEP()
                VRD(AFa[:, 32:40],
                    WTT[:, tt * 128:(tt + 1) * 128].rearrange("p (a b) -> p a b", b=16),
                    ALU.max)
                VSEP()
                P.do("vector", lambda v, tt=tt: v.match_replace(
                    SCRf(1, 128), AFa[:, 32:40], WTT[:, tt * 128:(tt + 1) * 128],
                    -1.0e30))
                VSEP()
                VRD(AFa[:, 40:48], SCRf(1, 128).rearrange("p (a b) -> p a b", b=16),
                    ALU.max)
                VSEP()
                for h in range(NH):
                    VTS(SCRf(1, 128)[:, h * 16:(h + 1) * 16],
                        WTT[:, tt * 128 + h * 16: tt * 128 + (h + 1) * 16],
                        AFa[:, 40 + h:41 + h], None, ALU.is_ge)
                VSEP()
                VTT(WTT[:, tt * 128:(tt + 1) * 128], WTT[:, tt * 128:(tt + 1) * 128],
                    SCRf(1, 128), ALU.mult)
                wfmv = VCP(WTB[:, tt * 128:(tt + 1) * 128],
                           WTT[:, tt * 128:(tt + 1) * 128], tick=True)
            P.wait("tensor", [("ve", wfmv)])

            geluv = [0, 0]
            sttv = {s: 0 for s in range(8)}
            b2ev = 0
            for h in range(NH):
                for tt in range(NTT):
                    TR(PS7B[0:16, tt * 128:(tt + 1) * 128],
                       WTB[:, tt * 128 + h * 16: tt * 128 + (h + 1) * 16],
                       waits=[("ve", b2ev)] if tt == 0 else (),
                       tick=(tt == NTT - 1))
                trw = P.cnt["pe"]
                wfv = VCP(WFMH[0:16, 0:NTT * 128], PS7B[0:16, 0:NTT * 128],
                          waits=[("pe", trw)], tick=True)
                for tt in range(NTT):
                    pb = PS01 if tt < 4 else PS23
                    MM(pb[:, (tt % 4) * 128:(tt % 4 + 1) * 128],
                       WFMH[0:16, tt * 128:(tt + 1) * 128],
                       B2B[0:16, :],
                       waits=([("ve", wfv), ("se", max(geluv))]
                              if tt == 0 else ()),
                       tick=(tt == NTT - 1))
                b2t = P.cnt["pe"]
                if NTT > 4:
                    b2ev = VCP(AFa[:, F0["CMBT"]:F0["CMBT"] + 4 * 128],
                               PS01[:, 0:4 * 128],
                               waits=[("pe", b2t), ("ve", max(sttv.values()))],
                               tick=True)
                    b2ev = VCP(AFa[:, F0["CMBT"] + 4 * 128:F0["CMBT"] + NTT * 128],
                               PS23[:, 0:(NTT - 4) * 128], tick=True)
                else:
                    b2ev = VCP(AFa[:, F0["CMBT"]:F0["CMBT"] + NTT * 128],
                               PS01[:, 0:NTT * 128],
                               waits=[("pe", b2t), ("ve", max(sttv.values()))],
                               tick=True)
                for e in range(NE):
                    par_he = (h * NE + e) % 2
                    for it in range(4):
                        pd = PS01 if it % 2 == 0 else PS23
                        for n2 in range(NW1):
                            ws = []
                            if n2 == 0:
                                ws = [("se", geluv[it % 2])]
                                if e == 0:
                                    ws.append(("ve", b2ev))
                            MM(pd[:, n2 * W1W:(n2 + 1) * W1W], W1B(e, it),
                               HBFs(h, n2 * W1W, W1W), start=True, stop=True,
                               waits=ws, tick=(n2 == NW1 - 1))
                        mmt = P.cnt["pe"]
                        geluv[it % 2] = ACT(
                            HMIDs(par_he, it, 0, TH), pd[:, 0:TH], AF.Gelu,
                            bias=CSM[:, 32 + e * 4 + it:33 + e * 4 + it],
                            waits=[("pe", mmt)], tick=True)
                    gall = max(geluv)
                    for tt in range(NTT):
                        bank = tt % 2
                        pd2 = PS4 if bank == 0 else PS5
                        so = ((tt // 2) % 4) * 128
                        s = tt % 8
                        for it in range(4):
                            ws = []
                            if it == 0:
                                ws = [("se", gall),
                                      ("ve", max(sttv[s], sttv.get(tt % 2 + ((tt - 2) % 8 // 2) * 2, 0) if tt >= 2 else 0, b2ev))]
                                ws = [("se", gall),
                                      ("ve", max(sttv[s],
                                                 sttv[(tt - 2) % 8] if tt >= 2 else 0,
                                                 b2ev))]
                            MM(pd2[:, so:so + 128],
                               HMIDs(par_he, it, tt * 128, 128), W2B(e, it),
                               start=(it == 0), stop=(it == 3),
                               waits=ws, tick=(it == 3))
                        w2t = P.cnt["pe"]
                        sttv[s] = VSTT(
                            CMBT(tt), pd2[:, so:so + 128],
                            WTT[:, tt * 128 + h * 16 + e: tt * 128 + h * 16 + e + 1],
                            CMBT(tt), ALU.mult, ALU.add,
                            waits=[("pe", w2t)], tick=True)
                csc = A0["HMID"] + 4 * TH
                cbv = 0
                for tt in range(NTT):
                    cbv = VCP(AB[:, csc + tt * 128:csc + (tt + 1) * 128], CMBT(tt),
                              tick=(tt == NTT - 1))
                for tt in range(NTT):
                    TR(PS7B[:, tt * 128:(tt + 1) * 128],
                       AB[:, csc + tt * 128:csc + (tt + 1) * 128],
                       waits=[("ve", cbv)] if tt == 0 else (),
                       tick=(tt == NTT - 1))
                trt = P.cnt["pe"]
                b2ev = VCP(CMBFs(h, 0, NTT * 128), PS7B[:, 0:NTT * 128],
                           waits=[("pe", trt)], tick=True)
            P.wait("tensor", [("ve", P.cnt["ve"]), ("se", P.cnt["se"])])
            rv = [0, 0]
            for tc in range(NTCH):
                for do in range(NDJ):
                    pd = PS01 if do % 2 == 0 else PS23
                    for h in range(NH):
                        ws = []
                        if h == 0:
                            ws = [("ve", rv[do % 2])]
                        MM(pd[:, 0:TCW], WOB(h, do), CMBFs(h, tc * TCW, TCW),
                           start=(h == 0), stop=(h == NH - 1),
                           waits=ws, tick=(h == NH - 1))
                    wot = P.cnt["pe"]
                    VTS(SCRf(1), pd[:, 0:TCW], CSM[:, 8 + do:9 + do], 0.5,
                        ALU.add, ALU.mult, waits=[("pe", wot)])
                    rs = RES[:, do * T + t0 + tc * TCW: do * T + t0 + tc * TCW + TCW]
                    rv[do % 2] = VTT(rs, rs, SCRf(1), ALU.add, tick=True)
            P.wait("tensor", [("ve", max(rv)), ("se", P.cnt["se"])])

    def emit_attn():
        P.barrier()
        if "ALL" in OPT_SER or "AB" in OPT_SER or "ST" in OPT_SER:
            P.serialize = True
        stage(WBF, 0, dr["qw"], 8192)
        stage(WBF, 8192, dr["kw"], 8192)
        stage(WBF, 16384, dr["vw"], 8192)
        stage(WBF, 24576, dr["gw"], 8192)
        stage(AB, A1["OWB"], dr["ow"], 8192)
        wv = stage(AB, A1["BWA"], dr["bwaw"], 128)
        P.wait("tensor", [("ve", wv)])
        if OPT_SER == "ST":
            P.serialize = False
        QWB = lambda dj, n: WBF[:, dj * 1024 + n: dj * 1024 + n + 512]
        KWB = lambda dj, n: WBF[:, 8192 + dj * 1024 + n: 8192 + dj * 1024 + n + 512]
        VWB = lambda dj, n: WBF[:, 16384 + dj * 1024 + n: 16384 + dj * 1024 + n + 512]
        GWB = lambda dj, n: WBF[:, 24576 + dj * 1024 + n: 24576 + dj * 1024 + n + 512]
        OWBs = lambda h, do: AB[:, A1["OWB"] + h * 1024 + do * 128:
                                A1["OWB"] + h * 1024 + (do + 1) * 128]
        BWAs = lambda dj: AB[:, A1["BWA"] + dj * 16: A1["BWA"] + (dj + 1) * 16]
        HB1 = lambda ci, dj: AB[:, A1["HB1"] + ci * 1024 + dj * 128:
                                A1["HB1"] + ci * 1024 + (dj + 1) * 128]
        def TMS(base):
            return lambda ci, lo, w: AB[:, A1[base] + ci * 1024 + lo:
                                        A1[base] + ci * 1024 + lo + w]
        QNs, KNs, VNs, GNs = TMS("QN"), TMS("KN"), TMS("VN"), TMS("GN")
        SCN = A1["SCN"]
        ABS = lambda k, w=128: AB[:, SCN + k * 128: SCN + k * 128 + w]
        QS = lambda k: AFa[:, F1["QS"] + k * 1024: F1["QS"] + (k + 1) * 1024]
        XET = AFa[:, F1["XET"]:F1["XET"] + 128]
        OTM = AFa[:, F1["OTM"]:F1["OTM"] + 128]
        X1P = lambda dj: AFa[:, F1["X1P"] + dj * 128: F1["X1P"] + (dj + 1) * 128]
        SCR2 = lambda k, w=128: AFa[:, F1["SCR2"] + k * 128: F1["SCR2"] + k * 128 + w]
        ZR = lambda ci, dj: STG[:, ci * 1024 + dj * 128: ci * 1024 + (dj + 1) * 128]
        SCc = lambda ci, k, w=8: SC8[:, ci * 80 + k: ci * 80 + k + w]

        z0 = P.do("vector", lambda v: v.memset(S0[:, :], 0.0), tick=True)
        P.wait("tensor", [("ve", z0)])

        NG = CNC // GRP
        if OPT_P1_STOP < 0:
            NG = 0
        NG = min(NG, OPT_NG)
        vseq = 0
        h2v = 0
        for g in range(NG):
            for ci in range(GRP):
                c = g * GRP + ci
                w0 = c * 128
                pemm = ln_stats(w0, 128, F1["SCR2"])
                co = ci * 128
                VTS(RWS[0:1, co:co + 128], PS6[0:1, 0:128], 1.0 / D, None, ALU.mult,
                    waits=[("pe", pemm)])
                VTT(RWS[32:33, co:co + 128], RWS[0:1, co:co + 128],
                    RWS[0:1, co:co + 128], ALU.mult)
                vv = VSTT(RWS[32:33, co:co + 128], PS5[0:1, 0:128], 1.0 / D,
                          RWS[32:33, co:co + 128], ALU.mult, ALU.subtract, tick=True)
                TMP = RWS[32:33, 256 + co:256 + co + 128]
                ACT(TMP, RWS[32:33, co:co + 128], AF.Ln,
                    bias=1e-5, waits=[("ve", vv)])
                ACT(RWS[64:65, co:co + 128], TMP, AF.Exp,
                    scale=-0.5)
                ACT(TMP, TMP, AF.Exp,
                    scale=-1.0, tick=True)
                s2 = P.cnt["se"]
                v3 = VTT(TMP, RWS[32:33, co:co + 128],
                         TMP, ALU.mult,
                         waits=[("se", s2)], tick=True)
                ACT(TMP, TMP, AF.Ln,
                    bias=1e-6, waits=[("ve", v3)])
                ACT(TMP, TMP, AF.Exp,
                    scale=-0.5, tick=True)
                s3 = P.cnt["se"]
                VCP(RWS[64:65, 256 + co:256 + co + 128], TMP, waits=[("se", s3)])
                VTT(RWS[64:65, 256 + co:256 + co + 128],
                    RWS[64:65, co:co + 128],
                    RWS[64:65, 256 + co:256 + co + 128], ALU.mult)
                VCP(RWR[0:1, co:co + 128], RWS[0:1, co:co + 128])
                VCP(RWR[32:33, co:co + 128], RWS[64:65, co:co + 128])
                v4 = VCP(RWR[64:65, co:co + 128],
                         RWS[64:65, 256 + co:256 + co + 128], tick=True)
                MM(PS4[:, 0:128], ONESR_R[0:1, :], RWR[0:1, co:co + 128],
                   waits=[("ve", v4)])
                MM(PS5[:, 0:128], ONESR_R[32:33, :], RWR[32:33, co:co + 128])
                bc = MM(PS01[:, 0:128], ONESR_R[64:65, :], RWR[64:65, co:co + 128],
                        tick=True)
                for dj in range(NDJ):
                    xs = RES[:, dj * T + w0: dj * T + w0 + 128]
                    VTT(SCR2(0), xs, PS4[:, 0:128], ALU.subtract,
                        waits=[("pe", bc)] if dj == 0 else ())
                    VTT(ZR(ci, dj), SCR2(0), PS5[:, 0:128], ALU.mult)
                    VTT(SCR2(1), SCR2(0), PS01[:, 0:128], ALU.mult)
                    VTS(HB1(ci, dj), SCR2(1), CSM[:, 16 + dj:17 + dj], None,
                        ALU.mult, tick=(dj == NDJ - 1))
                vseq = P.cnt["ve"]
                P.wait("tensor", [("ve", vseq)])

            if OPT_P1_STOP < 1:
                continue
            silv = [0, 0]
            for ci in range(GRP):
                for ip, wsl in enumerate([QWB, KWB]):
                    pd = PS01 if ip % 2 == 0 else PS23
                    qsv = 0
                    for n2 in range(2):
                        for dj in range(NDJ):
                            ws = []
                            if dj == 0:
                                ws = [("se", silv[ip % 2]), ("ve", vseq)]
                            MM(pd[:, n2 * 512:(n2 + 1) * 512], HB1(ci, dj),
                               wsl(dj, n2 * 512), start=(dj == 0),
                               stop=(dj == NDJ - 1), waits=ws,
                               tick=(dj == NDJ - 1))
                        mmt = P.cnt["pe"]
                        silv[ip % 2] = ACT(QS(ip)[:, n2 * 512:(n2 + 1) * 512],
                                           pd[:, n2 * 512:(n2 + 1) * 512], AF.Silu,
                                           waits=[("pe", mmt)], tick=True)
                    dst = QNs if ip == 0 else KNs
                    sso = 32 + ci * 32 + ip * 8
                    for hh in range(NH):
                        VSTT(SCR2(0), QS(ip)[:, hh * 128:(hh + 1) * 128], 1.0,
                             QS(ip)[:, hh * 128:(hh + 1) * 128], ALU.mult, ALU.mult,
                             accum=SCc(ci, 64 + ip * 8 + hh, 1),
                             waits=[("se", silv[ip % 2])] if hh == 0 else ())
                    vseq = VCP(dst(ci, 0, 1024), QS(ip)[:, 0:1024], tick=True)
                for ip, wsl in enumerate([VWB, GWB]):
                    pd = PS01 if ip % 2 == 0 else PS23
                    dst = VNs if ip == 0 else GNs
                    for n2 in range(2):
                        for dj in range(NDJ):
                            ws = []
                            if dj == 0:
                                ws = [("se", silv[ip % 2])]
                            MM(pd[:, n2 * 512:(n2 + 1) * 512], HB1(ci, dj),
                               wsl(dj, n2 * 512), start=(dj == 0),
                               stop=(dj == NDJ - 1), waits=ws,
                               tick=(dj == NDJ - 1))
                        mmt = P.cnt["pe"]
                        silv[ip % 2] = ACT(dst(ci, n2 * 512, 512),
                                           pd[:, n2 * 512:(n2 + 1) * 512], AF.Silu,
                                           waits=[("pe", mmt)], tick=True)
                for dj in range(NDJ):
                    MM(PS6[:, 0:16], HB1(ci, dj), BWAs(dj),
                       start=(dj == 0), stop=(dj == NDJ - 1),
                       waits=[("ve", vseq)] if dj == 0 else (),
                       tick=(dj == NDJ - 1))
                bat = P.cnt["pe"]
                if dbg:
                    VCP(SC8[:, GRP * 80 + 8: GRP * 80 + 16], PS6[:, 8:16],
                        waits=[("pe", bat)])
                tb = ACT(SCR2(0, 8), PS6[:, 0:8], AF.Tanh, scale=0.5,
                         waits=[("pe", bat)], tick=True)
                VTS(SCc(ci, 0), SCR2(0, 8), 0.5, 0.5, ALU.mult, ALU.add,
                    waits=[("se", tb)])
                VSEP()
                VTS(SCc(ci, 8), SCc(ci, 0), -1.0, None, ALU.mult)

                vseq = VCP(SCc(ci, 40), PS6[:, 8:16], tick=True)
                P.wait("tensor", [("ve", vseq)])

            if OPT_P1_STOP < 2:
                continue
            P.serialize = ("C" in OPT_SER) or ("ALL" in OPT_SER)
            for ci in range(GRP):
                c = g * GRP + ci
                v5 = VTT(SCR2(0, 8), SCc(ci, 40), SMALL[:, 8:16], ALU.add,
                         waits=[("se", P.cnt["se"])], tick=True)
                if dbg:
                    VCP(SC8[:, GRP * 80 + ci * 8: GRP * 80 + 8 + ci * 8],
                        SCc(ci, 40))
                    VCP(SC8[:, GRP * 80 + 16 + ci * 8 if False else GRP * 80 + 8 + ci * 8 - 8 + 8: GRP * 80 + 16 + ci * 8 - 8 + 8] if False else SC8[:, 0:0], SC8[:, 0:0]) if False else None
                ACT(SCR2(1, 8), SCR2(0, 8), AF.Abs, waits=[("ve", v5)])
                ACT(SCR2(1, 8), SCR2(1, 8), AF.Exp, scale=-1.0)
                sp1 = ACT(SCR2(1, 8), SCR2(1, 8), AF.Ln, bias=1.0, tick=True)
                VTS(SCR2(2, 8), SCR2(0, 8), 0.0, None, ALU.max, waits=[("se", sp1)])
                VSEP()
                VTT(SCR2(2, 8), SCR2(2, 8), SCR2(1, 8), ALU.add)
                VSEP()
                v6 = VTT(SCc(ci, 16), SCR2(2, 8), SMALL[:, 0:8], ALU.mult, tick=True)
                MM(PS6[:, 64:72], UI[:, :], SCc(ci, 16),
                   waits=[("ve", v6), ("se", P.cnt["se"])])
                ct = MM(PS6[0:1, 96:104], ONESC[:, :], SCc(ci, 16), tick=True)
                VCP(SCc(ci, 24), PS6[:, 64:72], waits=[("pe", ct)])
                vtt_ = VCP(TOTR[0:1, c * 8:(c + 1) * 8], PS6[0:1, 96:104], tick=True)
                a1 = ACT(SCc(ci, 32), PS6[:, 64:72], AF.Exp,
                         waits=[("pe", ct)], tick=True)
                VSEP()
                v7 = VCP(TOTRR[0:1, c * 8:(c + 1) * 8], TOTR[0:1, c * 8:(c + 1) * 8],
                         waits=[("ve", vtt_)], tick=True)
                tb8 = MM(PS6[:, 104:112], ONESR_R[0:1, :],
                         TOTRR[0:1, c * 8:(c + 1) * 8],
                         waits=[("ve", v7), ("se", a1)], tick=True)
                v8 = VTT(SCR2(0, 8), PS6[:, 104:112], SCc(ci, 24), ALU.subtract,
                         waits=[("pe", tb8)], tick=True)
                ACT(SCc(ci, 48), SCR2(0, 8), AF.Exp, waits=[("ve", v8)])
                ac1 = ACT(SCc(ci, 56), PS6[:, 104:112], AF.Exp, tick=True)
                VTT(SCc(ci, 40), SCc(ci, 8), SCc(ci, 32), ALU.mult,
                    waits=[("se", ac1)])
                lq = ACT(SCR2(0, 16), SCc(ci, 64, 16), AF.Ln, bias=1e-6)
                eq = ACT(SCR2(0, 16), SCR2(0, 16), AF.Exp, scale=-0.5, tick=True)
                for hh in range(NH):
                    VTS(QNs(ci, hh * 128, 128), QNs(ci, hh * 128, 128),
                        SCR2(0, 16)[:, hh:hh + 1], None, ALU.mult,
                        waits=[("se", eq)] if hh == 0 else ())
                    VTS(KNs(ci, hh * 128, 128), KNs(ci, hh * 128, 128),
                        SCR2(0, 16)[:, 8 + hh:9 + hh], None, ALU.mult,
                        tick=(hh == NH - 1))
                vseq = P.cnt["ve"]

                if OPT_P1_STOP < 3:
                    continue
                ofmv = [0] * NH
                for h in range(NH):
                    v9 = VTS(ABS(24), KNs(ci, h * 128, 128), SCc(ci, 8)[:, h:h + 1],
                             None, ALU.mult, waits=[("ve", vseq)] if h == 0 else (),
                             tick=True)
                    TR(PS7B[:, 0:128], QNs(ci, h * 128, 128), waits=[("ve", v9)])
                    TR(PS7B[:, 128:256], KNs(ci, h * 128, 128))
                    t1 = TR(PS7B[:, 256:384], ABS(24), tick=True)
                    v10 = VCP(ABS(0, 384), PS7B[:, 0:384], waits=[("pe", t1)],
                              tick=True)
                    VTS(SCR2(0), UI[:, :], SCc(ci, 16)[:, h:h + 1], None, ALU.mult,
                        waits=[("se", P.cnt["se"])] if h == 0 else ())
                    v11 = VTS(SCR2(1), MST[:, :], SCc(ci, 16)[:, h:h + 1], None,
                              ALU.mult, tick=True)
                    MM(PS01[:, 0:128], SU[:, :], SCR2(0),
                       waits=[("ve", v11), ("se", P.cnt["se"])])
                    x2 = MM(PS01[:, 512:640], UI[:, :], SCR2(1), tick=True)
                    MM(PS4[:, 128:256], ABS(1), ABS(2),
                       waits=[("ve", v10)])
                    MM(PS4[:, 256:384], ABS(1), ABS(0))
                    kk2 = MM(PS5[:, 128:256], ABS(2), ABS(1), tick=True)
                    xe1 = ACT(XET, PS01[:, 0:128], AF.Exp, waits=[("pe", x2)],
                              tick=True)
                    xe2 = ACT(SCR2(2), PS01[:, 512:640], AF.Exp, tick=True)
                    VTT(SCR2(3), PS4[:, 128:256], XET, ALU.mult,
                        waits=[("se", xe1), ("pe", kk2)])
                    VTT(ABS(3), SCR2(3), MSK[:, :], ALU.mult)
                    VTT(SCR2(3), PS4[:, 256:384], XET, ALU.mult)
                    VTT(ABS(9), SCR2(3), UI[:, :], ALU.mult)
                    VTT(SCR2(3), PS5[:, 128:256], SCR2(2), ALU.mult,
                        waits=[("se", xe2)])
                    VTT(ABS(4), SCR2(3), MST[:, :], ALU.mult)
                    nv0 = VTT(ABS(7), IDB[:, :], ABS(3), ALU.add, tick=True)
                    P.wait("tensor", [("ve", nv0)])
                    qcur, qnxt = 7, 8
                    ncur, tcur, nnxt, tnxt = 3, 4, 5, 6
                    for k in range(6):
                        MM(PS4[:, 0:128], ABS(tcur), ABS(ncur))
                        m1_ = MM(PS5[:, 0:128], ABS(ncur), ABS(tcur), tick=True)
                        ev1 = VCP(ABS(nnxt), PS4[:, 0:128], waits=[("pe", m1_)])
                        ev1 = VCP(ABS(tnxt), PS5[:, 0:128], tick=True)
                        m2_ = MM(PS4[:, 128:256], ABS(tnxt), ABS(qcur),
                                 waits=[("ve", ev1)], tick=True)
                        qv = VTT(ABS(qnxt), PS4[:, 128:256], ABS(qcur), ALU.add,
                                 waits=[("pe", m2_)], tick=True)
                        P.wait("tensor", [("ve", qv)])
                        qcur, qnxt = qnxt, qcur
                        ncur, nnxt = nnxt, ncur
                        tcur, tnxt = tnxt, tcur
                    WINV = ABS(qcur)
                    if OPT_P1_STOP < 4:
                        ofmv[h] = P.cnt["ve"]
                        continue
                    sv0 = VCP(ABS(13), S0[:, h * 128:(h + 1) * 128], tick=True)
                    ks0 = MM(PS23[:, 0:128], ABS(1), ABS(13), waits=[("ve", sv0)],
                             tick=True)
                    VTS(ABS(11), VNs(ci, h * 128, 128), SCc(ci, 0)[:, h:h + 1],
                        None, ALU.mult)
                    rv_ = VSTT(ABS(12), PS23[:, 0:128], SCc(ci, 40)[:, h:h + 1],
                               ABS(11), ALU.mult, ALU.add,
                               waits=[("pe", ks0)], tick=True)
                    up = MM(PS23[:, 512:640], WINV, ABS(12), waits=[("ve", rv_)],
                            tick=True)
                    uv = VCP(ABS(10), PS23[:, 512:640], waits=[("pe", up)], tick=True)
                    MM(PS23[:, 128:256], ABS(9), ABS(10), waits=[("ve", uv)])
                    o2 = MM(PS23[:, 256:384], ABS(0), ABS(13), tick=True)
                    VTS(SCR2(3), PS23[:, 256:384], SCc(ci, 32)[:, h:h + 1], None,
                        ALU.mult, waits=[("pe", o2)])
                    ov = VTT(OTM, PS23[:, 128:256], SCR2(3), ALU.add, tick=True)
                    kt_ = VTS(ABS(14), KNs(ci, h * 128, 128),
                              SCc(ci, 48)[:, h:h + 1], None, ALU.mult, tick=True)
                    sp_ = MM(PS23[:, 640:768], ABS(14), ABS(10),
                             waits=[("ve", kt_)], tick=True)
                    s0v = VSTT(S0[:, h * 128:(h + 1) * 128],
                               S0[:, h * 128:(h + 1) * 128],
                               SCc(ci, 56)[:, h:h + 1], PS23[:, 640:768],
                               ALU.mult, ALU.add, waits=[("pe", sp_)], tick=True)
                    if OPT_P1_STOP < 5:
                        ofmv[h] = s0v
                        continue
                    VTT(SCR2(3), OTM, OTM, ALU.mult, waits=[("ve", ov)])
                    VRD(AFa[:, 32:33], SCR2(3), ALU.add)
                    VSEP()
                    vsq = VSEP(tick=True)
                    ACT(AFa[:, 33:34], AFa[:, 32:33], AF.Ln, scale=1.0 / 128,
                        bias=1e-6, waits=[("ve", vsq)])
                    on1 = ACT(AFa[:, 33:34], AFa[:, 33:34], AF.Exp, scale=-0.5,
                              tick=True)
                    VTS(OTM, OTM, AFa[:, 33:34], None, ALU.mult, waits=[("se", on1)])
                    VTT(OTM, OTM, ONB[:, :], ALU.mult)
                    obv = VTT(ABS(15), OTM, GNs(ci, h * 128, 128), ALU.mult,
                              tick=True)
                    tro = TR(PS7B[:, 384:512], ABS(15), waits=[("ve", obv)],
                             tick=True)
                    ofmv[h] = VCP(ABS(16 + h), PS7B[:, 384:512],
                                  waits=[("pe", tro)], tick=True)
                    vseq = ofmv[h]
                if OPT_P1_STOP < 6:
                    continue
                if "OW" in OPT_SER or "O1" in OPT_SER:
                    P.serialize = True
                w0 = c * 128
                rvv = 0
                x1v = [0] * NDJ
                for do in range(NDJ):
                    pd = PS5 if do % 2 == 0 else PS23
                    po = 0 if do % 2 == 0 else 384
                    for h in range(NH):
                        ws = []
                        if h == 0 and do == 0:
                            ws = [("ve", max(ofmv))]
                        elif h == 0 and do >= 2:
                            ws = [("ve", x1v[do - 2])]
                        MM(pd[:, po:po + 128], OWBs(h, do), ABS(16 + h),
                           start=(h == 0), stop=(h == NH - 1), waits=ws,
                           tick=(h == NH - 1))
                    owt = P.cnt["pe"]
                    x1v[do] = VSTT(X1P(do), pd[:, po:po + 128], 0.5, ZR(ci, do),
                                   ALU.mult, ALU.add, waits=[("pe", owt)], tick=True)
                    rs = RES[:, do * T + w0: do * T + w0 + 128]
                    VTT(rs, rs, X1P(do), ALU.add)
                if "O1" in OPT_SER:
                    P.serialize = False
                if "O2" in OPT_SER:
                    P.serialize = True
                if OPT_P1_STOP < 7:
                    continue
                sqv = [0, 0]
                smv2 = [0] * NDJ
                for dj in range(NDJ):
                    sqv[dj % 2] = ACT(SCR2(dj % 2), X1P(dj), AF.Square,
                                      waits=([("pe", smv2[dj - 2])] if dj >= 2
                                             else [("ve", x1v[dj])]), tick=True)
                    smv2[dj] = MM(PS01[0:1, 0:128], ONESC[:, :], SCR2(dj % 2),
                                  start=(dj == 0), stop=(dj == NDJ - 1),
                                  waits=[("se", sqv[dj % 2])], tick=True)
                ACT(RWS[0:1, 384:512], PS01[0:1, 0:128], AF.Ln, scale=1.0 / D,
                    bias=1e-6, waits=[("pe", smv2[NDJ - 1])])
                r2s = ACT(RWS[0:1, 384:512], RWS[0:1, 384:512], AF.Exp, scale=-0.5,
                          tick=True)
                r2v = VCP(RWR[0:1, 384:512], RWS[0:1, 384:512],
                          waits=[("se", r2s)], tick=True)
                rb2 = MM(PS6[:, 0:128], ONESR_R[0:1, :], RWR[0:1, 384:512],
                         waits=[("ve", r2v)], tick=True)
                if OPT_P1_STOP < 8:
                    continue
                for dj in range(NDJ):
                    VTT(SCR2(0), X1P(dj), PS6[:, 0:128], ALU.mult,
                        waits=([("pe", rb2), ("ldg", h2v)] if dj == 0 else ()))
                    VTS(ABS(25 + dj), SCR2(0), CSM[:, 24 + dj:25 + dj], None,
                        ALU.mult, tick=(dj == NDJ - 1))
                if "OW" in OPT_SER:
                    P.serialize = False
                h2t = P.cnt["ve"]
                if not OPT_NO_H2D:
                    for dj in range(NDJ):
                        h2v = P.dma("sync",
                                    h2d[:, dj * T + w0: dj * T + w0 + 128],
                                    AB[:, SCN + (25 + dj) * 128:
                                       SCN + (26 + dj) * 128],
                                    waits=[("ve", h2t)] if dj == 0 else (),
                                    sem="ldg")
                vseq = h2t
        P.serialize = False
        if not OPT_NO_H2D:
            P.wait("sync", [("ldg", h2v)])

    def emit_ffn2():
        P.barrier()
        pwv = stage(WBF, 0, dr["pw"], 32768)
        FFAs = lambda ff: AB[:, A2["FFA"] + ff * P2W: A2["FFA"] + (ff + 1) * P2W]
        H2Ls = lambda dj: AB[:, A2["H2L"] + dj * P2W: A2["H2L"] + (dj + 1) * P2W]
        FCRs = lambda r: AB[:, A2["FCR"] + r * 1024: A2["FCR"] + (r + 1) * 1024]
        PWB = lambda ff, do: WBF[:, ff * 1024 + do * 128: ff * 1024 + (do + 1) * 128]
        fcv = {r: 0 for r in range(4)}
        fcu = {r: 0 for r in range(4)}
        h2ld = 0
        silv = [0, 0]
        evv = [0, 0]
        for tc in range(NP2):
            w0 = tc * P2W
            for dj in range(NDJ):
                h2ld = P.dma("sync",
                             AB[:, A2["H2L"] + dj * P2W: A2["H2L"] + (dj + 1) * P2W],
                             h2d[:, dj * T + w0: dj * T + w0 + P2W],
                             waits=([("ve", evv[0]), ("ve", evv[1])]
                                    if dj == 0 else ()), sem="ldh")
            for ff in range(32):
                r = ff % 4
                par = stg["par"]
                stg["par"] ^= 1
                sem = "sga" if par == 0 else "sgb"
                for dj in range(NDJ):
                    ldv = P.dma("sync", STG[:, par * 1024 + dj * 128:
                                            par * 1024 + (dj + 1) * 128],
                                dr["fcw"][:, dj * 4096 + ff * 128:
                                          dj * 4096 + (ff + 1) * 128],
                                waits=([("ve", stg["cast"][par])]
                                       if dj == 0 else ()), sem=sem)
                cv = VCP(FCRs(r), STG[:, par * 1024: par * 1024 + 1024],
                         waits=[(sem, ldv), ("pe", fcu[r])], tick=True)
                stg["cast"][par] = cv
                fcv[r] = cv
                pd = PS01 if ff % 2 == 0 else PS23
                for dj in range(NDJ):
                    ws = []
                    if dj == 0:
                        ws = [("ve", fcv[r]), ("se", silv[ff % 2]),
                              ("ldh", h2ld)]
                    MM(pd[:, 0:P2W], FCRs(r)[:, dj * 128:(dj + 1) * 128],
                       H2Ls(dj), start=(dj == 0), stop=(dj == NDJ - 1),
                       waits=ws, tick=(dj == NDJ - 1))
                fcu[r] = P.cnt["pe"]
                silv[ff % 2] = ACT(FFAs(ff), pd[:, 0:P2W], AF.Silu,
                                   waits=[("pe", fcu[r])], tick=True)
            sall = max(silv)
            for do in range(NDJ):
                pd = PS4 if do % 2 == 0 else PS5
                for ff in range(32):
                    ws = []
                    if ff == 0:
                        ws = [("se", sall), ("ve", evv[do % 2]),
                              ("ve", pwv)]
                    MM(pd[:, 0:P2W], PWB(ff, do), FFAs(ff),
                       start=(ff == 0), stop=(ff == 31), waits=ws,
                       tick=(ff == 31))
                pwt = P.cnt["pe"]
                VTS(AFa[:, 64:64 + P2W], pd[:, 0:P2W], 0.5, None, ALU.mult,
                    waits=[("pe", pwt)])
                rs = RES[:, do * T + w0: do * T + w0 + P2W]
                evv[do % 2] = VTT(rs, rs, AFa[:, 64:64 + P2W], ALU.add, tick=True)
        P.wait("tensor", [("ve", max(evv))])

    if phases[0]:
        emit_ffn()
    if phases[1]:
        emit_attn()
    if phases[2]:
        emit_ffn2()
    if phases[3]:
        emit_ffn()

    P.barrier()
    stv = 0
    for dj in range(NDJ):
        stv = P.dma("sync", y[:, dj * T:(dj + 1) * T], RES[:, dj * T:(dj + 1) * T],
                    sem="st")
    if dbg:
        dab = nc.dram_tensor("dbgab", [128, 24576], BF16, kind="ExternalOutput").ap()
        dwt = nc.dram_tensor("dbgwt", [128, NTT * 128], F32, kind="ExternalOutput").ap()
        dsc = nc.dram_tensor("dbgsc", [128, GRP * 80 + 16], F32, kind="ExternalOutput").ap()
        ds0 = nc.dram_tensor("dbgs0", [128, NH * 128], F32, kind="ExternalOutput").ap()
        dst_ = nc.dram_tensor("dbgstg", [128, 2048], F32, kind="ExternalOutput").ap()
        dwb = nc.dram_tensor("dbgwbf", [128, 32768], BF16, kind="ExternalOutput").ap()
        daf = nc.dram_tensor("dbgaf", [128, 3904], F32, kind="ExternalOutput").ap()
        dsm = nc.dram_tensor("dbgsm", [128, 96], F32, kind="ExternalOutput").ap()
        for a in range(0, 24576, 8192):
            stv = P.dma("sync", dab[:, a:a + 8192], AB[:, a:a + 8192], sem="st")
        stv = P.dma("sync", dwt, WTT[:, :], sem="st")
        stv = P.dma("sync", dsc, SC8[:, :], sem="st")
        stv = P.dma("sync", ds0, S0[:, :], sem="st")
        stv = P.dma("sync", dst_, STG[:, :], sem="st")
        for a in range(0, 32768, 8192):
            stv = P.dma("sync", dwb[:, a:a + 8192], WBF[:, a:a + 8192], sem="st")
        stv = P.dma("sync", daf, AFa[:, :], sem="st")
        stv = P.dma("sync", dsm[:, 0:16], SMALL[:, :], sem="st")
        stv = P.dma("sync", dsm[0:1, 16:80], ROW8[0:1, 0:64], sem="st")
    P.wait("sync", [("st", stv)])

    with nc.Block() as block:
        P.emit(block)
    return nc


def _fm(w):
    n = w.shape[1]
    return np.ascontiguousarray(w.reshape(8, 128, n).transpose(1, 0, 2)
                                .reshape(128, 8 * n))


def _prep_inputs(inputs, T):
    f32 = np.float32
    g = {k: np.asarray(v, dtype=f32) for k, v in inputs.items()}
    shared = {
        "wm": _fm(g["L0_ffn_Wm"]),
        "w1t": np.ascontiguousarray(g["L0_ffn_W1"].transpose(1, 0, 2)
                                    .reshape(128, 16 * 512)),
        "w2r": np.ascontiguousarray(g["L0_ffn_W2"].reshape(64, 128, 128)
                                    .transpose(1, 0, 2).reshape(128, 64 * 128)),
        "wo": _fm(g["L0_ffn_Wo"]),
        "qw": _fm(g["L0_attn_qw"]),
        "kw": _fm(g["L0_attn_kw"]),
        "vw": _fm(g["L0_attn_vw"]),
        "gw": _fm(g["L0_attn_gw"]),
        "ow": _fm(g["L0_attn_ow"]),
        "fcw": _fm(g["L0_attn_fcw"]),
        "pw": np.ascontiguousarray(g["L0_attn_pw"].reshape(32, 128, 1024)
                                   .transpose(1, 0, 2).reshape(128, 32 * 1024)),
        "bwaw": _fm(np.concatenate([g["L0_attn_bw"], g["L0_attn_aw"]], axis=1)),
        "emb": np.ascontiguousarray(g["L0_ffn_expert_emb"]),
        "b2": np.ascontiguousarray(g["L0_ffn_b2"]),
        "b1c": np.ascontiguousarray(g["L0_ffn_b1"].reshape(16, 4, 128)
                                    .transpose(2, 0, 1).reshape(128, 64)),
        "bmc": np.ascontiguousarray(g["L0_ffn_bm"].reshape(8, 128).T),
        "boc": np.ascontiguousarray(g["L0_ffn_bo"].reshape(8, 128).T),
        "ln1c": np.ascontiguousarray(g["L0_attn_ln1_w"].reshape(8, 128).T),
        "ln2c": np.ascontiguousarray(g["L0_attn_ln2_w"].reshape(8, 128).T),
        "alog": g["L0_attn_A_log"].reshape(1, 8),
        "dtb": g["L0_attn_dt_bias"].reshape(1, 8),
        "onormr": g["L0_attn_onorm_w"].reshape(1, 128),
        "identc": np.eye(128, dtype=f32),
        "suc": np.tril(np.ones((128, 128), f32), -1),
        "uic": np.triu(np.ones((128, 128), f32), 0),
        "msc": np.triu(np.ones((128, 128), f32), 1),
        "mstc": np.tril(np.ones((128, 128), f32), -1),
        "onesc": np.ones((128, 1), f32),
        "onesr": np.ones((128, 128), f32),
        "fconst": np.tile(np.array(FCONST_VALS, f32), (128, 1)),
    }
    shared = {k: np.ascontiguousarray(v, dtype=f32) for k, v in shared.items()}
    x = g["L0_attn_x"]
    in_maps = []
    for i in range(x.shape[0]):
        m = dict(shared)
        m["x"] = _fm(np.ascontiguousarray(x[i].T))
        in_maps.append(m)
    return in_maps


def kernel(**inputs):
    x = np.asarray(inputs["L0_attn_x"])
    B, T, _ = x.shape
    nc = build_program(T=T)
    in_maps = _prep_inputs(inputs, T)
    res = run_bass_kernel_spmd(nc, in_maps, core_ids=list(range(B)))
    outs = []
    for i in range(B):
        yy = res.results[i]["y"]
        full = yy.reshape(128, 8, T).transpose(1, 0, 2).reshape(1024, T)
        outs.append(full.T)
    return np.ascontiguousarray(np.stack(outs, axis=0), dtype=np.float32)


# version 05
# speedup vs baseline: 1.0613x; 1.0613x over previous
import numpy as np
import ml_dtypes
import concourse.bass as bass
import concourse.mybir as mybir
from concourse.bass_utils import run_bass_kernel_spmd

F32 = mybir.dt.float32
F32R = mybir.dt.float32r
BF16 = mybir.dt.bfloat16
AF = mybir.ActivationFunctionType
ALU = mybir.AluOpType
AX = mybir.AxisListType

ENGINES = ("tensor", "vector", "scalar", "gpsimd", "sync")
OPT_NO_H2D = False
OPT_SER = ""
OPT_NONCE = 0
OPT_P1_STOP = 99
OPT_NG = 99
FCONST_VALS = [1e-5, 1e-6, -0.5, -1.0, 0.5, 1.0 / 128.0, 1.0 / 1024.0,
               -1.0e30, 2.0, 3.0, 0.25, -2.0, 8.0, 16.0, 1.5, -1.5]
D = 1024
NDJ = 8
NH = 8
NE = 16
CHK = 128


class Prog:
    def __init__(self, nc):
        self.nc = nc
        self.streams = {e: [] for e in ENGINES}
        names = ["pe", "ve", "se", "ge", "ldy", "ldg", "st", "sga", "sgb", "ldh"]
        self.sem = {n: nc.alloc_semaphore("sem_" + n) for n in names}
        self.cnt = {n: 0 for n in names}
        self.engsem = {"tensor": "pe", "vector": "ve", "scalar": "se", "gpsimd": "ge"}

    def do(self, eng, fn, waits=(), tick=False):
        name = self.engsem.get(eng)
        if getattr(self, "serialize", False) and name is not None:
            waits = list(waits) + [(k, self.cnt[k]) for k in
                                   ("pe", "ve", "se", "ge") if k != name]
            tick = True
        tn = None
        if tick:
            self.cnt[name] += 1
            tn = name
        self.streams[eng].append((fn, [w for w in waits if w and w[1] > 0],
                                  self.sem[tn] if tn else None, 1))
        return self.cnt[name] if tick else None

    def dma(self, eng, out, in_, waits=(), sem=None):
        if sem is None:
            sem = "ldg" if eng == "gpsimd" else "ldy"
        self.cnt[sem] += 16
        self.streams[eng].append(
            ((lambda e, o=out, i=in_: e.dma_start(out=o, in_=i)),
             [w for w in waits if w and w[1] > 0], self.sem[sem], 16))
        return self.cnt[sem]

    def wait(self, eng, waits):
        self.streams[eng].append((None, [w for w in waits if w and w[1] > 0], None, 0))

    def barrier(self):
        snap = dict(self.cnt)
        for eng in ENGINES:
            own = self.engsem.get(eng)
            ws = [(k, v) for k, v in snap.items() if v > 0 and k != own]
            self.wait(eng, ws)

    def emit(self, block):
        for eng in ENGINES:
            stream = self.streams[eng]
            if not stream:
                continue

            def make(stream):
                def body(e):
                    for fn, waits, sem, n in stream:
                        for (wn, wv) in waits:
                            e.wait_ge(self.sem[wn], wv)
                        if fn is not None:
                            ins = fn(e)
                            if sem is not None:
                                ins.then_inc(sem, n)
                return body
            getattr(block, eng)(make(stream))


def build_program(T=2048, phases=(1, 1, 1, 1), dbg=False):
    CNC = T // CHK
    NHALF = 2 if T >= 1024 else 1
    TH = T // NHALF
    TCW = min(512, TH)
    NTCH = TH // TCW
    NTT = TH // 128
    GRP = 2
    NW1 = max(1, TH // 512)
    W1W = min(512, TH)
    P2W = 512 if T >= 1024 else (256 if T >= 512 else 128)
    NP2 = T // P2W

    nc = bass.Bass("TRN2", target_bir_lowering=False, debug=False,
                   dynamic_dma_scratch_size=2048,
                   detect_race_conditions=False)
    P = Prog(nc)

    dr = {}
    def dram_in(name, shape, dt=F32):
        dr[name] = nc.dram_tensor(name, shape, dt, kind="ExternalInput").ap()
        return dr[name]

    dram_in("x", [128, NDJ * T])
    for w in ["wm", "w1t", "w2r", "wo", "qw", "kw", "vw", "gw", "ow"]:
        dram_in(w, [128, 8192])
    dram_in("fcw", [128, NDJ * 4096])
    dram_in("pw", [128, 32 * 1024])
    dram_in("bwaw", [128, 128])
    dram_in("emb", [128, NE])
    dram_in("b2", [16, 128])
    dram_in("b1c", [128, 64])
    dram_in("bmc", [128, 8])
    dram_in("boc", [128, 8])
    dram_in("ln1c", [128, 8])
    dram_in("ln2c", [128, 8])
    dram_in("alog", [1, 8])
    dram_in("dtb", [1, 8])
    dram_in("onormr", [1, 128])
    dram_in("identc", [128, 128])
    dram_in("suc", [128, 128])
    dram_in("uic", [128, 128])
    dram_in("msc", [128, 128])
    dram_in("mstc", [128, 128])
    dram_in("onesc", [128, 1])
    dram_in("onesr", [128, 128])
    dram_in("fconst", [128, 16])

    y = nc.dram_tensor("y", [128, NDJ * T], F32, kind="ExternalOutput").ap()
    h2d = nc.dram_tensor("h2d", [128, NDJ * T], BF16, kind="Internal").ap()

    RES = nc.alloc_sbuf_tensor("RES", [128, NDJ * T], F32)
    WBF = nc.alloc_sbuf_tensor("WBF", [128, 32768], BF16)
    AB = nc.alloc_sbuf_tensor("AB", [128, 24576], BF16)
    AFa = nc.alloc_sbuf_tensor("AFa", [128, 3904], F32)
    STG = nc.alloc_sbuf_tensor("STG", [128, 2048], F32)
    IDB = nc.alloc_sbuf_tensor("IDB", [128, 128], BF16)
    SU = nc.alloc_sbuf_tensor("SUc", [128, 128], F32)
    UI = nc.alloc_sbuf_tensor("UIc", [128, 128], F32)
    MSK = nc.alloc_sbuf_tensor("MSc", [128, 128], F32)
    MST = nc.alloc_sbuf_tensor("MSTc", [128, 128], F32)
    ONESC = nc.alloc_sbuf_tensor("ONESCc", [128, 1], F32)
    ONESR_R = nc.alloc_sbuf_tensor("ONESR_R", [128, 128], F32R)
    CSM = nc.alloc_sbuf_tensor("CSM", [128, 128], F32)
    SMALL = nc.alloc_sbuf_tensor("SMALL", [128, 16], F32)
    ONB = nc.alloc_sbuf_tensor("ONB", [128, 128], F32)
    B2B = nc.alloc_sbuf_tensor("B2B", [16, 128], BF16)
    EMB = nc.alloc_sbuf_tensor("EMBB", [128, NE], BF16)
    ROW8 = nc.alloc_sbuf_tensor("ROW8", [1, 64], F32)
    ROW8R = nc.alloc_sbuf_tensor("ROW8R", [1, 64], F32R)
    RWS = nc.alloc_sbuf_tensor("RWS", [128, 512], F32)
    RWR = nc.alloc_sbuf_tensor("RWR", [128, 512], F32R)
    TOTR = nc.alloc_sbuf_tensor("TOTR", [1, 128], F32)
    TOTRR = nc.alloc_sbuf_tensor("TOTRR", [1, 128], F32R)
    FCON = nc.alloc_sbuf_tensor("FCON", [128, 16], F32)
    SEPT = nc.alloc_sbuf_tensor("SEPT", [128, 2], F32)
    S0 = nc.alloc_sbuf_tensor("S0", [128, NH * 128], F32)
    SC8 = nc.alloc_sbuf_tensor("SC8", [128, GRP * 80 + 16], F32)
    WTT = nc.alloc_sbuf_tensor("WTT", [128, NTT * 128], F32)
    WTB = nc.alloc_sbuf_tensor("WTB", [128, NTT * 128], BF16)
    WFMH = nc.alloc_sbuf_tensor("WFMH", [16, NTT * 128], BF16)

    A0 = dict(ZBF=0, HBF=NDJ * TCW, HMID=NDJ * TCW + NDJ * TH,
              CMBF=NDJ * TCW + NDJ * TH + 8 * TH)
    def cmbf_col(h):
        return h * TH if h < 4 else A0["CMBF"] + (h - 4) * TH
    assert A0["CMBF"] + 4 * TH <= 24576
    assert 4 * TH <= NDJ * TCW
    gq = GRP * 1024
    A1 = dict(OWB=0, BWA=8192, HB1=8320, QN=8320 + gq, KN=8320 + 2 * gq,
              VN=8320 + 3 * gq, GN=8320 + 4 * gq, SCN=8320 + 5 * gq)
    assert A1["SCN"] + 33 * 128 <= 24576
    A2 = dict(FFA=0, H2L=32 * P2W, FCR=32 * P2W + NDJ * P2W)
    assert A2["FCR"] + 4 * 1024 <= 24576
    F0 = dict(SCR=64, CMBT=64 + 2 * TCW)
    assert F0["CMBT"] + NTT * 128 <= 3904
    F1 = dict(QS=64, XET=64 + 2048, OTM=64 + 2048 + 128, X1P=64 + 2048 + 256,
              SCR2=64 + 2048 + 256 + 1024)
    assert F1["SCR2"] + 512 <= 3904

    PS01 = nc.alloc_psum_tensor("PS01", [128, 1024], F32)
    PS23 = nc.alloc_psum_tensor("PS23", [128, 1024], F32)
    PS4 = nc.alloc_psum_tensor("PS4", [128, 512], F32)
    PS5 = nc.alloc_psum_tensor("PS5", [128, 512], F32)
    PS6 = nc.alloc_psum_tensor("PS6", [128, 512], F32)
    PS7B = nc.alloc_psum_tensor("PS7B", [128, 1024], BF16)

    def MM(out, lhsT, rhs, start=True, stop=True, waits=(), tick=False):
        return P.do("tensor",
                    lambda t, o=out, l=lhsT, r=rhs, s1=start, s2=stop:
                    t.matmul(o, l, r, start=s1, stop=s2), waits, tick)

    def TR(out, in_, ident=None, waits=(), tick=False):
        idn = IDB if ident is None else ident
        return P.do("tensor", lambda t, o=out, i=in_, d=idn: t.transpose(o, i, d[:, :]),
                    waits, tick)

    def ACT(out, in_, func, bias=0.0, scale=1.0, accum=None, waits=(), tick=False):
        return P.do("scalar",
                    lambda s, o=out, i=in_, f=func, b=bias, sc=scale, a=accum:
                    s.activation(o, i, f, bias=b, scale=sc, accum_out=a), waits, tick)

    def VTT(out, a, b, op, waits=(), tick=False):
        return P.do("vector", lambda v, o=out, x=a, yy=b, p=op:
                    v.tensor_tensor(o, x, yy, p), waits, tick)

    def VTS(out, in0, s1, s2, op0, op1=ALU.bypass, waits=(), tick=False):
        return P.do("vector",
                    lambda v, o=out, i=in0, a=s1, b=s2, p0=op0, p1=op1:
                    v.tensor_scalar(o, i, a, b, p0, p1), waits, tick)

    def VSTT(out, in0, s, in1, op0, op1, accum=None, waits=(), tick=False):
        return P.do("vector",
                    lambda v, o=out, i0=in0, sc=s, i1=in1, p0=op0, p1=op1, a=accum:
                    v.scalar_tensor_tensor(o, i0, sc, i1, p0, p1, accum_out=a),
                    waits, tick)

    def VCP(out, in_, waits=(), tick=False):
        return P.do("vector", lambda v, o=out, i=in_: v.tensor_copy(o, i), waits, tick)

    def VRD(out, in_, op, waits=(), tick=False):
        return P.do("vector", lambda v, o=out, i=in_, p=op:
                    v.tensor_reduce(o, i, AX.X, p), waits, tick)

    def VSEP(tick=False):
        return P.do("vector", lambda v: v.tensor_copy(SEPT[:, 0:1], ONESC[:, :]),
                    (), tick)

    for _ in range(OPT_NONCE):
        P.do("vector", lambda v: v.tensor_copy(SEPT[:, 1:2], ONESC[:, :]))
    if dbg:
        P.do("vector", lambda v: v.memset(AB[:, :], 0.0))
        P.do("vector", lambda v: v.memset(WTT[:, :], 0.0))
        P.do("vector", lambda v: v.memset(S0[:, :], 0.0))
        P.do("vector", lambda v: v.memset(SC8[:, :], 0.0))
        P.do("vector", lambda v: v.memset(STG[:, :], 0.0))
        P.do("vector", lambda v: v.memset(WBF[:, :], 0.0))
        P.do("vector", lambda v: v.memset(AFa[:, :], 0.0), tick=True)
        P.wait("sync", [("ve", P.cnt["ve"])])
        P.wait("tensor", [("ve", P.cnt["ve"])])
        P.wait("scalar", [("ve", P.cnt["ve"])])
    for dj in range(NDJ):
        P.dma("sync", RES[:, dj * T:(dj + 1) * T], dr["x"][:, dj * T:(dj + 1) * T])
    P.dma("sync", SU[:, :], dr["suc"])
    P.dma("sync", UI[:, :], dr["uic"])
    P.dma("sync", MSK[:, :], dr["msc"])
    P.dma("sync", MST[:, :], dr["mstc"])
    P.dma("sync", ONESC[:, :], dr["onesc"])
    P.dma("sync", CSM[:, 0:8], dr["bmc"])
    P.dma("sync", CSM[:, 8:16], dr["boc"])
    P.dma("sync", CSM[:, 16:24], dr["ln1c"])
    P.dma("sync", CSM[:, 24:32], dr["ln2c"])
    P.dma("sync", CSM[:, 32:96], dr["b1c"])
    P.dma("sync", ROW8[:, 0:8], dr["alog"])
    P.dma("sync", ROW8[:, 8:16], dr["dtb"])
    P.dma("sync", STG[:, 640:768], dr["identc"])
    P.dma("sync", STG[:, 128:144], dr["emb"])
    P.dma("sync", STG[0:16, 256:384], dr["b2"])
    P.dma("sync", STG[0:1, 384:512], dr["onormr"])
    P.dma("sync", FCON[:, :], dr["fconst"])
    ld0 = P.dma("sync", STG[:, 512:640], dr["onesr"])
    for j, val in enumerate(FCONST_VALS):
        nc.const_aps.register(F32, val, FCON[:, j:j + 1])
    VCP(IDB[:, :], STG[:, 640:768], waits=[("ldy", ld0)])
    VCP(EMB[:, :], STG[:, 128:144])
    VCP(B2B[:, :], STG[0:16, 256:384])
    VCP(ONESR_R[:, :], STG[:, 512:640])
    ACT(SEPT[:, 0:1], ONESC[:, :], AF.Silu, waits=[("ldy", ld0)])
    ACT(SEPT[:, 0:1], ONESC[:, :], AF.Gelu)
    ACT(SEPT[:, 0:1], ONESC[:, :], AF.Ln)
    ACT(ROW8[:, 16:24], ROW8[:, 0:8], AF.Exp, tick=True)
    se0 = P.cnt["se"]
    VTS(ROW8[:, 24:32], ROW8[:, 16:24], -1.0, None, ALU.mult, waits=[("se", se0)])
    VSEP()
    VCP(ROW8R[:, 0:8], ROW8[:, 24:32])
    VCP(ROW8R[:, 8:16], ROW8[:, 8:16])
    v0 = VCP(TOTRR[:, 0:128], STG[0:1, 384:512], tick=True)
    MM(PS6[:, 0:8], ONESR_R[0:1, :], ROW8R[:, 0:8], waits=[("ve", v0)])
    MM(PS6[:, 8:16], ONESR_R[0:1, :], ROW8R[:, 8:16])
    m0 = MM(PS6[:, 128:256], ONESR_R[0:1, :], TOTRR[:, 0:128], tick=True)
    VCP(SMALL[:, 0:16], PS6[:, 0:16], waits=[("pe", m0)])
    v1 = VCP(ONB[:, :], PS6[:, 128:256], tick=True)
    P.wait("tensor", [("ve", v1)])

    stg = {"par": 0, "cast": [0, 0]}

    def stage(dst, dst_off, src_ap, ncols, extra_wait=None):
        last = 0
        done = 0
        while done < ncols:
            w = min(1024, ncols - done)
            par = stg["par"]
            stg["par"] ^= 1
            sem = "sga" if par == 0 else "sgb"
            ws = [("ve", stg["cast"][par])]
            if extra_wait:
                ws.append(extra_wait)
                extra_wait = None
            ldv = P.dma("sync", STG[:, par * 1024: par * 1024 + w],
                        src_ap[:, done:done + w], waits=ws, sem=sem)
            cv = VCP(dst[:, dst_off + done: dst_off + done + w],
                     STG[:, par * 1024: par * 1024 + w],
                     waits=[(sem, ldv)], tick=True)
            stg["cast"][par] = cv
            last = cv
            done += w
        return last

    def ln_stats(w0, W, scr_off, sq_ps=None):
        if sq_ps is None:
            sq_ps = PS5
        P.wait("tensor", [("ve", P.cnt["ve"]), ("se", P.cnt["se"])])
        P.wait("scalar", [("pe", P.cnt["pe"]), ("ve", P.cnt["ve"])])
        sqv = [0, 0]
        smv = [0] * NDJ
        for dj in range(NDJ):
            xs = RES[:, dj * T + w0: dj * T + w0 + W]
            sqv[dj % 2] = ACT(AFa[:, scr_off + (dj % 2) * W: scr_off + (dj % 2) * W + W],
                              xs, AF.Square,
                              waits=[("pe", smv[dj - 2])] if dj >= 2 else (),
                              tick=True)
            MM(PS6[0:1, 0:W], ONESC[:, :], xs,
               start=(dj == 0), stop=(dj == NDJ - 1))
            smv[dj] = MM(sq_ps[0:1, 0:W], ONESC[:, :],
                         AFa[:, scr_off + (dj % 2) * W: scr_off + (dj % 2) * W + W],
                         start=(dj == 0), stop=(dj == NDJ - 1),
                         waits=[("se", sqv[dj % 2])], tick=True)
        return smv[NDJ - 1]

    def emit_ffn():
        P.barrier()
        stage(WBF, 0, dr["wm"], 8192)
        stage(WBF, 8192, dr["w1t"], 8192)
        stage(WBF, 16384, dr["w2r"], 8192)
        wv = stage(WBF, 24576, dr["wo"], 8192)
        P.wait("tensor", [("ve", wv)])
        WMB = lambda dj, do: WBF[:, dj * 1024 + do * 128: dj * 1024 + (do + 1) * 128]
        W1B = lambda e, it: WBF[:, 8192 + e * 512 + it * 128: 8192 + e * 512 + (it + 1) * 128]
        W2B = lambda e, it: WBF[:, 16384 + (e * 4 + it) * 128: 16384 + (e * 4 + it) * 128 + 128]
        WOB = lambda h, do: WBF[:, 24576 + h * 1024 + do * 128: 24576 + h * 1024 + (do + 1) * 128]
        ZBF = lambda dj: AB[:, A0["ZBF"] + dj * TCW: A0["ZBF"] + (dj + 1) * TCW]
        HBFs = lambda h, a, w: AB[:, A0["HBF"] + h * TH + a: A0["HBF"] + h * TH + a + w]
        HMIDs = lambda par, it, a, w: AB[:, A0["HMID"] + par * 4 * TH + it * TH + a:
                                         A0["HMID"] + par * 4 * TH + it * TH + a + w]
        CMBFs = lambda h, a, w: AB[:, cmbf_col(h) + a: cmbf_col(h) + a + w]
        SCRf = lambda k, w=TCW: AFa[:, F0["SCR"] + k * TCW: F0["SCR"] + k * TCW + w]
        CMBT = lambda tt: AFa[:, F0["CMBT"] + tt * 128: F0["CMBT"] + (tt + 1) * 128]

        for half in range(NHALF):
            t0 = half * TH
            hv = 0
            for tc in range(NTCH):
                w0 = t0 + tc * TCW
                pemm = ln_stats(w0, TCW, F0["SCR"])
                VTS(RWS[0:1, 0:TCW], PS6[0:1, 0:TCW], 1.0 / D, None, ALU.mult,
                    waits=[("pe", pemm)])
                VTT(RWS[32:33, 0:TCW], RWS[0:1, 0:TCW], RWS[0:1, 0:TCW], ALU.mult)
                vs = VSTT(RWS[32:33, 0:TCW], PS5[0:1, 0:TCW], 1.0 / D,
                          RWS[32:33, 0:TCW], ALU.mult, ALU.subtract, tick=True)
                ACT(RWS[64:65, 0:TCW], RWS[32:33, 0:TCW], AF.Ln, bias=1e-5,
                    waits=[("ve", vs)])
                ACT(RWS[64:65, 0:TCW], RWS[64:65, 0:TCW], AF.Exp, scale=-0.5, tick=True)
                sr = P.cnt["se"]
                VCP(RWR[0:1, 0:TCW], RWS[0:1, 0:TCW], waits=[("se", sr)])
                vr = VCP(RWR[32:33, 0:TCW], RWS[64:65, 0:TCW], tick=True)
                MM(PS4[:, 0:TCW], ONESR_R[0:1, :], RWR[0:1, 0:TCW], waits=[("ve", vr)])
                bc = MM(PS5[:, 0:TCW], ONESR_R[32:33, :], RWR[32:33, 0:TCW], tick=True)
                for dj in range(NDJ):
                    xs = RES[:, dj * T + w0: dj * T + w0 + TCW]
                    VTT(SCRf(0), xs, PS4[:, 0:TCW], ALU.subtract,
                        waits=[("pe", bc)] if dj == 0 else ())
                    VTT(ZBF(dj), SCRf(0), PS5[:, 0:TCW], ALU.mult,
                        tick=(dj == NDJ - 1))
                zv = P.cnt["ve"]
                evs = [hv, hv]
                for do in range(NDJ):
                    pd = PS01 if do % 2 == 0 else PS23
                    for dj in range(NDJ):
                        ws = []
                        if dj == 0 and do == 0:
                            ws.append(("ve", zv))
                        if dj == 0:
                            ws.append(("se", evs[do % 2]))
                        MM(pd[:, 0:TCW], WMB(dj, do), ZBF(dj),
                           start=(dj == 0), stop=(dj == NDJ - 1),
                           waits=ws, tick=(dj == NDJ - 1))
                    mmt = P.cnt["pe"]
                    evs[do % 2] = ACT(HBFs(do, tc * TCW, TCW), pd[:, 0:TCW],
                                      AF.Identity, bias=CSM[:, do:do + 1],
                                      waits=[("pe", mmt)], tick=True)
                hv = max(evs)
            P.wait("tensor", [("se", hv)])

            wfmv = 0
            rt_prev = 0
            for tt in range(NTT):
                for h in range(NH):
                    MM(PS6[:, h * 16:(h + 1) * 16], HBFs(h, tt * 128, 128), EMB[:, :],
                       waits=[("se", rt_prev)] if h == 0 else (),
                       tick=(h == NH - 1))
                lv = P.cnt["pe"]
                VRD(AFa[:, 0:8], PS6[:, 0:128].rearrange("p (a b) -> p a b", b=16),
                    ALU.max, waits=[("pe", lv)])
                VSEP()
                nv = VTS(AFa[:, 8:16], AFa[:, 0:8], -1.0, None, ALU.mult, tick=True)
                for h in range(NH):
                    rt_prev = ACT(SCRf(0, 128)[:, h * 16:(h + 1) * 16],
                                  PS6[:, h * 16:(h + 1) * 16], AF.Exp,
                                  bias=AFa[:, 8 + h:9 + h],
                                  waits=[("ve", nv)] if h == 0 else (),
                                  tick=(h == NH - 1))
                VRD(AFa[:, 16:24], SCRf(0, 128).rearrange("p (a b) -> p a b", b=16),
                    ALU.add, waits=[("se", rt_prev)])
                VSEP()
                P.do("vector", lambda v: v.reciprocal(AFa[:, 24:32], AFa[:, 16:24]))
                VSEP()
                for h in range(NH):
                    VTS(WTT[:, tt * 128 + h * 16: tt * 128 + (h + 1) * 16],
                        SCRf(0, 128)[:, h * 16:(h + 1) * 16],
                        AFa[:, 24 + h:25 + h], None, ALU.mult)
                VSEP()
                VRD(AFa[:, 32:40],
                    WTT[:, tt * 128:(tt + 1) * 128].rearrange("p (a b) -> p a b", b=16),
                    ALU.max)
                VSEP()
                P.do("vector", lambda v, tt=tt: v.match_replace(
                    SCRf(1, 128), AFa[:, 32:40], WTT[:, tt * 128:(tt + 1) * 128],
                    -1.0e30))
                VSEP()
                VRD(AFa[:, 40:48], SCRf(1, 128).rearrange("p (a b) -> p a b", b=16),
                    ALU.max)
                VSEP()
                for h in range(NH):
                    VTS(SCRf(1, 128)[:, h * 16:(h + 1) * 16],
                        WTT[:, tt * 128 + h * 16: tt * 128 + (h + 1) * 16],
                        AFa[:, 40 + h:41 + h], None, ALU.is_ge)
                VSEP()
                VTT(WTT[:, tt * 128:(tt + 1) * 128], WTT[:, tt * 128:(tt + 1) * 128],
                    SCRf(1, 128), ALU.mult)
                wfmv = VCP(WTB[:, tt * 128:(tt + 1) * 128],
                           WTT[:, tt * 128:(tt + 1) * 128], tick=True)
            P.wait("tensor", [("ve", wfmv)])

            geluv = [0, 0]
            sttv = {s: 0 for s in range(8)}
            b2ev = 0
            for h in range(NH):
                for tt in range(NTT):
                    TR(PS7B[0:16, tt * 128:(tt + 1) * 128],
                       WTB[:, tt * 128 + h * 16: tt * 128 + (h + 1) * 16],
                       waits=[("ve", b2ev)] if tt == 0 else (),
                       tick=(tt == NTT - 1))
                trw = P.cnt["pe"]
                wfv = VCP(WFMH[0:16, 0:NTT * 128], PS7B[0:16, 0:NTT * 128],
                          waits=[("pe", trw)], tick=True)
                for tt in range(NTT):
                    pb = PS01 if tt < 4 else PS23
                    MM(pb[:, (tt % 4) * 128:(tt % 4 + 1) * 128],
                       WFMH[0:16, tt * 128:(tt + 1) * 128],
                       B2B[0:16, :],
                       waits=([("ve", wfv), ("se", max(geluv))]
                              if tt == 0 else ()),
                       tick=(tt == NTT - 1))
                b2t = P.cnt["pe"]
                if NTT > 4:
                    b2ev = VCP(AFa[:, F0["CMBT"]:F0["CMBT"] + 4 * 128],
                               PS01[:, 0:4 * 128],
                               waits=[("pe", b2t), ("ve", max(sttv.values()))],
                               tick=True)
                    b2ev = VCP(AFa[:, F0["CMBT"] + 4 * 128:F0["CMBT"] + NTT * 128],
                               PS23[:, 0:(NTT - 4) * 128], tick=True)
                else:
                    b2ev = VCP(AFa[:, F0["CMBT"]:F0["CMBT"] + NTT * 128],
                               PS01[:, 0:NTT * 128],
                               waits=[("pe", b2t), ("ve", max(sttv.values()))],
                               tick=True)
                for e in range(NE):
                    par_he = (h * NE + e) % 2
                    for it in range(4):
                        pd = PS01 if it % 2 == 0 else PS23
                        for n2 in range(NW1):
                            ws = []
                            if n2 == 0:
                                ws = [("se", geluv[it % 2])]
                                if e == 0:
                                    ws.append(("ve", b2ev))
                            MM(pd[:, n2 * W1W:(n2 + 1) * W1W], W1B(e, it),
                               HBFs(h, n2 * W1W, W1W), start=True, stop=True,
                               waits=ws, tick=(n2 == NW1 - 1))
                        mmt = P.cnt["pe"]
                        geluv[it % 2] = ACT(
                            HMIDs(par_he, it, 0, TH), pd[:, 0:TH], AF.Gelu,
                            bias=CSM[:, 32 + e * 4 + it:33 + e * 4 + it],
                            waits=[("pe", mmt)], tick=True)
                    gall = max(geluv)
                    for tt in range(NTT):
                        bank = tt % 2
                        pd2 = PS4 if bank == 0 else PS5
                        so = ((tt // 2) % 4) * 128
                        s = tt % 8
                        for it in range(4):
                            ws = []
                            if it == 0:
                                ws = [("se", gall),
                                      ("ve", max(sttv[s], sttv.get(tt % 2 + ((tt - 2) % 8 // 2) * 2, 0) if tt >= 2 else 0, b2ev))]
                                ws = [("se", gall),
                                      ("ve", max(sttv[s],
                                                 sttv[(tt - 2) % 8] if tt >= 2 else 0,
                                                 b2ev))]
                            MM(pd2[:, so:so + 128],
                               HMIDs(par_he, it, tt * 128, 128), W2B(e, it),
                               start=(it == 0), stop=(it == 3),
                               waits=ws, tick=(it == 3))
                        w2t = P.cnt["pe"]
                        sttv[s] = VSTT(
                            CMBT(tt), pd2[:, so:so + 128],
                            WTT[:, tt * 128 + h * 16 + e: tt * 128 + h * 16 + e + 1],
                            CMBT(tt), ALU.mult, ALU.add,
                            waits=[("pe", w2t)], tick=True)
                csc = A0["HMID"] + 4 * TH
                cbv = 0
                for tt in range(NTT):
                    cbv = VCP(AB[:, csc + tt * 128:csc + (tt + 1) * 128], CMBT(tt),
                              tick=(tt == NTT - 1))
                for tt in range(NTT):
                    TR(PS7B[:, tt * 128:(tt + 1) * 128],
                       AB[:, csc + tt * 128:csc + (tt + 1) * 128],
                       waits=[("ve", cbv)] if tt == 0 else (),
                       tick=(tt == NTT - 1))
                trt = P.cnt["pe"]
                b2ev = VCP(CMBFs(h, 0, NTT * 128), PS7B[:, 0:NTT * 128],
                           waits=[("pe", trt)], tick=True)
            P.wait("tensor", [("ve", P.cnt["ve"]), ("se", P.cnt["se"])])
            rv = [0, 0]
            for tc in range(NTCH):
                for do in range(NDJ):
                    pd = PS01 if do % 2 == 0 else PS23
                    for h in range(NH):
                        ws = []
                        if h == 0:
                            ws = [("ve", rv[do % 2])]
                        MM(pd[:, 0:TCW], WOB(h, do), CMBFs(h, tc * TCW, TCW),
                           start=(h == 0), stop=(h == NH - 1),
                           waits=ws, tick=(h == NH - 1))
                    wot = P.cnt["pe"]
                    VTS(SCRf(1), pd[:, 0:TCW], CSM[:, 8 + do:9 + do], 0.5,
                        ALU.add, ALU.mult, waits=[("pe", wot)])
                    rs = RES[:, do * T + t0 + tc * TCW: do * T + t0 + tc * TCW + TCW]
                    rv[do % 2] = VTT(rs, rs, SCRf(1), ALU.add, tick=True)
            P.wait("tensor", [("ve", max(rv)), ("se", P.cnt["se"])])

    def emit_attn():
        P.barrier()
        if "ALL" in OPT_SER or "AB" in OPT_SER or "ST" in OPT_SER:
            P.serialize = True
        stage(WBF, 0, dr["qw"], 8192)
        stage(WBF, 8192, dr["kw"], 8192)
        stage(WBF, 16384, dr["vw"], 8192)
        stage(WBF, 24576, dr["gw"], 8192)
        stage(AB, A1["OWB"], dr["ow"], 8192)
        wv = stage(AB, A1["BWA"], dr["bwaw"], 128)
        P.wait("tensor", [("ve", wv)])
        if OPT_SER == "ST":
            P.serialize = False
        QWB = lambda dj, n: WBF[:, dj * 1024 + n: dj * 1024 + n + 512]
        KWB = lambda dj, n: WBF[:, 8192 + dj * 1024 + n: 8192 + dj * 1024 + n + 512]
        VWB = lambda dj, n: WBF[:, 16384 + dj * 1024 + n: 16384 + dj * 1024 + n + 512]
        GWB = lambda dj, n: WBF[:, 24576 + dj * 1024 + n: 24576 + dj * 1024 + n + 512]
        OWBs = lambda h, do: AB[:, A1["OWB"] + h * 1024 + do * 128:
                                A1["OWB"] + h * 1024 + (do + 1) * 128]
        BWAs = lambda dj: AB[:, A1["BWA"] + dj * 16: A1["BWA"] + (dj + 1) * 16]
        HB1 = lambda ci, dj: AB[:, A1["HB1"] + ci * 1024 + dj * 128:
                                A1["HB1"] + ci * 1024 + (dj + 1) * 128]
        def TMS(base):
            return lambda ci, lo, w: AB[:, A1[base] + ci * 1024 + lo:
                                        A1[base] + ci * 1024 + lo + w]
        QNs, KNs, VNs, GNs = TMS("QN"), TMS("KN"), TMS("VN"), TMS("GN")
        SCN = A1["SCN"]
        ABS = lambda k, w=128: AB[:, SCN + k * 128: SCN + k * 128 + w]
        QS = lambda k: AFa[:, F1["QS"] + k * 1024: F1["QS"] + (k + 1) * 1024]
        XET = AFa[:, F1["XET"]:F1["XET"] + 128]
        OTM = AFa[:, F1["OTM"]:F1["OTM"] + 128]
        X1P = lambda dj: AFa[:, F1["X1P"] + dj * 128: F1["X1P"] + (dj + 1) * 128]
        SCR2 = lambda k, w=128: AFa[:, F1["SCR2"] + k * 128: F1["SCR2"] + k * 128 + w]
        ZR = lambda ci, dj: STG[:, ci * 1024 + dj * 128: ci * 1024 + (dj + 1) * 128]
        SCc = lambda ci, k, w=8: SC8[:, ci * 80 + k: ci * 80 + k + w]

        z0 = P.do("vector", lambda v: v.memset(S0[:, :], 0.0), tick=True)
        P.wait("tensor", [("ve", z0)])

        NG = CNC // GRP
        if OPT_P1_STOP < 0:
            NG = 0
        NG = min(NG, OPT_NG)
        vseq = 0
        h2v = 0
        for g in range(NG):
            for ci in range(GRP):
                c = g * GRP + ci
                w0 = c * 128
                pemm = ln_stats(w0, 128, F1["SCR2"])
                co = ci * 128
                VTS(RWS[0:1, co:co + 128], PS6[0:1, 0:128], 1.0 / D, None, ALU.mult,
                    waits=[("pe", pemm)])
                VTT(RWS[32:33, co:co + 128], RWS[0:1, co:co + 128],
                    RWS[0:1, co:co + 128], ALU.mult)
                vv = VSTT(RWS[32:33, co:co + 128], PS5[0:1, 0:128], 1.0 / D,
                          RWS[32:33, co:co + 128], ALU.mult, ALU.subtract, tick=True)
                TMP = RWS[32:33, 256 + co:256 + co + 128]
                ACT(TMP, RWS[32:33, co:co + 128], AF.Ln,
                    bias=1e-5, waits=[("ve", vv)])
                ACT(RWS[64:65, co:co + 128], TMP, AF.Exp,
                    scale=-0.5)
                ACT(TMP, TMP, AF.Exp,
                    scale=-1.0, tick=True)
                s2 = P.cnt["se"]
                v3 = VTT(TMP, RWS[32:33, co:co + 128],
                         TMP, ALU.mult,
                         waits=[("se", s2)], tick=True)
                ACT(TMP, TMP, AF.Ln,
                    bias=1e-6, waits=[("ve", v3)])
                ACT(TMP, TMP, AF.Exp,
                    scale=-0.5, tick=True)
                s3 = P.cnt["se"]
                VCP(RWS[64:65, 256 + co:256 + co + 128], TMP, waits=[("se", s3)])
                VTT(RWS[64:65, 256 + co:256 + co + 128],
                    RWS[64:65, co:co + 128],
                    RWS[64:65, 256 + co:256 + co + 128], ALU.mult)
                VCP(RWR[0:1, co:co + 128], RWS[0:1, co:co + 128])
                VCP(RWR[32:33, co:co + 128], RWS[64:65, co:co + 128])
                v4 = VCP(RWR[64:65, co:co + 128],
                         RWS[64:65, 256 + co:256 + co + 128], tick=True)
                MM(PS4[:, 0:128], ONESR_R[0:1, :], RWR[0:1, co:co + 128],
                   waits=[("ve", v4)])
                MM(PS5[:, 0:128], ONESR_R[32:33, :], RWR[32:33, co:co + 128])
                bc = MM(PS01[:, 0:128], ONESR_R[64:65, :], RWR[64:65, co:co + 128],
                        tick=True)
                for dj in range(NDJ):
                    xs = RES[:, dj * T + w0: dj * T + w0 + 128]
                    VTT(SCR2(0), xs, PS4[:, 0:128], ALU.subtract,
                        waits=[("pe", bc)] if dj == 0 else ())
                    VTT(ZR(ci, dj), SCR2(0), PS5[:, 0:128], ALU.mult)
                    VTT(SCR2(1), SCR2(0), PS01[:, 0:128], ALU.mult)
                    VTS(HB1(ci, dj), SCR2(1), CSM[:, 16 + dj:17 + dj], None,
                        ALU.mult, tick=(dj == NDJ - 1))
                vseq = P.cnt["ve"]
                P.wait("tensor", [("ve", vseq)])

            if OPT_P1_STOP < 1:
                continue
            silv = [0, 0]
            for ci in range(GRP):
                for ip, wsl in enumerate([QWB, KWB]):
                    pd = PS01 if ip % 2 == 0 else PS23
                    qsv = 0
                    for n2 in range(2):
                        for dj in range(NDJ):
                            ws = []
                            if dj == 0:
                                ws = [("se", silv[ip % 2]), ("ve", vseq)]
                            MM(pd[:, n2 * 512:(n2 + 1) * 512], HB1(ci, dj),
                               wsl(dj, n2 * 512), start=(dj == 0),
                               stop=(dj == NDJ - 1), waits=ws,
                               tick=(dj == NDJ - 1))
                        mmt = P.cnt["pe"]
                        silv[ip % 2] = ACT(QS(ip)[:, n2 * 512:(n2 + 1) * 512],
                                           pd[:, n2 * 512:(n2 + 1) * 512], AF.Silu,
                                           waits=[("pe", mmt)], tick=True)
                    dst = QNs if ip == 0 else KNs
                    sso = 32 + ci * 32 + ip * 8
                    for hh in range(NH):
                        VSTT(SCR2(0), QS(ip)[:, hh * 128:(hh + 1) * 128], 1.0,
                             QS(ip)[:, hh * 128:(hh + 1) * 128], ALU.mult, ALU.mult,
                             accum=SCc(ci, 64 + ip * 8 + hh, 1),
                             waits=[("se", silv[ip % 2])] if hh == 0 else ())
                    vseq = VCP(dst(ci, 0, 1024), QS(ip)[:, 0:1024], tick=True)
                for ip, wsl in enumerate([VWB, GWB]):
                    pd = PS01 if ip % 2 == 0 else PS23
                    dst = VNs if ip == 0 else GNs
                    for n2 in range(2):
                        for dj in range(NDJ):
                            ws = []
                            if dj == 0:
                                ws = [("se", silv[ip % 2])]
                            MM(pd[:, n2 * 512:(n2 + 1) * 512], HB1(ci, dj),
                               wsl(dj, n2 * 512), start=(dj == 0),
                               stop=(dj == NDJ - 1), waits=ws,
                               tick=(dj == NDJ - 1))
                        mmt = P.cnt["pe"]
                        silv[ip % 2] = ACT(dst(ci, n2 * 512, 512),
                                           pd[:, n2 * 512:(n2 + 1) * 512], AF.Silu,
                                           waits=[("pe", mmt)], tick=True)
                for dj in range(NDJ):
                    MM(PS6[:, 0:16], HB1(ci, dj), BWAs(dj),
                       start=(dj == 0), stop=(dj == NDJ - 1),
                       waits=[("ve", vseq)] if dj == 0 else (),
                       tick=(dj == NDJ - 1))
                bat = P.cnt["pe"]
                if dbg:
                    VCP(SC8[:, GRP * 80 + 8: GRP * 80 + 16], PS6[:, 8:16],
                        waits=[("pe", bat)])
                tb = ACT(SCR2(0, 8), PS6[:, 0:8], AF.Tanh, scale=0.5,
                         waits=[("pe", bat)], tick=True)
                VTS(SCc(ci, 0), SCR2(0, 8), 0.5, 0.5, ALU.mult, ALU.add,
                    waits=[("se", tb)])
                VSEP()
                VTS(SCc(ci, 8), SCc(ci, 0), -1.0, None, ALU.mult)

                vseq = VCP(SCc(ci, 40), PS6[:, 8:16], tick=True)
                P.wait("tensor", [("ve", vseq)])

            if OPT_P1_STOP < 2:
                continue
            P.serialize = ("C" in OPT_SER) or ("ALL" in OPT_SER)
            for ci in range(GRP):
                c = g * GRP + ci
                v5 = VTT(SCR2(0, 8), SCc(ci, 40), SMALL[:, 8:16], ALU.add,
                         waits=[("se", P.cnt["se"])], tick=True)
                if dbg:
                    VCP(SC8[:, GRP * 80 + ci * 8: GRP * 80 + 8 + ci * 8],
                        SCc(ci, 40))
                    VCP(SC8[:, GRP * 80 + 16 + ci * 8 if False else GRP * 80 + 8 + ci * 8 - 8 + 8: GRP * 80 + 16 + ci * 8 - 8 + 8] if False else SC8[:, 0:0], SC8[:, 0:0]) if False else None
                ACT(SCR2(1, 8), SCR2(0, 8), AF.Abs, waits=[("ve", v5)])
                ACT(SCR2(1, 8), SCR2(1, 8), AF.Exp, scale=-1.0)
                sp1 = ACT(SCR2(1, 8), SCR2(1, 8), AF.Ln, bias=1.0, tick=True)
                VTS(SCR2(2, 8), SCR2(0, 8), 0.0, None, ALU.max, waits=[("se", sp1)])
                VSEP()
                VTT(SCR2(2, 8), SCR2(2, 8), SCR2(1, 8), ALU.add)
                VSEP()
                v6 = VTT(SCc(ci, 16), SCR2(2, 8), SMALL[:, 0:8], ALU.mult, tick=True)
                MM(PS6[:, 64:72], UI[:, :], SCc(ci, 16),
                   waits=[("ve", v6), ("se", P.cnt["se"])])
                ct = MM(PS6[0:1, 96:104], ONESC[:, :], SCc(ci, 16), tick=True)
                VCP(SCc(ci, 24), PS6[:, 64:72], waits=[("pe", ct)])
                vtt_ = VCP(TOTR[0:1, c * 8:(c + 1) * 8], PS6[0:1, 96:104], tick=True)
                a1 = ACT(SCc(ci, 32), PS6[:, 64:72], AF.Exp,
                         waits=[("pe", ct)], tick=True)
                VSEP()
                v7 = VCP(TOTRR[0:1, c * 8:(c + 1) * 8], TOTR[0:1, c * 8:(c + 1) * 8],
                         waits=[("ve", vtt_)], tick=True)
                tb8 = MM(PS6[:, 104:112], ONESR_R[0:1, :],
                         TOTRR[0:1, c * 8:(c + 1) * 8],
                         waits=[("ve", v7), ("se", a1)], tick=True)
                v8 = VTT(SCR2(0, 8), PS6[:, 104:112], SCc(ci, 24), ALU.subtract,
                         waits=[("pe", tb8)], tick=True)
                ACT(SCc(ci, 48), SCR2(0, 8), AF.Exp, waits=[("ve", v8)])
                ac1 = ACT(SCc(ci, 56), PS6[:, 104:112], AF.Exp, tick=True)
                VTT(SCc(ci, 40), SCc(ci, 8), SCc(ci, 32), ALU.mult,
                    waits=[("se", ac1)])
                lq = ACT(SCR2(0, 16), SCc(ci, 64, 16), AF.Ln, bias=1e-6)
                eq = ACT(SCR2(0, 16), SCR2(0, 16), AF.Exp, scale=-0.5, tick=True)
                for hh in range(NH):
                    VTS(QNs(ci, hh * 128, 128), QNs(ci, hh * 128, 128),
                        SCR2(0, 16)[:, hh:hh + 1], None, ALU.mult,
                        waits=[("se", eq)] if hh == 0 else ())
                    VTS(KNs(ci, hh * 128, 128), KNs(ci, hh * 128, 128),
                        SCR2(0, 16)[:, 8 + hh:9 + hh], None, ALU.mult,
                        tick=(hh == NH - 1))
                vseq = P.cnt["ve"]

                if OPT_P1_STOP < 3:
                    continue
                ofmv = [0] * NH
                for h in range(NH):
                    v9 = VTS(ABS(24), KNs(ci, h * 128, 128), SCc(ci, 8)[:, h:h + 1],
                             None, ALU.mult, waits=[("ve", vseq)] if h == 0 else (),
                             tick=True)
                    TR(PS7B[:, 0:128], QNs(ci, h * 128, 128), waits=[("ve", v9)])
                    TR(PS7B[:, 128:256], KNs(ci, h * 128, 128))
                    t1 = TR(PS7B[:, 256:384], ABS(24), tick=True)
                    v10 = VCP(ABS(0, 384), PS7B[:, 0:384], waits=[("pe", t1)],
                              tick=True)
                    VTS(SCR2(0), UI[:, :], SCc(ci, 16)[:, h:h + 1], None, ALU.mult,
                        waits=[("se", P.cnt["se"])] if h == 0 else ())
                    v11 = VTS(SCR2(1), MST[:, :], SCc(ci, 16)[:, h:h + 1], None,
                              ALU.mult, tick=True)
                    MM(PS01[:, 0:128], SU[:, :], SCR2(0),
                       waits=[("ve", v11), ("se", P.cnt["se"])])
                    x2 = MM(PS01[:, 512:640], UI[:, :], SCR2(1), tick=True)
                    MM(PS4[:, 128:256], ABS(1), ABS(2),
                       waits=[("ve", v10)])
                    MM(PS4[:, 256:384], ABS(1), ABS(0))
                    kk2 = MM(PS5[:, 128:256], ABS(2), ABS(1), tick=True)
                    xe1 = ACT(XET, PS01[:, 0:128], AF.Exp, waits=[("pe", x2)],
                              tick=True)
                    xe2 = ACT(SCR2(2), PS01[:, 512:640], AF.Exp, tick=True)
                    VTT(SCR2(3), PS4[:, 128:256], XET, ALU.mult,
                        waits=[("se", xe1), ("pe", kk2)])
                    VTT(ABS(3), SCR2(3), MSK[:, :], ALU.mult)
                    VTT(SCR2(3), PS4[:, 256:384], XET, ALU.mult)
                    VTT(ABS(9), SCR2(3), UI[:, :], ALU.mult)
                    VTT(SCR2(3), PS5[:, 128:256], SCR2(2), ALU.mult,
                        waits=[("se", xe2)])
                    VTT(ABS(4), SCR2(3), MST[:, :], ALU.mult)
                    nv0 = VTT(ABS(7), IDB[:, :], ABS(3), ALU.add, tick=True)
                    P.wait("tensor", [("ve", nv0)])
                    qcur, qnxt = 7, 8
                    ncur, tcur, nnxt, tnxt = 3, 4, 5, 6
                    for k in range(6):
                        MM(PS4[:, 0:128], ABS(tcur), ABS(ncur))
                        m1_ = MM(PS5[:, 0:128], ABS(ncur), ABS(tcur), tick=True)
                        ev1 = VCP(ABS(nnxt), PS4[:, 0:128], waits=[("pe", m1_)])
                        ev1 = VCP(ABS(tnxt), PS5[:, 0:128], tick=True)
                        m2_ = MM(PS4[:, 128:256], ABS(tnxt), ABS(qcur),
                                 waits=[("ve", ev1)], tick=True)
                        qv = VTT(ABS(qnxt), PS4[:, 128:256], ABS(qcur), ALU.add,
                                 waits=[("pe", m2_)], tick=True)
                        P.wait("tensor", [("ve", qv)])
                        qcur, qnxt = qnxt, qcur
                        ncur, nnxt = nnxt, ncur
                        tcur, tnxt = tnxt, tcur
                    WINV = ABS(qcur)
                    if OPT_P1_STOP < 4:
                        ofmv[h] = P.cnt["ve"]
                        continue
                    sv0 = VCP(ABS(13), S0[:, h * 128:(h + 1) * 128], tick=True)
                    ks0 = MM(PS23[:, 0:128], ABS(1), ABS(13), waits=[("ve", sv0)],
                             tick=True)
                    VTS(ABS(11), VNs(ci, h * 128, 128), SCc(ci, 0)[:, h:h + 1],
                        None, ALU.mult)
                    rv_ = VSTT(ABS(12), PS23[:, 0:128], SCc(ci, 40)[:, h:h + 1],
                               ABS(11), ALU.mult, ALU.add,
                               waits=[("pe", ks0)], tick=True)
                    up = MM(PS23[:, 512:640], WINV, ABS(12), waits=[("ve", rv_)],
                            tick=True)
                    uv = VCP(ABS(10), PS23[:, 512:640], waits=[("pe", up)], tick=True)
                    MM(PS23[:, 128:256], ABS(9), ABS(10), waits=[("ve", uv)])
                    o2 = MM(PS23[:, 256:384], ABS(0), ABS(13), tick=True)
                    VTS(SCR2(3), PS23[:, 256:384], SCc(ci, 32)[:, h:h + 1], None,
                        ALU.mult, waits=[("pe", o2)])
                    ov = VTT(OTM, PS23[:, 128:256], SCR2(3), ALU.add, tick=True)
                    kt_ = VTS(ABS(14), KNs(ci, h * 128, 128),
                              SCc(ci, 48)[:, h:h + 1], None, ALU.mult, tick=True)
                    sp_ = MM(PS23[:, 640:768], ABS(14), ABS(10),
                             waits=[("ve", kt_)], tick=True)
                    s0v = VSTT(S0[:, h * 128:(h + 1) * 128],
                               S0[:, h * 128:(h + 1) * 128],
                               SCc(ci, 56)[:, h:h + 1], PS23[:, 640:768],
                               ALU.mult, ALU.add, waits=[("pe", sp_)], tick=True)
                    if OPT_P1_STOP < 5:
                        ofmv[h] = s0v
                        continue
                    VTT(SCR2(3), OTM, OTM, ALU.mult, waits=[("ve", ov)])
                    VRD(AFa[:, 32:33], SCR2(3), ALU.add)
                    VSEP()
                    vsq = VSEP(tick=True)
                    ACT(AFa[:, 33:34], AFa[:, 32:33], AF.Ln, scale=1.0 / 128,
                        bias=1e-6, waits=[("ve", vsq)])
                    on1 = ACT(AFa[:, 33:34], AFa[:, 33:34], AF.Exp, scale=-0.5,
                              tick=True)
                    VTS(OTM, OTM, AFa[:, 33:34], None, ALU.mult, waits=[("se", on1)])
                    VTT(OTM, OTM, ONB[:, :], ALU.mult)
                    obv = VTT(ABS(15), OTM, GNs(ci, h * 128, 128), ALU.mult,
                              tick=True)
                    tro = TR(PS7B[:, 384:512], ABS(15), waits=[("ve", obv)],
                             tick=True)
                    ofmv[h] = VCP(ABS(16 + h), PS7B[:, 384:512],
                                  waits=[("pe", tro)], tick=True)
                    vseq = ofmv[h]
                if OPT_P1_STOP < 6:
                    continue
                if "OW" in OPT_SER or "O1" in OPT_SER:
                    P.serialize = True
                w0 = c * 128
                rvv = 0
                x1v = [0] * NDJ
                for do in range(NDJ):
                    pd = PS5 if do % 2 == 0 else PS23
                    po = 0 if do % 2 == 0 else 384
                    for h in range(NH):
                        ws = []
                        if h == 0 and do == 0:
                            ws = [("ve", max(ofmv))]
                        elif h == 0 and do >= 2:
                            ws = [("ve", x1v[do - 2])]
                        MM(pd[:, po:po + 128], OWBs(h, do), ABS(16 + h),
                           start=(h == 0), stop=(h == NH - 1), waits=ws,
                           tick=(h == NH - 1))
                    owt = P.cnt["pe"]
                    x1v[do] = VSTT(X1P(do), pd[:, po:po + 128], 0.5, ZR(ci, do),
                                   ALU.mult, ALU.add, waits=[("pe", owt)], tick=True)
                    rs = RES[:, do * T + w0: do * T + w0 + 128]
                    VTT(rs, rs, X1P(do), ALU.add)
                if "O1" in OPT_SER:
                    P.serialize = False
                if "O2" in OPT_SER:
                    P.serialize = True
                if OPT_P1_STOP < 7:
                    continue
                sqv = [0, 0]
                smv2 = [0] * NDJ
                for dj in range(NDJ):
                    sqv[dj % 2] = ACT(SCR2(dj % 2), X1P(dj), AF.Square,
                                      waits=([("pe", smv2[dj - 2])] if dj >= 2
                                             else [("ve", x1v[dj])]), tick=True)
                    smv2[dj] = MM(PS01[0:1, 0:128], ONESC[:, :], SCR2(dj % 2),
                                  start=(dj == 0), stop=(dj == NDJ - 1),
                                  waits=[("se", sqv[dj % 2])], tick=True)
                ACT(RWS[0:1, 384:512], PS01[0:1, 0:128], AF.Ln, scale=1.0 / D,
                    bias=1e-6, waits=[("pe", smv2[NDJ - 1])])
                r2s = ACT(RWS[0:1, 384:512], RWS[0:1, 384:512], AF.Exp, scale=-0.5,
                          tick=True)
                r2v = VCP(RWR[0:1, 384:512], RWS[0:1, 384:512],
                          waits=[("se", r2s)], tick=True)
                rb2 = MM(PS6[:, 0:128], ONESR_R[0:1, :], RWR[0:1, 384:512],
                         waits=[("ve", r2v)], tick=True)
                if OPT_P1_STOP < 8:
                    continue
                for dj in range(NDJ):
                    VTT(SCR2(0), X1P(dj), PS6[:, 0:128], ALU.mult,
                        waits=([("pe", rb2), ("ldg", h2v)] if dj == 0 else ()))
                    VTS(ABS(25 + dj), SCR2(0), CSM[:, 24 + dj:25 + dj], None,
                        ALU.mult, tick=(dj == NDJ - 1))
                if "OW" in OPT_SER:
                    P.serialize = False
                h2t = P.cnt["ve"]
                if not OPT_NO_H2D:
                    for dj in range(NDJ):
                        h2v = P.dma("sync",
                                    h2d[:, dj * T + w0: dj * T + w0 + 128],
                                    AB[:, SCN + (25 + dj) * 128:
                                       SCN + (26 + dj) * 128],
                                    waits=[("ve", h2t)] if dj == 0 else (),
                                    sem="ldg")
                vseq = h2t
        P.serialize = False
        if not OPT_NO_H2D:
            P.wait("sync", [("ldg", h2v)])

    def emit_ffn2():
        P.barrier()
        pwv = stage(WBF, 0, dr["pw"], 32768)
        FFAs = lambda ff: AB[:, A2["FFA"] + ff * P2W: A2["FFA"] + (ff + 1) * P2W]
        H2Ls = lambda dj: AB[:, A2["H2L"] + dj * P2W: A2["H2L"] + (dj + 1) * P2W]
        FCRs = lambda r: AB[:, A2["FCR"] + r * 1024: A2["FCR"] + (r + 1) * 1024]
        PWB = lambda ff, do: WBF[:, ff * 1024 + do * 128: ff * 1024 + (do + 1) * 128]
        fcv = {r: 0 for r in range(4)}
        fcu = {r: 0 for r in range(4)}
        h2ld = 0
        silv = [0, 0]
        evv = [0, 0]
        for tc in range(NP2):
            w0 = tc * P2W
            for dj in range(NDJ):
                h2ld = P.dma("sync",
                             AB[:, A2["H2L"] + dj * P2W: A2["H2L"] + (dj + 1) * P2W],
                             h2d[:, dj * T + w0: dj * T + w0 + P2W],
                             waits=([("ve", evv[0]), ("ve", evv[1])]
                                    if dj == 0 else ()), sem="ldh")
            for ff in range(32):
                r = ff % 4
                par = stg["par"]
                stg["par"] ^= 1
                sem = "sga" if par == 0 else "sgb"
                for dj in range(NDJ):
                    ldv = P.dma("sync", STG[:, par * 1024 + dj * 128:
                                            par * 1024 + (dj + 1) * 128],
                                dr["fcw"][:, dj * 4096 + ff * 128:
                                          dj * 4096 + (ff + 1) * 128],
                                waits=([("ve", stg["cast"][par])]
                                       if dj == 0 else ()), sem=sem)
                cv = VCP(FCRs(r), STG[:, par * 1024: par * 1024 + 1024],
                         waits=[(sem, ldv), ("pe", fcu[r])], tick=True)
                stg["cast"][par] = cv
                fcv[r] = cv
                pd = PS01 if ff % 2 == 0 else PS23
                for dj in range(NDJ):
                    ws = []
                    if dj == 0:
                        ws = [("ve", fcv[r]), ("se", silv[ff % 2]),
                              ("ldh", h2ld)]
                    MM(pd[:, 0:P2W], FCRs(r)[:, dj * 128:(dj + 1) * 128],
                       H2Ls(dj), start=(dj == 0), stop=(dj == NDJ - 1),
                       waits=ws, tick=(dj == NDJ - 1))
                fcu[r] = P.cnt["pe"]
                silv[ff % 2] = ACT(FFAs(ff), pd[:, 0:P2W], AF.Silu,
                                   waits=[("pe", fcu[r])], tick=True)
            sall = max(silv)
            for do in range(NDJ):
                pd = PS4 if do % 2 == 0 else PS5
                for ff in range(32):
                    ws = []
                    if ff == 0:
                        ws = [("se", sall), ("ve", evv[do % 2]),
                              ("ve", pwv)]
                    MM(pd[:, 0:P2W], PWB(ff, do), FFAs(ff),
                       start=(ff == 0), stop=(ff == 31), waits=ws,
                       tick=(ff == 31))
                pwt = P.cnt["pe"]
                VTS(AFa[:, 64:64 + P2W], pd[:, 0:P2W], 0.5, None, ALU.mult,
                    waits=[("pe", pwt)])
                rs = RES[:, do * T + w0: do * T + w0 + P2W]
                evv[do % 2] = VTT(rs, rs, AFa[:, 64:64 + P2W], ALU.add, tick=True)
        P.wait("tensor", [("ve", max(evv))])

    if phases[0]:
        emit_ffn()
    if phases[1]:
        emit_attn()
    if phases[2]:
        emit_ffn2()
    if phases[3]:
        emit_ffn()

    P.barrier()
    stv = 0
    for dj in range(NDJ):
        stv = P.dma("sync", y[:, dj * T:(dj + 1) * T], RES[:, dj * T:(dj + 1) * T],
                    sem="st")
    if dbg:
        dab = nc.dram_tensor("dbgab", [128, 24576], BF16, kind="ExternalOutput").ap()
        dwt = nc.dram_tensor("dbgwt", [128, NTT * 128], F32, kind="ExternalOutput").ap()
        dsc = nc.dram_tensor("dbgsc", [128, GRP * 80 + 16], F32, kind="ExternalOutput").ap()
        ds0 = nc.dram_tensor("dbgs0", [128, NH * 128], F32, kind="ExternalOutput").ap()
        dst_ = nc.dram_tensor("dbgstg", [128, 2048], F32, kind="ExternalOutput").ap()
        dwb = nc.dram_tensor("dbgwbf", [128, 32768], BF16, kind="ExternalOutput").ap()
        daf = nc.dram_tensor("dbgaf", [128, 3904], F32, kind="ExternalOutput").ap()
        dsm = nc.dram_tensor("dbgsm", [128, 96], F32, kind="ExternalOutput").ap()
        for a in range(0, 24576, 8192):
            stv = P.dma("sync", dab[:, a:a + 8192], AB[:, a:a + 8192], sem="st")
        stv = P.dma("sync", dwt, WTT[:, :], sem="st")
        stv = P.dma("sync", dsc, SC8[:, :], sem="st")
        stv = P.dma("sync", ds0, S0[:, :], sem="st")
        stv = P.dma("sync", dst_, STG[:, :], sem="st")
        for a in range(0, 32768, 8192):
            stv = P.dma("sync", dwb[:, a:a + 8192], WBF[:, a:a + 8192], sem="st")
        stv = P.dma("sync", daf, AFa[:, :], sem="st")
        stv = P.dma("sync", dsm[:, 0:16], SMALL[:, :], sem="st")
        stv = P.dma("sync", dsm[0:1, 16:80], ROW8[0:1, 0:64], sem="st")
    P.wait("sync", [("st", stv)])

    with nc.Block() as block:
        P.emit(block)
    return nc


def _fm(w):
    n = w.shape[1]
    return np.ascontiguousarray(w.reshape(8, 128, n).transpose(1, 0, 2)
                                .reshape(128, 8 * n))


def _prep_inputs(inputs, T):
    f32 = np.float32
    g = {k: np.asarray(v, dtype=f32) for k, v in inputs.items()}
    shared = {
        "wm": _fm(g["L0_ffn_Wm"]),
        "w1t": np.ascontiguousarray(g["L0_ffn_W1"].transpose(1, 0, 2)
                                    .reshape(128, 16 * 512)),
        "w2r": np.ascontiguousarray(g["L0_ffn_W2"].reshape(64, 128, 128)
                                    .transpose(1, 0, 2).reshape(128, 64 * 128)),
        "wo": _fm(g["L0_ffn_Wo"]),
        "qw": _fm(g["L0_attn_qw"]),
        "kw": _fm(g["L0_attn_kw"]),
        "vw": _fm(g["L0_attn_vw"]),
        "gw": _fm(g["L0_attn_gw"]),
        "ow": _fm(g["L0_attn_ow"]),
        "fcw": _fm(g["L0_attn_fcw"]),
        "pw": np.ascontiguousarray(g["L0_attn_pw"].reshape(32, 128, 1024)
                                   .transpose(1, 0, 2).reshape(128, 32 * 1024)),
        "bwaw": _fm(np.concatenate([g["L0_attn_bw"], g["L0_attn_aw"]], axis=1)),
        "emb": np.ascontiguousarray(g["L0_ffn_expert_emb"]),
        "b2": np.ascontiguousarray(g["L0_ffn_b2"]),
        "b1c": np.ascontiguousarray(g["L0_ffn_b1"].reshape(16, 4, 128)
                                    .transpose(2, 0, 1).reshape(128, 64)),
        "bmc": np.ascontiguousarray(g["L0_ffn_bm"].reshape(8, 128).T),
        "boc": np.ascontiguousarray(g["L0_ffn_bo"].reshape(8, 128).T),
        "ln1c": np.ascontiguousarray(g["L0_attn_ln1_w"].reshape(8, 128).T),
        "ln2c": np.ascontiguousarray(g["L0_attn_ln2_w"].reshape(8, 128).T),
        "alog": g["L0_attn_A_log"].reshape(1, 8),
        "dtb": g["L0_attn_dt_bias"].reshape(1, 8),
        "onormr": g["L0_attn_onorm_w"].reshape(1, 128),
        "identc": np.eye(128, dtype=f32),
        "suc": np.tril(np.ones((128, 128), f32), -1),
        "uic": np.triu(np.ones((128, 128), f32), 0),
        "msc": np.triu(np.ones((128, 128), f32), 1),
        "mstc": np.tril(np.ones((128, 128), f32), -1),
        "onesc": np.ones((128, 1), f32),
        "onesr": np.ones((128, 128), f32),
        "fconst": np.tile(np.array(FCONST_VALS, f32), (128, 1)),
    }
    shared = {k: np.ascontiguousarray(v, dtype=f32) for k, v in shared.items()}
    x = g["L0_attn_x"]
    in_maps = []
    for i in range(x.shape[0]):
        m = dict(shared)
        m["x"] = _fm(np.ascontiguousarray(x[i].T))
        in_maps.append(m)
    return in_maps


def kernel(**inputs):
    x = np.asarray(inputs["L0_attn_x"])
    B, T, _ = x.shape
    nc = build_program(T=T)
    in_maps = _prep_inputs(inputs, T)
    res = run_bass_kernel_spmd(nc, in_maps, core_ids=list(range(B)))
    outs = []
    for i in range(B):
        yy = res.results[i]["y"]
        full = yy.reshape(128, 8, T).transpose(1, 0, 2).reshape(1024, T)
        outs.append(full.T)
    return np.ascontiguousarray(np.stack(outs, axis=0), dtype=np.float32)


# version 07
# speedup vs baseline: 1.0653x; 1.0038x over previous
import numpy as np
import ml_dtypes
import concourse.bass as bass
import concourse.mybir as mybir
from concourse.bass_utils import run_bass_kernel_spmd

F32 = mybir.dt.float32
F32R = mybir.dt.float32r
BF16 = mybir.dt.bfloat16
AF = mybir.ActivationFunctionType
ALU = mybir.AluOpType
AX = mybir.AxisListType

ENGINES = ("tensor", "vector", "scalar", "gpsimd", "sync")
OPT_NO_H2D = False
OPT_SER = ""
OPT_NONCE = 0
OPT_P1_STOP = 99
OPT_NG = 99
FCONST_VALS = [1e-5, 1e-6, -0.5, -1.0, 0.5, 1.0 / 128.0, 1.0 / 1024.0,
               -1.0e30, 2.0, 3.0, 0.25, -2.0, 8.0, 16.0, 1.5, -1.5]
D = 1024
NDJ = 8
NH = 8
NE = 16
CHK = 128


class Prog:
    def __init__(self, nc):
        self.nc = nc
        self.streams = {e: [] for e in ENGINES}
        names = ["pe", "ve", "se", "ge", "ldy", "ldg", "st", "sga", "sgb", "ldh"]
        self.sem = {n: nc.alloc_semaphore("sem_" + n) for n in names}
        self.cnt = {n: 0 for n in names}
        self.engsem = {"tensor": "pe", "vector": "ve", "scalar": "se", "gpsimd": "ge"}

    def do(self, eng, fn, waits=(), tick=False):
        name = self.engsem.get(eng)
        if getattr(self, "serialize", False) and name is not None:
            waits = list(waits) + [(k, self.cnt[k]) for k in
                                   ("pe", "ve", "se", "ge") if k != name]
            tick = True
        tn = None
        if tick:
            self.cnt[name] += 1
            tn = name
        self.streams[eng].append((fn, [w for w in waits if w and w[1] > 0],
                                  self.sem[tn] if tn else None, 1))
        return self.cnt[name] if tick else None

    def dma(self, eng, out, in_, waits=(), sem=None):
        if sem is None:
            sem = "ldg" if eng == "gpsimd" else "ldy"
        self.cnt[sem] += 16
        self.streams[eng].append(
            ((lambda e, o=out, i=in_: e.dma_start(out=o, in_=i)),
             [w for w in waits if w and w[1] > 0], self.sem[sem], 16))
        return self.cnt[sem]

    def wait(self, eng, waits):
        self.streams[eng].append((None, [w for w in waits if w and w[1] > 0], None, 0))

    def barrier(self):
        snap = dict(self.cnt)
        for eng in ENGINES:
            own = self.engsem.get(eng)
            ws = [(k, v) for k, v in snap.items() if v > 0 and k != own]
            self.wait(eng, ws)

    def emit(self, block):
        for eng in ENGINES:
            stream = self.streams[eng]
            if not stream:
                continue

            def make(stream):
                def body(e):
                    for fn, waits, sem, n in stream:
                        for (wn, wv) in waits:
                            e.wait_ge(self.sem[wn], wv)
                        if fn is not None:
                            ins = fn(e)
                            if sem is not None:
                                ins.then_inc(sem, n)
                return body
            getattr(block, eng)(make(stream))


def build_program(T=2048, phases=(1, 1, 1, 1), dbg=False):
    CNC = T // CHK
    NHALF = 2 if T >= 1024 else 1
    TH = T // NHALF
    TCW = min(512, TH)
    NTCH = TH // TCW
    NTT = TH // 128
    GRP = 2
    NW1 = max(1, TH // 512)
    W1W = min(512, TH)
    P2W = 512 if T >= 1024 else (256 if T >= 512 else 128)
    NP2 = T // P2W

    nc = bass.Bass("TRN2", target_bir_lowering=False, debug=False,
                   dynamic_dma_scratch_size=2048,
                   detect_race_conditions=False)
    P = Prog(nc)

    dr = {}
    def dram_in(name, shape, dt=F32):
        dr[name] = nc.dram_tensor(name, shape, dt, kind="ExternalInput").ap()
        return dr[name]

    dram_in("x", [128, NDJ * T])
    for w in ["wm", "w1t", "w2r", "wo", "qw", "kw", "vw", "gw", "ow"]:
        dram_in(w, [128, 8192])
    dram_in("fcw", [128, NDJ * 4096])
    dram_in("pw", [128, 32 * 1024])
    dram_in("bwaw", [128, 128])
    dram_in("emb", [128, NE])
    dram_in("b2", [16, 128])
    dram_in("b1c", [128, 64])
    dram_in("bmc", [128, 8])
    dram_in("boc", [128, 8])
    dram_in("ln1c", [128, 8])
    dram_in("ln2c", [128, 8])
    dram_in("alog", [1, 8])
    dram_in("dtb", [1, 8])
    dram_in("onormr", [1, 128])
    dram_in("identc", [128, 128])
    dram_in("suc", [128, 128])
    dram_in("uic", [128, 128])
    dram_in("msc", [128, 128])
    dram_in("mstc", [128, 128])
    dram_in("onesc", [128, 1])
    dram_in("onesr", [128, 128])
    dram_in("fconst", [128, 16])

    y = nc.dram_tensor("y", [128, NDJ * T], F32, kind="ExternalOutput").ap()
    h2d = nc.dram_tensor("h2d", [128, NDJ * T], BF16, kind="Internal").ap()

    RES = nc.alloc_sbuf_tensor("RES", [128, NDJ * T], F32)
    WBF = nc.alloc_sbuf_tensor("WBF", [128, 32768], BF16)
    AB = nc.alloc_sbuf_tensor("AB", [128, 24576], BF16)
    AFa = nc.alloc_sbuf_tensor("AFa", [128, 3904], F32)
    STG = nc.alloc_sbuf_tensor("STG", [128, 2048], F32)
    IDB = nc.alloc_sbuf_tensor("IDB", [128, 128], BF16)
    SU = nc.alloc_sbuf_tensor("SUc", [128, 128], F32)
    UI = nc.alloc_sbuf_tensor("UIc", [128, 128], F32)
    MSK = nc.alloc_sbuf_tensor("MSc", [128, 128], F32)
    MST = nc.alloc_sbuf_tensor("MSTc", [128, 128], F32)
    ONESC = nc.alloc_sbuf_tensor("ONESCc", [128, 1], F32)
    ONESR_R = nc.alloc_sbuf_tensor("ONESR_R", [128, 128], F32R)
    CSM = nc.alloc_sbuf_tensor("CSM", [128, 128], F32)
    SMALL = nc.alloc_sbuf_tensor("SMALL", [128, 16], F32)
    ONB = nc.alloc_sbuf_tensor("ONB", [128, 128], F32)
    B2B = nc.alloc_sbuf_tensor("B2B", [16, 128], BF16)
    EMB = nc.alloc_sbuf_tensor("EMBB", [128, NE], BF16)
    ROW8 = nc.alloc_sbuf_tensor("ROW8", [1, 64], F32)
    ROW8R = nc.alloc_sbuf_tensor("ROW8R", [1, 64], F32R)
    RWS = nc.alloc_sbuf_tensor("RWS", [128, 512], F32)
    RWR = nc.alloc_sbuf_tensor("RWR", [128, 512], F32R)
    TOTR = nc.alloc_sbuf_tensor("TOTR", [1, 128], F32)
    TOTRR = nc.alloc_sbuf_tensor("TOTRR", [1, 128], F32R)
    FCON = nc.alloc_sbuf_tensor("FCON", [128, 16], F32)
    SEPT = nc.alloc_sbuf_tensor("SEPT", [128, 2], F32)
    S0 = nc.alloc_sbuf_tensor("S0", [128, NH * 128], F32)
    SC8 = nc.alloc_sbuf_tensor("SC8", [128, GRP * 80 + 16], F32)
    WTT = nc.alloc_sbuf_tensor("WTT", [128, NTT * 128], F32)
    WTB = nc.alloc_sbuf_tensor("WTB", [128, NTT * 128], BF16)
    WFMH = nc.alloc_sbuf_tensor("WFMH", [16, NTT * 128], BF16)

    A0 = dict(ZBF=0, HBF=NDJ * TCW, HMID=NDJ * TCW + NDJ * TH,
              CMBF=NDJ * TCW + NDJ * TH + 8 * TH)
    def cmbf_col(h):
        return h * TH if h < 4 else A0["CMBF"] + (h - 4) * TH
    assert A0["CMBF"] + 4 * TH <= 24576
    assert 4 * TH <= NDJ * TCW
    gq = GRP * 1024
    A1 = dict(OWB=0, BWA=8192, HB1=8320, QN=8320 + gq, KN=8320 + 2 * gq,
              VN=8320 + 3 * gq, GN=8320 + 4 * gq, SCN=8320 + 5 * gq)
    assert A1["SCN"] + 33 * 128 <= 24576
    A2 = dict(FFA=0, H2L=32 * P2W, FCR=32 * P2W + NDJ * P2W)
    assert A2["FCR"] + 4 * 1024 <= 24576
    F0 = dict(SCR=64, CMBT=64 + 2 * TCW)
    assert F0["CMBT"] + NTT * 128 <= 3904
    F1 = dict(QS=64, XET=64 + 2048, OTM=64 + 2048 + 128, X1P=64 + 2048 + 256,
              SCR2=64 + 2048 + 256 + 1024)
    assert F1["SCR2"] + 512 <= 3904

    PS01 = nc.alloc_psum_tensor("PS01", [128, 1024], F32)
    PS23 = nc.alloc_psum_tensor("PS23", [128, 1024], F32)
    PS4 = nc.alloc_psum_tensor("PS4", [128, 512], F32)
    PS5 = nc.alloc_psum_tensor("PS5", [128, 512], F32)
    PS6 = nc.alloc_psum_tensor("PS6", [128, 512], F32)
    PS7B = nc.alloc_psum_tensor("PS7B", [128, 1024], BF16)

    def MM(out, lhsT, rhs, start=True, stop=True, waits=(), tick=False):
        return P.do("tensor",
                    lambda t, o=out, l=lhsT, r=rhs, s1=start, s2=stop:
                    t.matmul(o, l, r, start=s1, stop=s2), waits, tick)

    def TR(out, in_, ident=None, waits=(), tick=False):
        idn = IDB if ident is None else ident
        return P.do("tensor", lambda t, o=out, i=in_, d=idn: t.transpose(o, i, d[:, :]),
                    waits, tick)

    def ACT(out, in_, func, bias=0.0, scale=1.0, accum=None, waits=(), tick=False):
        return P.do("scalar",
                    lambda s, o=out, i=in_, f=func, b=bias, sc=scale, a=accum:
                    s.activation(o, i, f, bias=b, scale=sc, accum_out=a), waits, tick)

    def VTT(out, a, b, op, waits=(), tick=False):
        return P.do("vector", lambda v, o=out, x=a, yy=b, p=op:
                    v.tensor_tensor(o, x, yy, p), waits, tick)

    def VTS(out, in0, s1, s2, op0, op1=ALU.bypass, waits=(), tick=False):
        return P.do("vector",
                    lambda v, o=out, i=in0, a=s1, b=s2, p0=op0, p1=op1:
                    v.tensor_scalar(o, i, a, b, p0, p1), waits, tick)

    def VSTT(out, in0, s, in1, op0, op1, accum=None, waits=(), tick=False):
        return P.do("vector",
                    lambda v, o=out, i0=in0, sc=s, i1=in1, p0=op0, p1=op1, a=accum:
                    v.scalar_tensor_tensor(o, i0, sc, i1, p0, p1, accum_out=a),
                    waits, tick)

    def VCP(out, in_, waits=(), tick=False):
        return P.do("vector", lambda v, o=out, i=in_: v.tensor_copy(o, i), waits, tick)

    def VRD(out, in_, op, waits=(), tick=False):
        return P.do("vector", lambda v, o=out, i=in_, p=op:
                    v.tensor_reduce(o, i, AX.X, p), waits, tick)

    def VSEP(tick=False):
        return P.do("vector", lambda v: v.tensor_copy(SEPT[:, 0:1], ONESC[:, :]),
                    (), tick)

    for _ in range(OPT_NONCE):
        P.do("vector", lambda v: v.tensor_copy(SEPT[:, 1:2], ONESC[:, :]))
    if dbg:
        P.do("vector", lambda v: v.memset(AB[:, :], 0.0))
        P.do("vector", lambda v: v.memset(WTT[:, :], 0.0))
        P.do("vector", lambda v: v.memset(S0[:, :], 0.0))
        P.do("vector", lambda v: v.memset(SC8[:, :], 0.0))
        P.do("vector", lambda v: v.memset(STG[:, :], 0.0))
        P.do("vector", lambda v: v.memset(WBF[:, :], 0.0))
        P.do("vector", lambda v: v.memset(AFa[:, :], 0.0), tick=True)
        P.wait("sync", [("ve", P.cnt["ve"])])
        P.wait("tensor", [("ve", P.cnt["ve"])])
        P.wait("scalar", [("ve", P.cnt["ve"])])
    for dj in range(NDJ):
        P.dma("sync", RES[:, dj * T:(dj + 1) * T], dr["x"][:, dj * T:(dj + 1) * T])
    P.dma("sync", SU[:, :], dr["suc"])
    P.dma("sync", UI[:, :], dr["uic"])
    P.dma("sync", MSK[:, :], dr["msc"])
    P.dma("sync", MST[:, :], dr["mstc"])
    P.dma("sync", ONESC[:, :], dr["onesc"])
    P.dma("sync", CSM[:, 0:8], dr["bmc"])
    P.dma("sync", CSM[:, 8:16], dr["boc"])
    P.dma("sync", CSM[:, 16:24], dr["ln1c"])
    P.dma("sync", CSM[:, 24:32], dr["ln2c"])
    P.dma("sync", CSM[:, 32:96], dr["b1c"])
    P.dma("sync", ROW8[:, 0:8], dr["alog"])
    P.dma("sync", ROW8[:, 8:16], dr["dtb"])
    P.dma("sync", STG[:, 640:768], dr["identc"])
    P.dma("sync", STG[:, 128:144], dr["emb"])
    P.dma("sync", STG[0:16, 256:384], dr["b2"])
    P.dma("sync", STG[0:1, 384:512], dr["onormr"])
    P.dma("sync", FCON[:, :], dr["fconst"])
    ld0 = P.dma("sync", STG[:, 512:640], dr["onesr"])
    for j, val in enumerate(FCONST_VALS):
        nc.const_aps.register(F32, val, FCON[:, j:j + 1])
    VCP(IDB[:, :], STG[:, 640:768], waits=[("ldy", ld0)])
    VCP(EMB[:, :], STG[:, 128:144])
    VCP(B2B[:, :], STG[0:16, 256:384])
    VCP(ONESR_R[:, :], STG[:, 512:640])
    ACT(SEPT[:, 0:1], ONESC[:, :], AF.Silu, waits=[("ldy", ld0)])
    ACT(SEPT[:, 0:1], ONESC[:, :], AF.Gelu)
    ACT(SEPT[:, 0:1], ONESC[:, :], AF.Ln)
    ACT(ROW8[:, 16:24], ROW8[:, 0:8], AF.Exp, tick=True)
    se0 = P.cnt["se"]
    VTS(ROW8[:, 24:32], ROW8[:, 16:24], -1.0, None, ALU.mult, waits=[("se", se0)])
    VSEP()
    VCP(ROW8R[:, 0:8], ROW8[:, 24:32])
    VCP(ROW8R[:, 8:16], ROW8[:, 8:16])
    v0 = VCP(TOTRR[:, 0:128], STG[0:1, 384:512], tick=True)
    MM(PS6[:, 0:8], ONESR_R[0:1, :], ROW8R[:, 0:8], waits=[("ve", v0)])
    MM(PS6[:, 8:16], ONESR_R[0:1, :], ROW8R[:, 8:16])
    m0 = MM(PS6[:, 128:256], ONESR_R[0:1, :], TOTRR[:, 0:128], tick=True)
    VCP(SMALL[:, 0:16], PS6[:, 0:16], waits=[("pe", m0)])
    v1 = VCP(ONB[:, :], PS6[:, 128:256], tick=True)
    P.wait("tensor", [("ve", v1)])

    stg = {"par": 0, "cast": [("ve", 0), ("ve", 0)]}

    def stage(dst, dst_off, src_ap, ncols, extra_wait=None, eng="vector"):
        last = 0
        done = 0
        while done < ncols:
            w = min(1024, ncols - done)
            par = stg["par"]
            stg["par"] ^= 1
            sem = "sga" if par == 0 else "sgb"
            ws = [stg["cast"][par]]
            if extra_wait:
                ws.append(extra_wait)
                extra_wait = None
            ldv = P.dma("sync", STG[:, par * 1024: par * 1024 + w],
                        src_ap[:, done:done + w], waits=ws, sem=sem)
            do_ = dst[:, dst_off + done: dst_off + done + w]
            so_ = STG[:, par * 1024: par * 1024 + w]
            if eng == "gpsimd":
                cv = P.do("gpsimd", lambda g, d=do_, s=so_: g.tensor_copy(d, s),
                          waits=[(sem, ldv)], tick=True)
                stg["cast"][par] = ("ge", cv)
            else:
                cv = VCP(do_, so_, waits=[(sem, ldv)], tick=True)
                stg["cast"][par] = ("ve", cv)
            last = cv
            done += w
        return last

    def ln_stats(w0, W, scr_off, sq_ps=None):
        if sq_ps is None:
            sq_ps = PS5
        P.wait("tensor", [("ve", P.cnt["ve"]), ("se", P.cnt["se"])])
        P.wait("scalar", [("pe", P.cnt["pe"]), ("ve", P.cnt["ve"])])
        sqv = [0, 0]
        smv = [0] * NDJ
        for dj in range(NDJ):
            xs = RES[:, dj * T + w0: dj * T + w0 + W]
            sqv[dj % 2] = ACT(AFa[:, scr_off + (dj % 2) * W: scr_off + (dj % 2) * W + W],
                              xs, AF.Square,
                              waits=[("pe", smv[dj - 2])] if dj >= 2 else (),
                              tick=True)
            MM(PS6[0:1, 0:W], ONESC[:, :], xs,
               start=(dj == 0), stop=(dj == NDJ - 1))
            smv[dj] = MM(sq_ps[0:1, 0:W], ONESC[:, :],
                         AFa[:, scr_off + (dj % 2) * W: scr_off + (dj % 2) * W + W],
                         start=(dj == 0), stop=(dj == NDJ - 1),
                         waits=[("se", sqv[dj % 2])], tick=True)
        return smv[NDJ - 1]

    def emit_ffn():
        P.barrier()
        wmv = stage(WBF, 0, dr["wm"], 8192)
        P.wait("tensor", [("ve", wmv)])
        stage(WBF, 8192, dr["w1t"], 8192, eng="gpsimd")
        w2gv = stage(WBF, 16384, dr["w2r"], 8192, eng="gpsimd")
        wogv = stage(WBF, 24576, dr["wo"], 8192, eng="gpsimd")
        WMB = lambda dj, do: WBF[:, dj * 1024 + do * 128: dj * 1024 + (do + 1) * 128]
        W1B = lambda e, it: WBF[:, 8192 + e * 512 + it * 128: 8192 + e * 512 + (it + 1) * 128]
        W2B = lambda e, it: WBF[:, 16384 + (e * 4 + it) * 128: 16384 + (e * 4 + it) * 128 + 128]
        WOB = lambda h, do: WBF[:, 24576 + h * 1024 + do * 128: 24576 + h * 1024 + (do + 1) * 128]
        ZBF = lambda dj: AB[:, A0["ZBF"] + dj * TCW: A0["ZBF"] + (dj + 1) * TCW]
        HBFs = lambda h, a, w: AB[:, A0["HBF"] + h * TH + a: A0["HBF"] + h * TH + a + w]
        HMIDs = lambda par, it, a, w: AB[:, A0["HMID"] + par * 4 * TH + it * TH + a:
                                         A0["HMID"] + par * 4 * TH + it * TH + a + w]
        CMBFs = lambda h, a, w: AB[:, cmbf_col(h) + a: cmbf_col(h) + a + w]
        SCRf = lambda k, w=TCW: AFa[:, F0["SCR"] + k * TCW: F0["SCR"] + k * TCW + w]
        CMBT = lambda tt: AFa[:, F0["CMBT"] + tt * 128: F0["CMBT"] + (tt + 1) * 128]

        for half in range(NHALF):
            t0 = half * TH
            hv = 0
            for tc in range(NTCH):
                w0 = t0 + tc * TCW
                pemm = ln_stats(w0, TCW, F0["SCR"])
                VTS(RWS[0:1, 0:TCW], PS6[0:1, 0:TCW], 1.0 / D, None, ALU.mult,
                    waits=[("pe", pemm)])
                VTT(RWS[32:33, 0:TCW], RWS[0:1, 0:TCW], RWS[0:1, 0:TCW], ALU.mult)
                vs = VSTT(RWS[32:33, 0:TCW], PS5[0:1, 0:TCW], 1.0 / D,
                          RWS[32:33, 0:TCW], ALU.mult, ALU.subtract, tick=True)
                ACT(RWS[64:65, 0:TCW], RWS[32:33, 0:TCW], AF.Ln, bias=1e-5,
                    waits=[("ve", vs)])
                ACT(RWS[64:65, 0:TCW], RWS[64:65, 0:TCW], AF.Exp, scale=-0.5, tick=True)
                sr = P.cnt["se"]
                VCP(RWR[0:1, 0:TCW], RWS[0:1, 0:TCW], waits=[("se", sr)])
                vr = VCP(RWR[32:33, 0:TCW], RWS[64:65, 0:TCW], tick=True)
                MM(PS4[:, 0:TCW], ONESR_R[0:1, :], RWR[0:1, 0:TCW], waits=[("ve", vr)])
                bc = MM(PS5[:, 0:TCW], ONESR_R[32:33, :], RWR[32:33, 0:TCW], tick=True)
                for dj in range(NDJ):
                    xs = RES[:, dj * T + w0: dj * T + w0 + TCW]
                    VTT(SCRf(0), xs, PS4[:, 0:TCW], ALU.subtract,
                        waits=[("pe", bc)] if dj == 0 else ())
                    VTT(ZBF(dj), SCRf(0), PS5[:, 0:TCW], ALU.mult,
                        tick=(dj == NDJ - 1))
                zv = P.cnt["ve"]
                evs = [hv, hv]
                for do in range(NDJ):
                    pd = PS01 if do % 2 == 0 else PS23
                    for dj in range(NDJ):
                        ws = []
                        if dj == 0 and do == 0:
                            ws.append(("ve", zv))
                        if dj == 0:
                            ws.append(("se", evs[do % 2]))
                        MM(pd[:, 0:TCW], WMB(dj, do), ZBF(dj),
                           start=(dj == 0), stop=(dj == NDJ - 1),
                           waits=ws, tick=(dj == NDJ - 1))
                    mmt = P.cnt["pe"]
                    evs[do % 2] = ACT(HBFs(do, tc * TCW, TCW), pd[:, 0:TCW],
                                      AF.Identity, bias=CSM[:, do:do + 1],
                                      waits=[("pe", mmt)], tick=True)
                hv = max(evs)
            P.wait("tensor", [("se", hv)])

            wfmv = 0
            rt_prev = 0
            for tt in range(NTT):
                for h in range(NH):
                    MM(PS6[:, h * 16:(h + 1) * 16], HBFs(h, tt * 128, 128), EMB[:, :],
                       waits=[("se", rt_prev)] if h == 0 else (),
                       tick=(h == NH - 1))
                lv = P.cnt["pe"]
                VRD(AFa[:, 0:8], PS6[:, 0:128].rearrange("p (a b) -> p a b", b=16),
                    ALU.max, waits=[("pe", lv)])
                VSEP()
                nv = VTS(AFa[:, 8:16], AFa[:, 0:8], -1.0, None, ALU.mult, tick=True)
                for h in range(NH):
                    rt_prev = ACT(SCRf(0, 128)[:, h * 16:(h + 1) * 16],
                                  PS6[:, h * 16:(h + 1) * 16], AF.Exp,
                                  bias=AFa[:, 8 + h:9 + h],
                                  waits=[("ve", nv)] if h == 0 else (),
                                  tick=(h == NH - 1))
                VRD(AFa[:, 16:24], SCRf(0, 128).rearrange("p (a b) -> p a b", b=16),
                    ALU.add, waits=[("se", rt_prev)])
                VSEP()
                P.do("vector", lambda v: v.reciprocal(AFa[:, 24:32], AFa[:, 16:24]))
                VSEP()
                for h in range(NH):
                    VTS(WTT[:, tt * 128 + h * 16: tt * 128 + (h + 1) * 16],
                        SCRf(0, 128)[:, h * 16:(h + 1) * 16],
                        AFa[:, 24 + h:25 + h], None, ALU.mult)
                VSEP()
                VRD(AFa[:, 32:40],
                    WTT[:, tt * 128:(tt + 1) * 128].rearrange("p (a b) -> p a b", b=16),
                    ALU.max)
                VSEP()
                P.do("vector", lambda v, tt=tt: v.match_replace(
                    SCRf(1, 128), AFa[:, 32:40], WTT[:, tt * 128:(tt + 1) * 128],
                    -1.0e30))
                VSEP()
                VRD(AFa[:, 40:48], SCRf(1, 128).rearrange("p (a b) -> p a b", b=16),
                    ALU.max)
                VSEP()
                for h in range(NH):
                    VTS(SCRf(1, 128)[:, h * 16:(h + 1) * 16],
                        WTT[:, tt * 128 + h * 16: tt * 128 + (h + 1) * 16],
                        AFa[:, 40 + h:41 + h], None, ALU.is_ge)
                VSEP()
                VTT(WTT[:, tt * 128:(tt + 1) * 128], WTT[:, tt * 128:(tt + 1) * 128],
                    SCRf(1, 128), ALU.mult)
                wfmv = VCP(WTB[:, tt * 128:(tt + 1) * 128],
                           WTT[:, tt * 128:(tt + 1) * 128], tick=True)
            P.wait("tensor", [("ve", wfmv), ("ge", w2gv)])

            geluv = [0, 0]
            sttv = {s: 0 for s in range(8)}
            b2ev = 0
            for h in range(NH):
                for tt in range(NTT):
                    TR(PS7B[0:16, tt * 128:(tt + 1) * 128],
                       WTB[:, tt * 128 + h * 16: tt * 128 + (h + 1) * 16],
                       waits=[("ve", b2ev)] if tt == 0 else (),
                       tick=(tt == NTT - 1))
                trw = P.cnt["pe"]
                wfv = VCP(WFMH[0:16, 0:NTT * 128], PS7B[0:16, 0:NTT * 128],
                          waits=[("pe", trw)], tick=True)
                for tt in range(NTT):
                    pb = PS01 if tt < 4 else PS23
                    MM(pb[:, (tt % 4) * 128:(tt % 4 + 1) * 128],
                       WFMH[0:16, tt * 128:(tt + 1) * 128],
                       B2B[0:16, :],
                       waits=([("ve", wfv), ("se", max(geluv))]
                              if tt == 0 else ()),
                       tick=(tt == NTT - 1))
                b2t = P.cnt["pe"]
                if NTT > 4:
                    b2ev = VCP(AFa[:, F0["CMBT"]:F0["CMBT"] + 4 * 128],
                               PS01[:, 0:4 * 128],
                               waits=[("pe", b2t), ("ve", max(sttv.values()))],
                               tick=True)
                    b2ev = VCP(AFa[:, F0["CMBT"] + 4 * 128:F0["CMBT"] + NTT * 128],
                               PS23[:, 0:(NTT - 4) * 128], tick=True)
                else:
                    b2ev = VCP(AFa[:, F0["CMBT"]:F0["CMBT"] + NTT * 128],
                               PS01[:, 0:NTT * 128],
                               waits=[("pe", b2t), ("ve", max(sttv.values()))],
                               tick=True)
                for e in range(NE):
                    par_he = (h * NE + e) % 2
                    for it in range(4):
                        pd = PS01 if it % 2 == 0 else PS23
                        for n2 in range(NW1):
                            ws = []
                            if n2 == 0:
                                ws = [("se", geluv[it % 2])]
                                if e == 0:
                                    ws.append(("ve", b2ev))
                            MM(pd[:, n2 * W1W:(n2 + 1) * W1W], W1B(e, it),
                               HBFs(h, n2 * W1W, W1W), start=True, stop=True,
                               waits=ws, tick=(n2 == NW1 - 1))
                        mmt = P.cnt["pe"]
                        geluv[it % 2] = ACT(
                            HMIDs(par_he, it, 0, TH), pd[:, 0:TH], AF.Gelu,
                            bias=CSM[:, 32 + e * 4 + it:33 + e * 4 + it],
                            waits=[("pe", mmt)], tick=True)
                    gall = max(geluv)
                    for tt in range(NTT):
                        bank = tt % 2
                        pd2 = PS4 if bank == 0 else PS5
                        so = ((tt // 2) % 4) * 128
                        s = tt % 8
                        for it in range(4):
                            ws = []
                            if it == 0:
                                ws = [("se", gall),
                                      ("ve", max(sttv[s], sttv.get(tt % 2 + ((tt - 2) % 8 // 2) * 2, 0) if tt >= 2 else 0, b2ev))]
                                ws = [("se", gall),
                                      ("ve", max(sttv[s],
                                                 sttv[(tt - 2) % 8] if tt >= 2 else 0,
                                                 b2ev))]
                            MM(pd2[:, so:so + 128],
                               HMIDs(par_he, it, tt * 128, 128), W2B(e, it),
                               start=(it == 0), stop=(it == 3),
                               waits=ws, tick=(it == 3))
                        w2t = P.cnt["pe"]
                        sttv[s] = VSTT(
                            CMBT(tt), pd2[:, so:so + 128],
                            WTT[:, tt * 128 + h * 16 + e: tt * 128 + h * 16 + e + 1],
                            CMBT(tt), ALU.mult, ALU.add,
                            waits=[("pe", w2t)], tick=True)
                csc = A0["HMID"] + 4 * TH
                cbv = 0
                for tt in range(NTT):
                    cbv = VCP(AB[:, csc + tt * 128:csc + (tt + 1) * 128], CMBT(tt),
                              tick=(tt == NTT - 1))
                for tt in range(NTT):
                    TR(PS7B[:, tt * 128:(tt + 1) * 128],
                       AB[:, csc + tt * 128:csc + (tt + 1) * 128],
                       waits=[("ve", cbv)] if tt == 0 else (),
                       tick=(tt == NTT - 1))
                trt = P.cnt["pe"]
                b2ev = VCP(CMBFs(h, 0, NTT * 128), PS7B[:, 0:NTT * 128],
                           waits=[("pe", trt)], tick=True)
            P.wait("tensor", [("ve", P.cnt["ve"]), ("se", P.cnt["se"]),
                              ("ge", wogv)])
            rv = [0, 0]
            for tc in range(NTCH):
                for do in range(NDJ):
                    pd = PS01 if do % 2 == 0 else PS23
                    for h in range(NH):
                        ws = []
                        if h == 0:
                            ws = [("ve", rv[do % 2])]
                        MM(pd[:, 0:TCW], WOB(h, do), CMBFs(h, tc * TCW, TCW),
                           start=(h == 0), stop=(h == NH - 1),
                           waits=ws, tick=(h == NH - 1))
                    wot = P.cnt["pe"]
                    VTS(SCRf(1), pd[:, 0:TCW], CSM[:, 8 + do:9 + do], 0.5,
                        ALU.add, ALU.mult, waits=[("pe", wot)])
                    rs = RES[:, do * T + t0 + tc * TCW: do * T + t0 + tc * TCW + TCW]
                    rv[do % 2] = VTT(rs, rs, SCRf(1), ALU.add, tick=True)
            P.wait("tensor", [("ve", max(rv)), ("se", P.cnt["se"])])

    def emit_attn():
        P.barrier()
        if "ALL" in OPT_SER or "AB" in OPT_SER or "ST" in OPT_SER:
            P.serialize = True
        stage(WBF, 0, dr["qw"], 8192, eng="gpsimd")
        stage(WBF, 8192, dr["kw"], 8192, eng="gpsimd")
        stage(WBF, 16384, dr["vw"], 8192, eng="gpsimd")
        stage(WBF, 24576, dr["gw"], 8192, eng="gpsimd")
        stage(AB, A1["OWB"], dr["ow"], 8192, eng="gpsimd")
        wgv = stage(AB, A1["BWA"], dr["bwaw"], 128, eng="gpsimd")
        if OPT_SER == "ST":
            P.serialize = False
        QWB = lambda dj, n: WBF[:, dj * 1024 + n: dj * 1024 + n + 512]
        KWB = lambda dj, n: WBF[:, 8192 + dj * 1024 + n: 8192 + dj * 1024 + n + 512]
        VWB = lambda dj, n: WBF[:, 16384 + dj * 1024 + n: 16384 + dj * 1024 + n + 512]
        GWB = lambda dj, n: WBF[:, 24576 + dj * 1024 + n: 24576 + dj * 1024 + n + 512]
        OWBs = lambda h, do: AB[:, A1["OWB"] + h * 1024 + do * 128:
                                A1["OWB"] + h * 1024 + (do + 1) * 128]
        BWAs = lambda dj: AB[:, A1["BWA"] + dj * 16: A1["BWA"] + (dj + 1) * 16]
        HB1 = lambda ci, dj: AB[:, A1["HB1"] + ci * 1024 + dj * 128:
                                A1["HB1"] + ci * 1024 + (dj + 1) * 128]
        def TMS(base):
            return lambda ci, lo, w: AB[:, A1[base] + ci * 1024 + lo:
                                        A1[base] + ci * 1024 + lo + w]
        QNs, KNs, VNs, GNs = TMS("QN"), TMS("KN"), TMS("VN"), TMS("GN")
        SCN = A1["SCN"]
        ABS = lambda k, w=128: AB[:, SCN + k * 128: SCN + k * 128 + w]
        QS = lambda k: AFa[:, F1["QS"] + k * 1024: F1["QS"] + (k + 1) * 1024]
        XET = AFa[:, F1["XET"]:F1["XET"] + 128]
        OTM = AFa[:, F1["OTM"]:F1["OTM"] + 128]
        X1P = lambda dj: AFa[:, F1["X1P"] + dj * 128: F1["X1P"] + (dj + 1) * 128]
        SCR2 = lambda k, w=128: AFa[:, F1["SCR2"] + k * 128: F1["SCR2"] + k * 128 + w]
        ZR = lambda ci, dj: STG[:, ci * 1024 + dj * 128: ci * 1024 + (dj + 1) * 128]
        SCc = lambda ci, k, w=8: SC8[:, ci * 80 + k: ci * 80 + k + w]

        z0 = P.do("vector", lambda v: v.memset(S0[:, :], 0.0), tick=True)
        P.wait("tensor", [("ve", z0)])

        NG = CNC // GRP
        if OPT_P1_STOP < 0:
            NG = 0
        NG = min(NG, OPT_NG)
        vseq = 0
        h2v = 0
        for g in range(NG):
            for ci in range(GRP):
                c = g * GRP + ci
                w0 = c * 128
                pemm = ln_stats(w0, 128, F1["SCR2"])
                co = ci * 128
                VTS(RWS[0:1, co:co + 128], PS6[0:1, 0:128], 1.0 / D, None, ALU.mult,
                    waits=[("pe", pemm)])
                VTT(RWS[32:33, co:co + 128], RWS[0:1, co:co + 128],
                    RWS[0:1, co:co + 128], ALU.mult)
                vv = VSTT(RWS[32:33, co:co + 128], PS5[0:1, 0:128], 1.0 / D,
                          RWS[32:33, co:co + 128], ALU.mult, ALU.subtract, tick=True)
                TMP = RWS[32:33, 256 + co:256 + co + 128]
                ACT(TMP, RWS[32:33, co:co + 128], AF.Ln,
                    bias=1e-5, waits=[("ve", vv)])
                ACT(RWS[64:65, co:co + 128], TMP, AF.Exp,
                    scale=-0.5)
                ACT(TMP, TMP, AF.Exp,
                    scale=-1.0, tick=True)
                s2 = P.cnt["se"]
                v3 = VTT(TMP, RWS[32:33, co:co + 128],
                         TMP, ALU.mult,
                         waits=[("se", s2)], tick=True)
                ACT(TMP, TMP, AF.Ln,
                    bias=1e-6, waits=[("ve", v3)])
                ACT(TMP, TMP, AF.Exp,
                    scale=-0.5, tick=True)
                s3 = P.cnt["se"]
                VCP(RWS[64:65, 256 + co:256 + co + 128], TMP, waits=[("se", s3)])
                VTT(RWS[64:65, 256 + co:256 + co + 128],
                    RWS[64:65, co:co + 128],
                    RWS[64:65, 256 + co:256 + co + 128], ALU.mult)
                VCP(RWR[0:1, co:co + 128], RWS[0:1, co:co + 128])
                VCP(RWR[32:33, co:co + 128], RWS[64:65, co:co + 128])
                v4 = VCP(RWR[64:65, co:co + 128],
                         RWS[64:65, 256 + co:256 + co + 128], tick=True)
                MM(PS4[:, 0:128], ONESR_R[0:1, :], RWR[0:1, co:co + 128],
                   waits=[("ve", v4)])
                MM(PS5[:, 0:128], ONESR_R[32:33, :], RWR[32:33, co:co + 128])
                bc = MM(PS01[:, 0:128], ONESR_R[64:65, :], RWR[64:65, co:co + 128],
                        tick=True)
                for dj in range(NDJ):
                    xs = RES[:, dj * T + w0: dj * T + w0 + 128]
                    ws0 = [("pe", bc)] if dj == 0 else []
                    if c == 0 and dj == 0:
                        ws0.append(("ge", wgv))
                    VTT(SCR2(0), xs, PS4[:, 0:128], ALU.subtract, waits=ws0)
                    VTT(ZR(ci, dj), SCR2(0), PS5[:, 0:128], ALU.mult)
                    VTT(SCR2(1), SCR2(0), PS01[:, 0:128], ALU.mult)
                    VTS(HB1(ci, dj), SCR2(1), CSM[:, 16 + dj:17 + dj], None,
                        ALU.mult, tick=(dj == NDJ - 1))
                vseq = P.cnt["ve"]
                P.wait("tensor", [("ve", vseq)])

            if OPT_P1_STOP < 1:
                continue
            silv = [0, 0]
            for ci in range(GRP):
                for ip, wsl in enumerate([QWB, KWB]):
                    pd = PS01 if ip % 2 == 0 else PS23
                    qsv = 0
                    for n2 in range(2):
                        for dj in range(NDJ):
                            ws = []
                            if dj == 0:
                                ws = [("se", silv[ip % 2]), ("ve", vseq),
                                      ("ge", wgv)]
                            MM(pd[:, n2 * 512:(n2 + 1) * 512], HB1(ci, dj),
                               wsl(dj, n2 * 512), start=(dj == 0),
                               stop=(dj == NDJ - 1), waits=ws,
                               tick=(dj == NDJ - 1))
                        mmt = P.cnt["pe"]
                        silv[ip % 2] = ACT(QS(ip)[:, n2 * 512:(n2 + 1) * 512],
                                           pd[:, n2 * 512:(n2 + 1) * 512], AF.Silu,
                                           waits=[("pe", mmt)], tick=True)
                    dst = QNs if ip == 0 else KNs
                    sso = 32 + ci * 32 + ip * 8
                    for hh in range(NH):
                        VSTT(SCR2(0), QS(ip)[:, hh * 128:(hh + 1) * 128], 1.0,
                             QS(ip)[:, hh * 128:(hh + 1) * 128], ALU.mult, ALU.mult,
                             accum=SCc(ci, 64 + ip * 8 + hh, 1),
                             waits=[("se", silv[ip % 2])] if hh == 0 else ())
                    vseq = VCP(dst(ci, 0, 1024), QS(ip)[:, 0:1024], tick=True)
                for ip, wsl in enumerate([VWB, GWB]):
                    pd = PS01 if ip % 2 == 0 else PS23
                    dst = VNs if ip == 0 else GNs
                    for n2 in range(2):
                        for dj in range(NDJ):
                            ws = []
                            if dj == 0:
                                ws = [("se", silv[ip % 2])]
                            MM(pd[:, n2 * 512:(n2 + 1) * 512], HB1(ci, dj),
                               wsl(dj, n2 * 512), start=(dj == 0),
                               stop=(dj == NDJ - 1), waits=ws,
                               tick=(dj == NDJ - 1))
                        mmt = P.cnt["pe"]
                        silv[ip % 2] = ACT(dst(ci, n2 * 512, 512),
                                           pd[:, n2 * 512:(n2 + 1) * 512], AF.Silu,
                                           waits=[("pe", mmt)], tick=True)
                for dj in range(NDJ):
                    MM(PS6[:, 0:16], HB1(ci, dj), BWAs(dj),
                       start=(dj == 0), stop=(dj == NDJ - 1),
                       waits=[("ve", vseq)] if dj == 0 else (),
                       tick=(dj == NDJ - 1))
                bat = P.cnt["pe"]
                if dbg:
                    VCP(SC8[:, GRP * 80 + 8: GRP * 80 + 16], PS6[:, 8:16],
                        waits=[("pe", bat)])
                tb = ACT(SCR2(0, 8), PS6[:, 0:8], AF.Tanh, scale=0.5,
                         waits=[("pe", bat)], tick=True)
                VTS(SCc(ci, 0), SCR2(0, 8), 0.5, 0.5, ALU.mult, ALU.add,
                    waits=[("se", tb)])
                VSEP()
                VTS(SCc(ci, 8), SCc(ci, 0), -1.0, None, ALU.mult)

                vseq = VCP(SCc(ci, 40), PS6[:, 8:16], tick=True)
                P.wait("tensor", [("ve", vseq)])

            if OPT_P1_STOP < 2:
                continue
            P.serialize = ("C" in OPT_SER) or ("ALL" in OPT_SER)
            for ci in range(GRP):
                c = g * GRP + ci
                v5 = VTT(SCR2(0, 8), SCc(ci, 40), SMALL[:, 8:16], ALU.add,
                         waits=[("se", P.cnt["se"])], tick=True)
                if dbg:
                    VCP(SC8[:, GRP * 80 + ci * 8: GRP * 80 + 8 + ci * 8],
                        SCc(ci, 40))
                    VCP(SC8[:, GRP * 80 + 16 + ci * 8 if False else GRP * 80 + 8 + ci * 8 - 8 + 8: GRP * 80 + 16 + ci * 8 - 8 + 8] if False else SC8[:, 0:0], SC8[:, 0:0]) if False else None
                ACT(SCR2(1, 8), SCR2(0, 8), AF.Abs, waits=[("ve", v5)])
                ACT(SCR2(1, 8), SCR2(1, 8), AF.Exp, scale=-1.0)
                sp1 = ACT(SCR2(1, 8), SCR2(1, 8), AF.Ln, bias=1.0, tick=True)
                VTS(SCR2(2, 8), SCR2(0, 8), 0.0, None, ALU.max, waits=[("se", sp1)])
                VSEP()
                VTT(SCR2(2, 8), SCR2(2, 8), SCR2(1, 8), ALU.add)
                VSEP()
                v6 = VTT(SCc(ci, 16), SCR2(2, 8), SMALL[:, 0:8], ALU.mult, tick=True)
                MM(PS6[:, 64:72], UI[:, :], SCc(ci, 16),
                   waits=[("ve", v6), ("se", P.cnt["se"])])
                ct = MM(PS6[0:1, 96:104], ONESC[:, :], SCc(ci, 16), tick=True)
                VCP(SCc(ci, 24), PS6[:, 64:72], waits=[("pe", ct)])
                vtt_ = VCP(TOTR[0:1, c * 8:(c + 1) * 8], PS6[0:1, 96:104], tick=True)
                a1 = ACT(SCc(ci, 32), PS6[:, 64:72], AF.Exp,
                         waits=[("pe", ct)], tick=True)
                VSEP()
                v7 = VCP(TOTRR[0:1, c * 8:(c + 1) * 8], TOTR[0:1, c * 8:(c + 1) * 8],
                         waits=[("ve", vtt_)], tick=True)
                tb8 = MM(PS6[:, 104:112], ONESR_R[0:1, :],
                         TOTRR[0:1, c * 8:(c + 1) * 8],
                         waits=[("ve", v7), ("se", a1)], tick=True)
                v8 = VTT(SCR2(0, 8), PS6[:, 104:112], SCc(ci, 24), ALU.subtract,
                         waits=[("pe", tb8)], tick=True)
                ACT(SCc(ci, 48), SCR2(0, 8), AF.Exp, waits=[("ve", v8)])
                ac1 = ACT(SCc(ci, 56), PS6[:, 104:112], AF.Exp, tick=True)
                VTT(SCc(ci, 40), SCc(ci, 8), SCc(ci, 32), ALU.mult,
                    waits=[("se", ac1)])
                lq = ACT(SCR2(0, 16), SCc(ci, 64, 16), AF.Ln, bias=1e-6)
                eq = ACT(SCR2(0, 16), SCR2(0, 16), AF.Exp, scale=-0.5, tick=True)
                for hh in range(NH):
                    VTS(QNs(ci, hh * 128, 128), QNs(ci, hh * 128, 128),
                        SCR2(0, 16)[:, hh:hh + 1], None, ALU.mult,
                        waits=[("se", eq)] if hh == 0 else ())
                    VTS(KNs(ci, hh * 128, 128), KNs(ci, hh * 128, 128),
                        SCR2(0, 16)[:, 8 + hh:9 + hh], None, ALU.mult,
                        tick=(hh == NH - 1))
                vseq = P.cnt["ve"]

                if OPT_P1_STOP < 3:
                    continue
                ofmv = [0] * NH
                for h in range(NH):
                    v9 = VTS(ABS(24), KNs(ci, h * 128, 128), SCc(ci, 8)[:, h:h + 1],
                             None, ALU.mult, waits=[("ve", vseq)] if h == 0 else (),
                             tick=True)
                    TR(PS7B[:, 0:128], QNs(ci, h * 128, 128), waits=[("ve", v9)])
                    TR(PS7B[:, 128:256], KNs(ci, h * 128, 128))
                    t1 = TR(PS7B[:, 256:384], ABS(24), tick=True)
                    v10 = VCP(ABS(0, 384), PS7B[:, 0:384], waits=[("pe", t1)],
                              tick=True)
                    VTS(SCR2(0), UI[:, :], SCc(ci, 16)[:, h:h + 1], None, ALU.mult,
                        waits=[("se", P.cnt["se"])] if h == 0 else ())
                    v11 = VTS(SCR2(1), MST[:, :], SCc(ci, 16)[:, h:h + 1], None,
                              ALU.mult, tick=True)
                    MM(PS01[:, 0:128], SU[:, :], SCR2(0),
                       waits=[("ve", v11), ("se", P.cnt["se"])])
                    x2 = MM(PS01[:, 512:640], UI[:, :], SCR2(1), tick=True)
                    MM(PS4[:, 128:256], ABS(1), ABS(2),
                       waits=[("ve", v10)])
                    MM(PS4[:, 256:384], ABS(1), ABS(0))
                    kk2 = MM(PS5[:, 128:256], ABS(2), ABS(1), tick=True)
                    xe1 = ACT(XET, PS01[:, 0:128], AF.Exp, waits=[("pe", x2)],
                              tick=True)
                    xe2 = ACT(SCR2(2), PS01[:, 512:640], AF.Exp, tick=True)
                    VTT(SCR2(3), PS4[:, 128:256], XET, ALU.mult,
                        waits=[("se", xe1), ("pe", kk2)])
                    VTT(ABS(3), SCR2(3), MSK[:, :], ALU.mult)
                    VTT(SCR2(3), PS4[:, 256:384], XET, ALU.mult)
                    VTT(ABS(9), SCR2(3), UI[:, :], ALU.mult)
                    VTT(SCR2(3), PS5[:, 128:256], SCR2(2), ALU.mult,
                        waits=[("se", xe2)])
                    VTT(ABS(4), SCR2(3), MST[:, :], ALU.mult)
                    nv0 = VTT(ABS(7), IDB[:, :], ABS(3), ALU.add, tick=True)
                    P.wait("tensor", [("ve", nv0)])
                    qcur, qnxt = 7, 8
                    ncur, tcur, nnxt, tnxt = 3, 4, 5, 6
                    for k in range(6):
                        MM(PS4[:, 0:128], ABS(tcur), ABS(ncur))
                        m1_ = MM(PS5[:, 0:128], ABS(ncur), ABS(tcur), tick=True)
                        ev1 = VCP(ABS(nnxt), PS4[:, 0:128], waits=[("pe", m1_)])
                        ev1 = VCP(ABS(tnxt), PS5[:, 0:128], tick=True)
                        m2_ = MM(PS4[:, 128:256], ABS(tnxt), ABS(qcur),
                                 waits=[("ve", ev1)], tick=True)
                        qv = VTT(ABS(qnxt), PS4[:, 128:256], ABS(qcur), ALU.add,
                                 waits=[("pe", m2_)], tick=True)
                        P.wait("tensor", [("ve", qv)])
                        qcur, qnxt = qnxt, qcur
                        ncur, nnxt = nnxt, ncur
                        tcur, tnxt = tnxt, tcur
                    WINV = ABS(qcur)
                    if OPT_P1_STOP < 4:
                        ofmv[h] = P.cnt["ve"]
                        continue
                    sv0 = VCP(ABS(13), S0[:, h * 128:(h + 1) * 128], tick=True)
                    ks0 = MM(PS23[:, 0:128], ABS(1), ABS(13), waits=[("ve", sv0)],
                             tick=True)
                    VTS(ABS(11), VNs(ci, h * 128, 128), SCc(ci, 0)[:, h:h + 1],
                        None, ALU.mult)
                    rv_ = VSTT(ABS(12), PS23[:, 0:128], SCc(ci, 40)[:, h:h + 1],
                               ABS(11), ALU.mult, ALU.add,
                               waits=[("pe", ks0)], tick=True)
                    up = MM(PS23[:, 512:640], WINV, ABS(12), waits=[("ve", rv_)],
                            tick=True)
                    uv = VCP(ABS(10), PS23[:, 512:640], waits=[("pe", up)], tick=True)
                    MM(PS23[:, 128:256], ABS(9), ABS(10), waits=[("ve", uv)])
                    o2 = MM(PS23[:, 256:384], ABS(0), ABS(13), tick=True)
                    VTS(SCR2(3), PS23[:, 256:384], SCc(ci, 32)[:, h:h + 1], None,
                        ALU.mult, waits=[("pe", o2)])
                    ov = VTT(OTM, PS23[:, 128:256], SCR2(3), ALU.add, tick=True)
                    kt_ = VTS(ABS(14), KNs(ci, h * 128, 128),
                              SCc(ci, 48)[:, h:h + 1], None, ALU.mult, tick=True)
                    sp_ = MM(PS23[:, 640:768], ABS(14), ABS(10),
                             waits=[("ve", kt_)], tick=True)
                    s0v = VSTT(S0[:, h * 128:(h + 1) * 128],
                               S0[:, h * 128:(h + 1) * 128],
                               SCc(ci, 56)[:, h:h + 1], PS23[:, 640:768],
                               ALU.mult, ALU.add, waits=[("pe", sp_)], tick=True)
                    if OPT_P1_STOP < 5:
                        ofmv[h] = s0v
                        continue
                    VTT(SCR2(3), OTM, OTM, ALU.mult, waits=[("ve", ov)])
                    VRD(AFa[:, 32:33], SCR2(3), ALU.add)
                    VSEP()
                    vsq = VSEP(tick=True)
                    ACT(AFa[:, 33:34], AFa[:, 32:33], AF.Ln, scale=1.0 / 128,
                        bias=1e-6, waits=[("ve", vsq)])
                    on1 = ACT(AFa[:, 33:34], AFa[:, 33:34], AF.Exp, scale=-0.5,
                              tick=True)
                    VTS(OTM, OTM, AFa[:, 33:34], None, ALU.mult, waits=[("se", on1)])
                    VTT(OTM, OTM, ONB[:, :], ALU.mult)
                    obv = VTT(ABS(15), OTM, GNs(ci, h * 128, 128), ALU.mult,
                              tick=True)
                    tro = TR(PS7B[:, 384:512], ABS(15), waits=[("ve", obv)],
                             tick=True)
                    ofmv[h] = VCP(ABS(16 + h), PS7B[:, 384:512],
                                  waits=[("pe", tro)], tick=True)
                    vseq = ofmv[h]
                if OPT_P1_STOP < 6:
                    continue
                if "OW" in OPT_SER or "O1" in OPT_SER:
                    P.serialize = True
                w0 = c * 128
                rvv = 0
                x1v = [0] * NDJ
                for do in range(NDJ):
                    pd = PS5 if do % 2 == 0 else PS23
                    po = 0 if do % 2 == 0 else 384
                    for h in range(NH):
                        ws = []
                        if h == 0 and do == 0:
                            ws = [("ve", max(ofmv))]
                        elif h == 0 and do >= 2:
                            ws = [("ve", x1v[do - 2])]
                        MM(pd[:, po:po + 128], OWBs(h, do), ABS(16 + h),
                           start=(h == 0), stop=(h == NH - 1), waits=ws,
                           tick=(h == NH - 1))
                    owt = P.cnt["pe"]
                    x1v[do] = VSTT(X1P(do), pd[:, po:po + 128], 0.5, ZR(ci, do),
                                   ALU.mult, ALU.add, waits=[("pe", owt)], tick=True)
                    rs = RES[:, do * T + w0: do * T + w0 + 128]
                    VTT(rs, rs, X1P(do), ALU.add)
                if "O1" in OPT_SER:
                    P.serialize = False
                if "O2" in OPT_SER:
                    P.serialize = True
                if OPT_P1_STOP < 7:
                    continue
                sqv = [0, 0]
                smv2 = [0] * NDJ
                for dj in range(NDJ):
                    sqv[dj % 2] = ACT(SCR2(dj % 2), X1P(dj), AF.Square,
                                      waits=([("pe", smv2[dj - 2])] if dj >= 2
                                             else [("ve", x1v[dj])]), tick=True)
                    smv2[dj] = MM(PS01[0:1, 0:128], ONESC[:, :], SCR2(dj % 2),
                                  start=(dj == 0), stop=(dj == NDJ - 1),
                                  waits=[("se", sqv[dj % 2])], tick=True)
                ACT(RWS[0:1, 384:512], PS01[0:1, 0:128], AF.Ln, scale=1.0 / D,
                    bias=1e-6, waits=[("pe", smv2[NDJ - 1])])
                r2s = ACT(RWS[0:1, 384:512], RWS[0:1, 384:512], AF.Exp, scale=-0.5,
                          tick=True)
                r2v = VCP(RWR[0:1, 384:512], RWS[0:1, 384:512],
                          waits=[("se", r2s)], tick=True)
                rb2 = MM(PS6[:, 0:128], ONESR_R[0:1, :], RWR[0:1, 384:512],
                         waits=[("ve", r2v)], tick=True)
                if OPT_P1_STOP < 8:
                    continue
                for dj in range(NDJ):
                    VTT(SCR2(0), X1P(dj), PS6[:, 0:128], ALU.mult,
                        waits=([("pe", rb2), ("ldg", h2v)] if dj == 0 else ()))
                    VTS(ABS(25 + dj), SCR2(0), CSM[:, 24 + dj:25 + dj], None,
                        ALU.mult, tick=(dj == NDJ - 1))
                if "OW" in OPT_SER:
                    P.serialize = False
                h2t = P.cnt["ve"]
                if not OPT_NO_H2D:
                    for dj in range(NDJ):
                        h2v = P.dma("sync",
                                    h2d[:, dj * T + w0: dj * T + w0 + 128],
                                    AB[:, SCN + (25 + dj) * 128:
                                       SCN + (26 + dj) * 128],
                                    waits=[("ve", h2t)] if dj == 0 else (),
                                    sem="ldg")
                vseq = h2t
        P.serialize = False
        if not OPT_NO_H2D:
            P.wait("sync", [("ldg", h2v)])

    def emit_ffn2():
        P.barrier()
        pwv = stage(WBF, 0, dr["pw"], 32768)
        FFAs = lambda ff: AB[:, A2["FFA"] + ff * P2W: A2["FFA"] + (ff + 1) * P2W]
        H2Ls = lambda dj: AB[:, A2["H2L"] + dj * P2W: A2["H2L"] + (dj + 1) * P2W]
        FCRs = lambda r: AB[:, A2["FCR"] + r * 1024: A2["FCR"] + (r + 1) * 1024]
        PWB = lambda ff, do: WBF[:, ff * 1024 + do * 128: ff * 1024 + (do + 1) * 128]
        fcv = {r: 0 for r in range(4)}
        fcu = {r: 0 for r in range(4)}
        h2ld = 0
        silv = [0, 0]
        evv = [0, 0]
        for tc in range(NP2):
            w0 = tc * P2W
            for dj in range(NDJ):
                h2ld = P.dma("sync",
                             AB[:, A2["H2L"] + dj * P2W: A2["H2L"] + (dj + 1) * P2W],
                             h2d[:, dj * T + w0: dj * T + w0 + P2W],
                             waits=([("ve", evv[0]), ("ve", evv[1])]
                                    if dj == 0 else ()), sem="ldh")
            for ff in range(32):
                r = ff % 4
                par = stg["par"]
                stg["par"] ^= 1
                sem = "sga" if par == 0 else "sgb"
                for dj in range(NDJ):
                    ldv = P.dma("sync", STG[:, par * 1024 + dj * 128:
                                            par * 1024 + (dj + 1) * 128],
                                dr["fcw"][:, dj * 4096 + ff * 128:
                                          dj * 4096 + (ff + 1) * 128],
                                waits=([stg["cast"][par]]
                                       if dj == 0 else ()), sem=sem)
                cv = VCP(FCRs(r), STG[:, par * 1024: par * 1024 + 1024],
                         waits=[(sem, ldv), ("pe", fcu[r])], tick=True)
                stg["cast"][par] = ("ve", cv)
                fcv[r] = cv
                pd = PS01 if ff % 2 == 0 else PS23
                for dj in range(NDJ):
                    ws = []
                    if dj == 0:
                        ws = [("ve", fcv[r]), ("se", silv[ff % 2]),
                              ("ldh", h2ld)]
                    MM(pd[:, 0:P2W], FCRs(r)[:, dj * 128:(dj + 1) * 128],
                       H2Ls(dj), start=(dj == 0), stop=(dj == NDJ - 1),
                       waits=ws, tick=(dj == NDJ - 1))
                fcu[r] = P.cnt["pe"]
                silv[ff % 2] = ACT(FFAs(ff), pd[:, 0:P2W], AF.Silu,
                                   waits=[("pe", fcu[r])], tick=True)
            sall = max(silv)
            for do in range(NDJ):
                pd = PS4 if do % 2 == 0 else PS5
                for ff in range(32):
                    ws = []
                    if ff == 0:
                        ws = [("se", sall), ("ve", evv[do % 2]),
                              ("ve", pwv)]
                    MM(pd[:, 0:P2W], PWB(ff, do), FFAs(ff),
                       start=(ff == 0), stop=(ff == 31), waits=ws,
                       tick=(ff == 31))
                pwt = P.cnt["pe"]
                VTS(AFa[:, 64:64 + P2W], pd[:, 0:P2W], 0.5, None, ALU.mult,
                    waits=[("pe", pwt)])
                rs = RES[:, do * T + w0: do * T + w0 + P2W]
                evv[do % 2] = VTT(rs, rs, AFa[:, 64:64 + P2W], ALU.add, tick=True)
        P.wait("tensor", [("ve", max(evv))])

    if phases[0]:
        emit_ffn()
    if phases[1]:
        emit_attn()
    if phases[2]:
        emit_ffn2()
    if phases[3]:
        emit_ffn()

    P.barrier()
    stv = 0
    for dj in range(NDJ):
        stv = P.dma("sync", y[:, dj * T:(dj + 1) * T], RES[:, dj * T:(dj + 1) * T],
                    sem="st")
    if dbg:
        dab = nc.dram_tensor("dbgab", [128, 24576], BF16, kind="ExternalOutput").ap()
        dwt = nc.dram_tensor("dbgwt", [128, NTT * 128], F32, kind="ExternalOutput").ap()
        dsc = nc.dram_tensor("dbgsc", [128, GRP * 80 + 16], F32, kind="ExternalOutput").ap()
        ds0 = nc.dram_tensor("dbgs0", [128, NH * 128], F32, kind="ExternalOutput").ap()
        dst_ = nc.dram_tensor("dbgstg", [128, 2048], F32, kind="ExternalOutput").ap()
        dwb = nc.dram_tensor("dbgwbf", [128, 32768], BF16, kind="ExternalOutput").ap()
        daf = nc.dram_tensor("dbgaf", [128, 3904], F32, kind="ExternalOutput").ap()
        dsm = nc.dram_tensor("dbgsm", [128, 96], F32, kind="ExternalOutput").ap()
        for a in range(0, 24576, 8192):
            stv = P.dma("sync", dab[:, a:a + 8192], AB[:, a:a + 8192], sem="st")
        stv = P.dma("sync", dwt, WTT[:, :], sem="st")
        stv = P.dma("sync", dsc, SC8[:, :], sem="st")
        stv = P.dma("sync", ds0, S0[:, :], sem="st")
        stv = P.dma("sync", dst_, STG[:, :], sem="st")
        for a in range(0, 32768, 8192):
            stv = P.dma("sync", dwb[:, a:a + 8192], WBF[:, a:a + 8192], sem="st")
        stv = P.dma("sync", daf, AFa[:, :], sem="st")
        stv = P.dma("sync", dsm[:, 0:16], SMALL[:, :], sem="st")
        stv = P.dma("sync", dsm[0:1, 16:80], ROW8[0:1, 0:64], sem="st")
    P.wait("sync", [("st", stv)])

    with nc.Block() as block:
        P.emit(block)
    return nc


def _fm(w):
    n = w.shape[1]
    return np.ascontiguousarray(w.reshape(8, 128, n).transpose(1, 0, 2)
                                .reshape(128, 8 * n))


def _prep_inputs(inputs, T):
    f32 = np.float32
    g = {k: np.asarray(v, dtype=f32) for k, v in inputs.items()}
    shared = {
        "wm": _fm(g["L0_ffn_Wm"]),
        "w1t": np.ascontiguousarray(g["L0_ffn_W1"].transpose(1, 0, 2)
                                    .reshape(128, 16 * 512)),
        "w2r": np.ascontiguousarray(g["L0_ffn_W2"].reshape(64, 128, 128)
                                    .transpose(1, 0, 2).reshape(128, 64 * 128)),
        "wo": _fm(g["L0_ffn_Wo"]),
        "qw": _fm(g["L0_attn_qw"]),
        "kw": _fm(g["L0_attn_kw"]),
        "vw": _fm(g["L0_attn_vw"]),
        "gw": _fm(g["L0_attn_gw"]),
        "ow": _fm(g["L0_attn_ow"]),
        "fcw": _fm(g["L0_attn_fcw"]),
        "pw": np.ascontiguousarray(g["L0_attn_pw"].reshape(32, 128, 1024)
                                   .transpose(1, 0, 2).reshape(128, 32 * 1024)),
        "bwaw": _fm(np.concatenate([g["L0_attn_bw"], g["L0_attn_aw"]], axis=1)),
        "emb": np.ascontiguousarray(g["L0_ffn_expert_emb"]),
        "b2": np.ascontiguousarray(g["L0_ffn_b2"]),
        "b1c": np.ascontiguousarray(g["L0_ffn_b1"].reshape(16, 4, 128)
                                    .transpose(2, 0, 1).reshape(128, 64)),
        "bmc": np.ascontiguousarray(g["L0_ffn_bm"].reshape(8, 128).T),
        "boc": np.ascontiguousarray(g["L0_ffn_bo"].reshape(8, 128).T),
        "ln1c": np.ascontiguousarray(g["L0_attn_ln1_w"].reshape(8, 128).T),
        "ln2c": np.ascontiguousarray(g["L0_attn_ln2_w"].reshape(8, 128).T),
        "alog": g["L0_attn_A_log"].reshape(1, 8),
        "dtb": g["L0_attn_dt_bias"].reshape(1, 8),
        "onormr": g["L0_attn_onorm_w"].reshape(1, 128),
        "identc": np.eye(128, dtype=f32),
        "suc": np.tril(np.ones((128, 128), f32), -1),
        "uic": np.triu(np.ones((128, 128), f32), 0),
        "msc": np.triu(np.ones((128, 128), f32), 1),
        "mstc": np.tril(np.ones((128, 128), f32), -1),
        "onesc": np.ones((128, 1), f32),
        "onesr": np.ones((128, 128), f32),
        "fconst": np.tile(np.array(FCONST_VALS, f32), (128, 1)),
    }
    shared = {k: np.ascontiguousarray(v, dtype=f32) for k, v in shared.items()}
    x = g["L0_attn_x"]
    in_maps = []
    for i in range(x.shape[0]):
        m = dict(shared)
        m["x"] = _fm(np.ascontiguousarray(x[i].T))
        in_maps.append(m)
    return in_maps


def kernel(**inputs):
    x = np.asarray(inputs["L0_attn_x"])
    B, T, _ = x.shape
    nc = build_program(T=T)
    in_maps = _prep_inputs(inputs, T)
    res = run_bass_kernel_spmd(nc, in_maps, core_ids=list(range(B)))
    outs = []
    for i in range(B):
        yy = res.results[i]["y"]
        full = yy.reshape(128, 8, T).transpose(1, 0, 2).reshape(1024, T)
        outs.append(full.T)
    return np.ascontiguousarray(np.stack(outs, axis=0), dtype=np.float32)


# version 08 (final)
# speedup vs baseline: 1.0750x; 1.0091x over previous
import numpy as np
import ml_dtypes
import concourse.bass as bass
import concourse.mybir as mybir
from concourse.bass_utils import run_bass_kernel_spmd

F32 = mybir.dt.float32
F32R = mybir.dt.float32r
BF16 = mybir.dt.bfloat16
AF = mybir.ActivationFunctionType
ALU = mybir.AluOpType
AX = mybir.AxisListType

ENGINES = ("tensor", "vector", "scalar", "gpsimd", "sync")
OPT_NO_H2D = False
OPT_SER = ""
OPT_NONCE = 0
OPT_P1_STOP = 99
OPT_NG = 99
FCONST_VALS = [1e-5, 1e-6, -0.5, -1.0, 0.5, 1.0 / 128.0, 1.0 / 1024.0,
               -1.0e30, 2.0, 3.0, 0.25, -2.0, 8.0, 16.0, 1.5, -1.5]
D = 1024
NDJ = 8
NH = 8
NE = 16
CHK = 128


class Prog:
    def __init__(self, nc):
        self.nc = nc
        self.streams = {e: [] for e in ENGINES}
        names = ["pe", "ve", "se", "ge", "ldy", "ldg", "st", "sga", "sgb", "ldh"]
        self.sem = {n: nc.alloc_semaphore("sem_" + n) for n in names}
        self.cnt = {n: 0 for n in names}
        self.engsem = {"tensor": "pe", "vector": "ve", "scalar": "se", "gpsimd": "ge"}

    def do(self, eng, fn, waits=(), tick=False):
        name = self.engsem.get(eng)
        if getattr(self, "serialize", False) and name is not None:
            waits = list(waits) + [(k, self.cnt[k]) for k in
                                   ("pe", "ve", "se", "ge") if k != name]
            tick = True
        tn = None
        if tick:
            self.cnt[name] += 1
            tn = name
        self.streams[eng].append((fn, [w for w in waits if w and w[1] > 0],
                                  self.sem[tn] if tn else None, 1))
        return self.cnt[name] if tick else None

    def dma(self, eng, out, in_, waits=(), sem=None):
        if sem is None:
            sem = "ldg" if eng == "gpsimd" else "ldy"
        self.cnt[sem] += 16
        self.streams[eng].append(
            ((lambda e, o=out, i=in_: e.dma_start(out=o, in_=i)),
             [w for w in waits if w and w[1] > 0], self.sem[sem], 16))
        return self.cnt[sem]

    def wait(self, eng, waits):
        self.streams[eng].append((None, [w for w in waits if w and w[1] > 0], None, 0))

    def barrier(self):
        snap = dict(self.cnt)
        for eng in ENGINES:
            own = self.engsem.get(eng)
            ws = [(k, v) for k, v in snap.items() if v > 0 and k != own]
            self.wait(eng, ws)

    def emit(self, block):
        for eng in ENGINES:
            stream = self.streams[eng]
            if not stream:
                continue

            def make(stream):
                def body(e):
                    for fn, waits, sem, n in stream:
                        for (wn, wv) in waits:
                            e.wait_ge(self.sem[wn], wv)
                        if fn is not None:
                            ins = fn(e)
                            if sem is not None:
                                ins.then_inc(sem, n)
                return body
            getattr(block, eng)(make(stream))


def build_program(T=2048, phases=(1, 1, 1, 1), dbg=False):
    CNC = T // CHK
    NHALF = 2 if T >= 1024 else 1
    TH = T // NHALF
    TCW = min(512, TH)
    NTCH = TH // TCW
    NTT = TH // 128
    GRP = 2
    NW1 = max(1, TH // 512)
    W1W = min(512, TH)
    P2W = 512 if T >= 1024 else (256 if T >= 512 else 128)
    NP2 = T // P2W

    nc = bass.Bass("TRN2", target_bir_lowering=False, debug=False,
                   dynamic_dma_scratch_size=2048,
                   detect_race_conditions=False)
    P = Prog(nc)

    dr = {}
    def dram_in(name, shape, dt=F32):
        dr[name] = nc.dram_tensor(name, shape, dt, kind="ExternalInput").ap()
        return dr[name]

    dram_in("x", [128, NDJ * T])
    for w in ["wm", "w1t", "w2r", "wo", "qw", "kw", "vw", "gw", "ow"]:
        dram_in(w, [128, 8192])
    dram_in("fcw", [128, NDJ * 4096])
    dram_in("pw", [128, 32 * 1024])
    dram_in("bwaw", [128, 128])
    dram_in("emb", [128, NE])
    dram_in("b2", [16, 128])
    dram_in("b1c", [128, 64])
    dram_in("bmc", [128, 8])
    dram_in("boc", [128, 8])
    dram_in("ln1c", [128, 8])
    dram_in("ln2c", [128, 8])
    dram_in("alog", [1, 8])
    dram_in("dtb", [1, 8])
    dram_in("onormr", [1, 128])
    dram_in("identc", [128, 128])
    dram_in("suc", [128, 128])
    dram_in("uic", [128, 128])
    dram_in("msc", [128, 128])
    dram_in("mstc", [128, 128])
    dram_in("onesc", [128, 1])
    dram_in("onesr", [128, 128])
    dram_in("fconst", [128, 16])

    y = nc.dram_tensor("y", [128, NDJ * T], F32, kind="ExternalOutput").ap()
    h2d = nc.dram_tensor("h2d", [128, NDJ * T], BF16, kind="Internal").ap()

    RES = nc.alloc_sbuf_tensor("RES", [128, NDJ * T], F32)
    WBF = nc.alloc_sbuf_tensor("WBF", [128, 32768], BF16)
    AB = nc.alloc_sbuf_tensor("AB", [128, 24576], BF16)
    AFa = nc.alloc_sbuf_tensor("AFa", [128, 3904], F32)
    STG = nc.alloc_sbuf_tensor("STG", [128, 2048], F32)
    IDB = nc.alloc_sbuf_tensor("IDB", [128, 128], BF16)
    SU = nc.alloc_sbuf_tensor("SUc", [128, 128], F32)
    UI = nc.alloc_sbuf_tensor("UIc", [128, 128], F32)
    MSK = nc.alloc_sbuf_tensor("MSc", [128, 128], F32)
    MST = nc.alloc_sbuf_tensor("MSTc", [128, 128], F32)
    ONESC = nc.alloc_sbuf_tensor("ONESCc", [128, 1], F32)
    ONESR_R = nc.alloc_sbuf_tensor("ONESR_R", [128, 128], F32R)
    CSM = nc.alloc_sbuf_tensor("CSM", [128, 128], F32)
    SMALL = nc.alloc_sbuf_tensor("SMALL", [128, 16], F32)
    ONB = nc.alloc_sbuf_tensor("ONB", [128, 128], F32)
    B2B = nc.alloc_sbuf_tensor("B2B", [16, 128], BF16)
    EMB = nc.alloc_sbuf_tensor("EMBB", [128, NE], BF16)
    ROW8 = nc.alloc_sbuf_tensor("ROW8", [1, 64], F32)
    ROW8R = nc.alloc_sbuf_tensor("ROW8R", [1, 64], F32R)
    RWS = nc.alloc_sbuf_tensor("RWS", [128, 512], F32)
    RWR = nc.alloc_sbuf_tensor("RWR", [128, 512], F32R)
    TOTR = nc.alloc_sbuf_tensor("TOTR", [1, 128], F32)
    TOTRR = nc.alloc_sbuf_tensor("TOTRR", [1, 128], F32R)
    FCON = nc.alloc_sbuf_tensor("FCON", [128, 16], F32)
    SEPT = nc.alloc_sbuf_tensor("SEPT", [128, 2], F32)
    S0 = nc.alloc_sbuf_tensor("S0", [128, NH * 128], F32)
    SC8 = nc.alloc_sbuf_tensor("SC8", [128, GRP * 80 + 16], F32)
    WTT = nc.alloc_sbuf_tensor("WTT", [128, NTT * 128], F32)
    WTB = nc.alloc_sbuf_tensor("WTB", [128, NTT * 128], BF16)
    WFMH = nc.alloc_sbuf_tensor("WFMH", [16, NTT * 128], BF16)

    A0 = dict(ZBF=0, HBF=NDJ * TCW, HMID=NDJ * TCW + NDJ * TH,
              CMBF=NDJ * TCW + NDJ * TH + 8 * TH)
    def cmbf_col(h):
        return h * TH if h < 4 else A0["CMBF"] + (h - 4) * TH
    assert A0["CMBF"] + 4 * TH <= 24576
    assert 4 * TH <= NDJ * TCW
    gq = GRP * 1024
    A1 = dict(OWB=0, BWA=8192, HB1=8320, QN=8320 + gq, KN=8320 + 2 * gq,
              VN=8320 + 3 * gq, GN=8320 + 4 * gq, SCN=8320 + 5 * gq)
    assert A1["SCN"] + 33 * 128 <= 24576
    A2 = dict(FFA=0, H2L=32 * P2W, FCR=32 * P2W + NDJ * P2W)
    assert A2["FCR"] + 4 * 1024 <= 24576
    F0 = dict(SCR=64, CMBT=64 + 2 * TCW)
    assert F0["CMBT"] + NTT * 128 <= 3904
    F1 = dict(QS=64, XET=64 + 2048, OTM=64 + 2048 + 128, X1P=64 + 2048 + 256,
              SCR2=64 + 2048 + 256 + 1024)
    assert F1["SCR2"] + 512 <= 3904

    PS01 = nc.alloc_psum_tensor("PS01", [128, 1024], F32)
    PS23 = nc.alloc_psum_tensor("PS23", [128, 1024], F32)
    PS4 = nc.alloc_psum_tensor("PS4", [128, 512], F32)
    PS5 = nc.alloc_psum_tensor("PS5", [128, 512], F32)
    PS6 = nc.alloc_psum_tensor("PS6", [128, 512], F32)
    PS7B = nc.alloc_psum_tensor("PS7B", [128, 1024], BF16)

    def MM(out, lhsT, rhs, start=True, stop=True, waits=(), tick=False):
        return P.do("tensor",
                    lambda t, o=out, l=lhsT, r=rhs, s1=start, s2=stop:
                    t.matmul(o, l, r, start=s1, stop=s2), waits, tick)

    def TR(out, in_, ident=None, waits=(), tick=False):
        idn = IDB if ident is None else ident
        return P.do("tensor", lambda t, o=out, i=in_, d=idn: t.transpose(o, i, d[:, :]),
                    waits, tick)

    def ACT(out, in_, func, bias=0.0, scale=1.0, accum=None, waits=(), tick=False):
        return P.do("scalar",
                    lambda s, o=out, i=in_, f=func, b=bias, sc=scale, a=accum:
                    s.activation(o, i, f, bias=b, scale=sc, accum_out=a), waits, tick)

    def VTT(out, a, b, op, waits=(), tick=False):
        return P.do("vector", lambda v, o=out, x=a, yy=b, p=op:
                    v.tensor_tensor(o, x, yy, p), waits, tick)

    def VTS(out, in0, s1, s2, op0, op1=ALU.bypass, waits=(), tick=False):
        return P.do("vector",
                    lambda v, o=out, i=in0, a=s1, b=s2, p0=op0, p1=op1:
                    v.tensor_scalar(o, i, a, b, p0, p1), waits, tick)

    def VSTT(out, in0, s, in1, op0, op1, accum=None, waits=(), tick=False):
        return P.do("vector",
                    lambda v, o=out, i0=in0, sc=s, i1=in1, p0=op0, p1=op1, a=accum:
                    v.scalar_tensor_tensor(o, i0, sc, i1, p0, p1, accum_out=a),
                    waits, tick)

    def VCP(out, in_, waits=(), tick=False):
        return P.do("vector", lambda v, o=out, i=in_: v.tensor_copy(o, i), waits, tick)

    def VRD(out, in_, op, waits=(), tick=False):
        return P.do("vector", lambda v, o=out, i=in_, p=op:
                    v.tensor_reduce(o, i, AX.X, p), waits, tick)

    def VSEP(tick=False):
        return P.do("vector", lambda v: v.tensor_copy(SEPT[:, 0:1], ONESC[:, :]),
                    (), tick)

    for _ in range(OPT_NONCE):
        P.do("vector", lambda v: v.tensor_copy(SEPT[:, 1:2], ONESC[:, :]))
    if dbg:
        P.do("vector", lambda v: v.memset(AB[:, :], 0.0))
        P.do("vector", lambda v: v.memset(WTT[:, :], 0.0))
        P.do("vector", lambda v: v.memset(S0[:, :], 0.0))
        P.do("vector", lambda v: v.memset(SC8[:, :], 0.0))
        P.do("vector", lambda v: v.memset(STG[:, :], 0.0))
        P.do("vector", lambda v: v.memset(WBF[:, :], 0.0))
        P.do("vector", lambda v: v.memset(AFa[:, :], 0.0), tick=True)
        P.wait("sync", [("ve", P.cnt["ve"])])
        P.wait("tensor", [("ve", P.cnt["ve"])])
        P.wait("scalar", [("ve", P.cnt["ve"])])
    for dj in range(NDJ):
        P.dma("sync", RES[:, dj * T:(dj + 1) * T], dr["x"][:, dj * T:(dj + 1) * T])
    P.dma("sync", SU[:, :], dr["suc"])
    P.dma("sync", UI[:, :], dr["uic"])
    P.dma("sync", MSK[:, :], dr["msc"])
    P.dma("sync", MST[:, :], dr["mstc"])
    P.dma("sync", ONESC[:, :], dr["onesc"])
    P.dma("sync", CSM[:, 0:8], dr["bmc"])
    P.dma("sync", CSM[:, 8:16], dr["boc"])
    P.dma("sync", CSM[:, 16:24], dr["ln1c"])
    P.dma("sync", CSM[:, 24:32], dr["ln2c"])
    P.dma("sync", CSM[:, 32:96], dr["b1c"])
    P.dma("sync", ROW8[:, 0:8], dr["alog"])
    P.dma("sync", ROW8[:, 8:16], dr["dtb"])
    P.dma("sync", STG[:, 640:768], dr["identc"])
    P.dma("sync", STG[:, 128:144], dr["emb"])
    P.dma("sync", STG[0:16, 256:384], dr["b2"])
    P.dma("sync", STG[0:1, 384:512], dr["onormr"])
    P.dma("sync", FCON[:, :], dr["fconst"])
    ld0 = P.dma("sync", STG[:, 512:640], dr["onesr"])
    for j, val in enumerate(FCONST_VALS):
        nc.const_aps.register(F32, val, FCON[:, j:j + 1])
    VCP(IDB[:, :], STG[:, 640:768], waits=[("ldy", ld0)])
    VCP(EMB[:, :], STG[:, 128:144])
    VCP(B2B[:, :], STG[0:16, 256:384])
    VCP(ONESR_R[:, :], STG[:, 512:640])
    ACT(SEPT[:, 0:1], ONESC[:, :], AF.Silu, waits=[("ldy", ld0)])
    ACT(SEPT[:, 0:1], ONESC[:, :], AF.Gelu)
    ACT(SEPT[:, 0:1], ONESC[:, :], AF.Ln)
    ACT(ROW8[:, 16:24], ROW8[:, 0:8], AF.Exp, tick=True)
    se0 = P.cnt["se"]
    VTS(ROW8[:, 24:32], ROW8[:, 16:24], -1.0, None, ALU.mult, waits=[("se", se0)])
    VSEP()
    VCP(ROW8R[:, 0:8], ROW8[:, 24:32])
    VCP(ROW8R[:, 8:16], ROW8[:, 8:16])
    v0 = VCP(TOTRR[:, 0:128], STG[0:1, 384:512], tick=True)
    MM(PS6[:, 0:8], ONESR_R[0:1, :], ROW8R[:, 0:8], waits=[("ve", v0)])
    MM(PS6[:, 8:16], ONESR_R[0:1, :], ROW8R[:, 8:16])
    m0 = MM(PS6[:, 128:256], ONESR_R[0:1, :], TOTRR[:, 0:128], tick=True)
    VCP(SMALL[:, 0:16], PS6[:, 0:16], waits=[("pe", m0)])
    v1 = VCP(ONB[:, :], PS6[:, 128:256], tick=True)
    P.wait("tensor", [("ve", v1)])

    stg = {"par": 0, "cast": [0, 0]}

    def stage(dst, dst_off, src_ap, ncols, extra_wait=None):
        last = 0
        done = 0
        while done < ncols:
            w = min(1024, ncols - done)
            par = stg["par"]
            stg["par"] ^= 1
            sem = "sga" if par == 0 else "sgb"
            ws = [("ve", stg["cast"][par])]
            if extra_wait:
                ws.append(extra_wait)
                extra_wait = None
            ldv = P.dma("sync", STG[:, par * 1024: par * 1024 + w],
                        src_ap[:, done:done + w], waits=ws, sem=sem)
            cv = VCP(dst[:, dst_off + done: dst_off + done + w],
                     STG[:, par * 1024: par * 1024 + w],
                     waits=[(sem, ldv)], tick=True)
            stg["cast"][par] = cv
            last = cv
            done += w
        return last

    def ln_stats(w0, W, scr_off, sq_ps=None):
        if sq_ps is None:
            sq_ps = PS5
        P.wait("tensor", [("ve", P.cnt["ve"]), ("se", P.cnt["se"])])
        P.wait("scalar", [("pe", P.cnt["pe"]), ("ve", P.cnt["ve"])])
        sqv = [0, 0]
        smv = [0] * NDJ
        for dj in range(NDJ):
            xs = RES[:, dj * T + w0: dj * T + w0 + W]
            sqv[dj % 2] = ACT(AFa[:, scr_off + (dj % 2) * W: scr_off + (dj % 2) * W + W],
                              xs, AF.Square,
                              waits=[("pe", smv[dj - 2])] if dj >= 2 else (),
                              tick=True)
            MM(PS6[0:1, 0:W], ONESC[:, :], xs,
               start=(dj == 0), stop=(dj == NDJ - 1))
            smv[dj] = MM(sq_ps[0:1, 0:W], ONESC[:, :],
                         AFa[:, scr_off + (dj % 2) * W: scr_off + (dj % 2) * W + W],
                         start=(dj == 0), stop=(dj == NDJ - 1),
                         waits=[("se", sqv[dj % 2])], tick=True)
        return smv[NDJ - 1]

    def emit_ffn():
        P.barrier()
        stage(WBF, 0, dr["wm"], 8192)
        stage(WBF, 8192, dr["w1t"], 8192)
        stage(WBF, 16384, dr["w2r"], 8192)
        wv = stage(WBF, 24576, dr["wo"], 8192)
        P.wait("tensor", [("ve", wv)])
        WMB = lambda dj, do: WBF[:, dj * 1024 + do * 128: dj * 1024 + (do + 1) * 128]
        W1B = lambda e, it: WBF[:, 8192 + e * 512 + it * 128: 8192 + e * 512 + (it + 1) * 128]
        W2B = lambda e, it: WBF[:, 16384 + (e * 4 + it) * 128: 16384 + (e * 4 + it) * 128 + 128]
        WOB = lambda h, do: WBF[:, 24576 + h * 1024 + do * 128: 24576 + h * 1024 + (do + 1) * 128]
        ZBF = lambda dj: AB[:, A0["ZBF"] + dj * TCW: A0["ZBF"] + (dj + 1) * TCW]
        HBFs = lambda h, a, w: AB[:, A0["HBF"] + h * TH + a: A0["HBF"] + h * TH + a + w]
        HMIDs = lambda par, it, a, w: AB[:, A0["HMID"] + par * 4 * TH + it * TH + a:
                                         A0["HMID"] + par * 4 * TH + it * TH + a + w]
        CMBFs = lambda h, a, w: AB[:, cmbf_col(h) + a: cmbf_col(h) + a + w]
        SCRf = lambda k, w=TCW: AFa[:, F0["SCR"] + k * TCW: F0["SCR"] + k * TCW + w]
        CMBT = lambda tt: AFa[:, F0["CMBT"] + tt * 128: F0["CMBT"] + (tt + 1) * 128]

        for half in range(NHALF):
            t0 = half * TH
            hv = 0
            for tc in range(NTCH):
                w0 = t0 + tc * TCW
                pemm = ln_stats(w0, TCW, F0["SCR"])
                VTS(RWS[0:1, 0:TCW], PS6[0:1, 0:TCW], 1.0 / D, None, ALU.mult,
                    waits=[("pe", pemm)])
                VTT(RWS[32:33, 0:TCW], RWS[0:1, 0:TCW], RWS[0:1, 0:TCW], ALU.mult)
                vs = VSTT(RWS[32:33, 0:TCW], PS5[0:1, 0:TCW], 1.0 / D,
                          RWS[32:33, 0:TCW], ALU.mult, ALU.subtract, tick=True)
                ACT(RWS[64:65, 0:TCW], RWS[32:33, 0:TCW], AF.Ln, bias=1e-5,
                    waits=[("ve", vs)])
                ACT(RWS[64:65, 0:TCW], RWS[64:65, 0:TCW], AF.Exp, scale=-0.5, tick=True)
                sr = P.cnt["se"]
                VCP(RWR[0:1, 0:TCW], RWS[0:1, 0:TCW], waits=[("se", sr)])
                vr = VCP(RWR[32:33, 0:TCW], RWS[64:65, 0:TCW], tick=True)
                MM(PS4[:, 0:TCW], ONESR_R[0:1, :], RWR[0:1, 0:TCW], waits=[("ve", vr)])
                bc = MM(PS5[:, 0:TCW], ONESR_R[32:33, :], RWR[32:33, 0:TCW], tick=True)
                for dj in range(NDJ):
                    xs = RES[:, dj * T + w0: dj * T + w0 + TCW]
                    VTT(SCRf(0), xs, PS4[:, 0:TCW], ALU.subtract,
                        waits=[("pe", bc)] if dj == 0 else ())
                    VTT(ZBF(dj), SCRf(0), PS5[:, 0:TCW], ALU.mult,
                        tick=(dj == NDJ - 1))
                zv = P.cnt["ve"]
                evs = [hv, hv]
                for do in range(NDJ):
                    pd = PS01 if do % 2 == 0 else PS23
                    for dj in range(NDJ):
                        ws = []
                        if dj == 0 and do == 0:
                            ws.append(("ve", zv))
                        if dj == 0:
                            ws.append(("se", evs[do % 2]))
                        MM(pd[:, 0:TCW], WMB(dj, do), ZBF(dj),
                           start=(dj == 0), stop=(dj == NDJ - 1),
                           waits=ws, tick=(dj == NDJ - 1))
                    mmt = P.cnt["pe"]
                    evs[do % 2] = ACT(HBFs(do, tc * TCW, TCW), pd[:, 0:TCW],
                                      AF.Identity, bias=CSM[:, do:do + 1],
                                      waits=[("pe", mmt)], tick=True)
                hv = max(evs)
            P.wait("tensor", [("se", hv)])

            wfmv = 0
            rt_prev = 0
            for tt in range(NTT):
                for h in range(NH):
                    MM(PS6[:, h * 16:(h + 1) * 16], HBFs(h, tt * 128, 128), EMB[:, :],
                       waits=[("se", rt_prev)] if h == 0 else (),
                       tick=(h == NH - 1))
                lv = P.cnt["pe"]
                VRD(AFa[:, 0:8], PS6[:, 0:128].rearrange("p (a b) -> p a b", b=16),
                    ALU.max, waits=[("pe", lv)])
                VSEP()
                nv = VTS(AFa[:, 8:16], AFa[:, 0:8], -1.0, None, ALU.mult, tick=True)
                for h in range(NH):
                    rt_prev = ACT(SCRf(0, 128)[:, h * 16:(h + 1) * 16],
                                  PS6[:, h * 16:(h + 1) * 16], AF.Exp,
                                  bias=AFa[:, 8 + h:9 + h],
                                  waits=[("ve", nv)] if h == 0 else (),
                                  tick=(h == NH - 1))
                VRD(AFa[:, 16:24], SCRf(0, 128).rearrange("p (a b) -> p a b", b=16),
                    ALU.add, waits=[("se", rt_prev)])
                VSEP()
                P.do("vector", lambda v: v.reciprocal(AFa[:, 24:32], AFa[:, 16:24]))
                VSEP()
                for h in range(NH):
                    VTS(WTT[:, tt * 128 + h * 16: tt * 128 + (h + 1) * 16],
                        SCRf(0, 128)[:, h * 16:(h + 1) * 16],
                        AFa[:, 24 + h:25 + h], None, ALU.mult)
                VSEP()
                VRD(AFa[:, 32:40],
                    WTT[:, tt * 128:(tt + 1) * 128].rearrange("p (a b) -> p a b", b=16),
                    ALU.max)
                VSEP()
                P.do("vector", lambda v, tt=tt: v.match_replace(
                    SCRf(1, 128), AFa[:, 32:40], WTT[:, tt * 128:(tt + 1) * 128],
                    -1.0e30))
                VSEP()
                VRD(AFa[:, 40:48], SCRf(1, 128).rearrange("p (a b) -> p a b", b=16),
                    ALU.max)
                VSEP()
                for h in range(NH):
                    VTS(SCRf(1, 128)[:, h * 16:(h + 1) * 16],
                        WTT[:, tt * 128 + h * 16: tt * 128 + (h + 1) * 16],
                        AFa[:, 40 + h:41 + h], None, ALU.is_ge)
                VSEP()
                VTT(WTT[:, tt * 128:(tt + 1) * 128], WTT[:, tt * 128:(tt + 1) * 128],
                    SCRf(1, 128), ALU.mult)
                wfmv = VCP(WTB[:, tt * 128:(tt + 1) * 128],
                           WTT[:, tt * 128:(tt + 1) * 128], tick=True)
            P.wait("tensor", [("ve", wfmv)])

            geluv = [0, 0]
            sttv = {s: 0 for s in range(8)}
            b2ev = 0
            for h in range(NH):
                for tt in range(NTT):
                    TR(PS7B[0:16, tt * 128:(tt + 1) * 128],
                       WTB[:, tt * 128 + h * 16: tt * 128 + (h + 1) * 16],
                       waits=[("ve", b2ev)] if tt == 0 else (),
                       tick=(tt == NTT - 1))
                trw = P.cnt["pe"]
                wfv = VCP(WFMH[0:16, 0:NTT * 128], PS7B[0:16, 0:NTT * 128],
                          waits=[("pe", trw)], tick=True)
                for tt in range(NTT):
                    pb = PS01 if tt < 4 else PS23
                    MM(pb[:, (tt % 4) * 128:(tt % 4 + 1) * 128],
                       WFMH[0:16, tt * 128:(tt + 1) * 128],
                       B2B[0:16, :],
                       waits=([("ve", wfv), ("se", max(geluv))]
                              if tt == 0 else ()),
                       tick=(tt == NTT - 1))
                b2t = P.cnt["pe"]
                if NTT > 4:
                    b2ev = VCP(AFa[:, F0["CMBT"]:F0["CMBT"] + 4 * 128],
                               PS01[:, 0:4 * 128],
                               waits=[("pe", b2t), ("ve", max(sttv.values()))],
                               tick=True)
                    b2ev = VCP(AFa[:, F0["CMBT"] + 4 * 128:F0["CMBT"] + NTT * 128],
                               PS23[:, 0:(NTT - 4) * 128], tick=True)
                else:
                    b2ev = VCP(AFa[:, F0["CMBT"]:F0["CMBT"] + NTT * 128],
                               PS01[:, 0:NTT * 128],
                               waits=[("pe", b2t), ("ve", max(sttv.values()))],
                               tick=True)
                for e in range(NE):
                    par_he = (h * NE + e) % 2
                    for it in range(4):
                        pd = PS01 if it % 2 == 0 else PS23
                        for n2 in range(NW1):
                            ws = []
                            if n2 == 0:
                                ws = [("se", geluv[it % 2])]
                                if e == 0:
                                    ws.append(("ve", b2ev))
                            MM(pd[:, n2 * W1W:(n2 + 1) * W1W], W1B(e, it),
                               HBFs(h, n2 * W1W, W1W), start=True, stop=True,
                               waits=ws, tick=(n2 == NW1 - 1))
                        mmt = P.cnt["pe"]
                        geluv[it % 2] = ACT(
                            HMIDs(par_he, it, 0, TH), pd[:, 0:TH], AF.Gelu,
                            bias=CSM[:, 32 + e * 4 + it:33 + e * 4 + it],
                            waits=[("pe", mmt)], tick=True)
                    gall = max(geluv)
                    for tt in range(NTT):
                        bank = tt % 2
                        pd2 = PS4 if bank == 0 else PS5
                        so = ((tt // 2) % 4) * 128
                        s = tt % 8
                        for it in range(4):
                            ws = []
                            if it == 0:
                                ws = [("se", gall),
                                      ("ve", max(sttv[s], sttv.get(tt % 2 + ((tt - 2) % 8 // 2) * 2, 0) if tt >= 2 else 0, b2ev))]
                                ws = [("se", gall),
                                      ("ve", max(sttv[s],
                                                 sttv[(tt - 2) % 8] if tt >= 2 else 0,
                                                 b2ev))]
                            MM(pd2[:, so:so + 128],
                               HMIDs(par_he, it, tt * 128, 128), W2B(e, it),
                               start=(it == 0), stop=(it == 3),
                               waits=ws, tick=(it == 3))
                        w2t = P.cnt["pe"]
                        sttv[s] = VSTT(
                            CMBT(tt), pd2[:, so:so + 128],
                            WTT[:, tt * 128 + h * 16 + e: tt * 128 + h * 16 + e + 1],
                            CMBT(tt), ALU.mult, ALU.add,
                            waits=[("pe", w2t)], tick=True)
                csc = A0["HMID"] + 4 * TH
                cbv = 0
                for tt in range(NTT):
                    cbv = VCP(AB[:, csc + tt * 128:csc + (tt + 1) * 128], CMBT(tt),
                              tick=(tt == NTT - 1))
                for tt in range(NTT):
                    TR(PS7B[:, tt * 128:(tt + 1) * 128],
                       AB[:, csc + tt * 128:csc + (tt + 1) * 128],
                       waits=[("ve", cbv)] if tt == 0 else (),
                       tick=(tt == NTT - 1))
                trt = P.cnt["pe"]
                b2ev = VCP(CMBFs(h, 0, NTT * 128), PS7B[:, 0:NTT * 128],
                           waits=[("pe", trt)], tick=True)
            P.wait("tensor", [("ve", P.cnt["ve"]), ("se", P.cnt["se"])])
            rv = [0, 0]
            for tc in range(NTCH):
                for do in range(NDJ):
                    pd = PS01 if do % 2 == 0 else PS23
                    for h in range(NH):
                        ws = []
                        if h == 0:
                            ws = [("ve", rv[do % 2])]
                        MM(pd[:, 0:TCW], WOB(h, do), CMBFs(h, tc * TCW, TCW),
                           start=(h == 0), stop=(h == NH - 1),
                           waits=ws, tick=(h == NH - 1))
                    wot = P.cnt["pe"]
                    VTS(SCRf(1), pd[:, 0:TCW], CSM[:, 8 + do:9 + do], 0.5,
                        ALU.add, ALU.mult, waits=[("pe", wot)])
                    rs = RES[:, do * T + t0 + tc * TCW: do * T + t0 + tc * TCW + TCW]
                    rv[do % 2] = VTT(rs, rs, SCRf(1), ALU.add, tick=True)
            P.wait("tensor", [("ve", max(rv)), ("se", P.cnt["se"])])

    def emit_attn():
        P.barrier()
        if "ALL" in OPT_SER or "AB" in OPT_SER or "ST" in OPT_SER:
            P.serialize = True
        stage(WBF, 0, dr["qw"], 8192)
        stage(WBF, 8192, dr["kw"], 8192)
        stage(WBF, 16384, dr["vw"], 8192)
        stage(WBF, 24576, dr["gw"], 8192)
        stage(AB, A1["OWB"], dr["ow"], 8192)
        wv = stage(AB, A1["BWA"], dr["bwaw"], 128)
        P.wait("tensor", [("ve", wv)])
        if OPT_SER == "ST":
            P.serialize = False
        QWB = lambda dj, n: WBF[:, dj * 1024 + n: dj * 1024 + n + 512]
        KWB = lambda dj, n: WBF[:, 8192 + dj * 1024 + n: 8192 + dj * 1024 + n + 512]
        VWB = lambda dj, n: WBF[:, 16384 + dj * 1024 + n: 16384 + dj * 1024 + n + 512]
        GWB = lambda dj, n: WBF[:, 24576 + dj * 1024 + n: 24576 + dj * 1024 + n + 512]
        OWBs = lambda h, do: AB[:, A1["OWB"] + h * 1024 + do * 128:
                                A1["OWB"] + h * 1024 + (do + 1) * 128]
        BWAs = lambda dj: AB[:, A1["BWA"] + dj * 16: A1["BWA"] + (dj + 1) * 16]
        HB1 = lambda ci, dj: AB[:, A1["HB1"] + ci * 1024 + dj * 128:
                                A1["HB1"] + ci * 1024 + (dj + 1) * 128]
        def TMS(base):
            return lambda ci, lo, w: AB[:, A1[base] + ci * 1024 + lo:
                                        A1[base] + ci * 1024 + lo + w]
        QNs, KNs, VNs, GNs = TMS("QN"), TMS("KN"), TMS("VN"), TMS("GN")
        SCN = A1["SCN"]
        ABS = lambda k, w=128: AB[:, SCN + k * 128: SCN + k * 128 + w]
        QS = lambda k: AFa[:, F1["QS"] + k * 1024: F1["QS"] + (k + 1) * 1024]
        XET = AFa[:, F1["XET"]:F1["XET"] + 128]
        OTM = AFa[:, F1["OTM"]:F1["OTM"] + 128]
        X1P = lambda dj: AFa[:, F1["X1P"] + dj * 128: F1["X1P"] + (dj + 1) * 128]
        SCR2 = lambda k, w=128: AFa[:, F1["SCR2"] + k * 128: F1["SCR2"] + k * 128 + w]
        ZR = lambda ci, dj: STG[:, ci * 1024 + dj * 128: ci * 1024 + (dj + 1) * 128]
        SCc = lambda ci, k, w=8: SC8[:, ci * 80 + k: ci * 80 + k + w]

        z0 = P.do("vector", lambda v: v.memset(S0[:, :], 0.0), tick=True)
        P.wait("tensor", [("ve", z0)])

        NG = CNC // GRP
        if OPT_P1_STOP < 0:
            NG = 0
        NG = min(NG, OPT_NG)
        vseq = 0
        h2v = 0
        for g in range(NG):
            for ci in range(GRP):
                c = g * GRP + ci
                w0 = c * 128
                pemm = ln_stats(w0, 128, F1["SCR2"])
                co = ci * 128
                VTS(RWS[0:1, co:co + 128], PS6[0:1, 0:128], 1.0 / D, None, ALU.mult,
                    waits=[("pe", pemm)])
                VTT(RWS[32:33, co:co + 128], RWS[0:1, co:co + 128],
                    RWS[0:1, co:co + 128], ALU.mult)
                vv = VSTT(RWS[32:33, co:co + 128], PS5[0:1, 0:128], 1.0 / D,
                          RWS[32:33, co:co + 128], ALU.mult, ALU.subtract, tick=True)
                TMP = RWS[32:33, 256 + co:256 + co + 128]
                ACT(TMP, RWS[32:33, co:co + 128], AF.Ln,
                    bias=1e-5, waits=[("ve", vv)])
                ACT(RWS[64:65, co:co + 128], TMP, AF.Exp,
                    scale=-0.5)
                ACT(TMP, TMP, AF.Exp,
                    scale=-1.0, tick=True)
                s2 = P.cnt["se"]
                v3 = VTT(TMP, RWS[32:33, co:co + 128],
                         TMP, ALU.mult,
                         waits=[("se", s2)], tick=True)
                ACT(TMP, TMP, AF.Ln,
                    bias=1e-6, waits=[("ve", v3)])
                ACT(TMP, TMP, AF.Exp,
                    scale=-0.5, tick=True)
                s3 = P.cnt["se"]
                VCP(RWS[64:65, 256 + co:256 + co + 128], TMP, waits=[("se", s3)])
                VTT(RWS[64:65, 256 + co:256 + co + 128],
                    RWS[64:65, co:co + 128],
                    RWS[64:65, 256 + co:256 + co + 128], ALU.mult)
                VCP(RWR[0:1, co:co + 128], RWS[0:1, co:co + 128])
                VCP(RWR[32:33, co:co + 128], RWS[64:65, co:co + 128])
                v4 = VCP(RWR[64:65, co:co + 128],
                         RWS[64:65, 256 + co:256 + co + 128], tick=True)
                MM(PS4[:, 0:128], ONESR_R[0:1, :], RWR[0:1, co:co + 128],
                   waits=[("ve", v4)])
                MM(PS5[:, 0:128], ONESR_R[32:33, :], RWR[32:33, co:co + 128])
                bc = MM(PS01[:, 0:128], ONESR_R[64:65, :], RWR[64:65, co:co + 128],
                        tick=True)
                for dj in range(NDJ):
                    xs = RES[:, dj * T + w0: dj * T + w0 + 128]
                    VTT(SCR2(0), xs, PS4[:, 0:128], ALU.subtract,
                        waits=[("pe", bc)] if dj == 0 else ())
                    VTT(ZR(ci, dj), SCR2(0), PS5[:, 0:128], ALU.mult)
                    VTT(SCR2(1), SCR2(0), PS01[:, 0:128], ALU.mult)
                    VTS(HB1(ci, dj), SCR2(1), CSM[:, 16 + dj:17 + dj], None,
                        ALU.mult, tick=(dj == NDJ - 1))
                vseq = P.cnt["ve"]
                P.wait("tensor", [("ve", vseq)])

            if OPT_P1_STOP < 1:
                continue
            silv = [0, 0]
            for ci in range(GRP):
                for ip, wsl in enumerate([QWB, KWB]):
                    pd = PS01 if ip % 2 == 0 else PS23
                    qsv = 0
                    for n2 in range(2):
                        for dj in range(NDJ):
                            ws = []
                            if dj == 0:
                                ws = [("se", silv[ip % 2]), ("ve", vseq)]
                            MM(pd[:, n2 * 512:(n2 + 1) * 512], HB1(ci, dj),
                               wsl(dj, n2 * 512), start=(dj == 0),
                               stop=(dj == NDJ - 1), waits=ws,
                               tick=(dj == NDJ - 1))
                        mmt = P.cnt["pe"]
                        silv[ip % 2] = ACT(QS(ip)[:, n2 * 512:(n2 + 1) * 512],
                                           pd[:, n2 * 512:(n2 + 1) * 512], AF.Silu,
                                           waits=[("pe", mmt)], tick=True)
                    dst = QNs if ip == 0 else KNs
                    sso = 32 + ci * 32 + ip * 8
                    for hh in range(NH):
                        VSTT(SCR2(0), QS(ip)[:, hh * 128:(hh + 1) * 128], 1.0,
                             QS(ip)[:, hh * 128:(hh + 1) * 128], ALU.mult, ALU.mult,
                             accum=SCc(ci, 64 + ip * 8 + hh, 1),
                             waits=[("se", silv[ip % 2])] if hh == 0 else ())
                    vseq = VCP(dst(ci, 0, 1024), QS(ip)[:, 0:1024], tick=True)
                for ip, wsl in enumerate([VWB, GWB]):
                    pd = PS01 if ip % 2 == 0 else PS23
                    dst = VNs if ip == 0 else GNs
                    for n2 in range(2):
                        for dj in range(NDJ):
                            ws = []
                            if dj == 0:
                                ws = [("se", silv[ip % 2])]
                            MM(pd[:, n2 * 512:(n2 + 1) * 512], HB1(ci, dj),
                               wsl(dj, n2 * 512), start=(dj == 0),
                               stop=(dj == NDJ - 1), waits=ws,
                               tick=(dj == NDJ - 1))
                        mmt = P.cnt["pe"]
                        silv[ip % 2] = ACT(dst(ci, n2 * 512, 512),
                                           pd[:, n2 * 512:(n2 + 1) * 512], AF.Silu,
                                           waits=[("pe", mmt)], tick=True)
                for dj in range(NDJ):
                    MM(PS6[:, 0:16], HB1(ci, dj), BWAs(dj),
                       start=(dj == 0), stop=(dj == NDJ - 1),
                       waits=[("ve", vseq)] if dj == 0 else (),
                       tick=(dj == NDJ - 1))
                bat = P.cnt["pe"]
                if dbg:
                    VCP(SC8[:, GRP * 80 + 8: GRP * 80 + 16], PS6[:, 8:16],
                        waits=[("pe", bat)])
                tb = ACT(SCR2(0, 8), PS6[:, 0:8], AF.Tanh, scale=0.5,
                         waits=[("pe", bat)], tick=True)
                VTS(SCc(ci, 0), SCR2(0, 8), 0.5, 0.5, ALU.mult, ALU.add,
                    waits=[("se", tb)])
                VSEP()
                VTS(SCc(ci, 8), SCc(ci, 0), -1.0, None, ALU.mult)

                vseq = VCP(SCc(ci, 40), PS6[:, 8:16], tick=True)
                P.wait("tensor", [("ve", vseq)])

            if OPT_P1_STOP < 2:
                continue
            P.serialize = ("C" in OPT_SER) or ("ALL" in OPT_SER)
            for ci in range(GRP):
                c = g * GRP + ci
                v5 = VTT(SCR2(0, 8), SCc(ci, 40), SMALL[:, 8:16], ALU.add,
                         waits=[("se", P.cnt["se"])], tick=True)
                if dbg:
                    VCP(SC8[:, GRP * 80 + ci * 8: GRP * 80 + 8 + ci * 8],
                        SCc(ci, 40))
                    VCP(SC8[:, GRP * 80 + 16 + ci * 8 if False else GRP * 80 + 8 + ci * 8 - 8 + 8: GRP * 80 + 16 + ci * 8 - 8 + 8] if False else SC8[:, 0:0], SC8[:, 0:0]) if False else None
                ACT(SCR2(1, 8), SCR2(0, 8), AF.Abs, waits=[("ve", v5)])
                ACT(SCR2(1, 8), SCR2(1, 8), AF.Exp, scale=-1.0)
                sp1 = ACT(SCR2(1, 8), SCR2(1, 8), AF.Ln, bias=1.0, tick=True)
                VTS(SCR2(2, 8), SCR2(0, 8), 0.0, None, ALU.max, waits=[("se", sp1)])
                VSEP()
                VTT(SCR2(2, 8), SCR2(2, 8), SCR2(1, 8), ALU.add)
                VSEP()
                v6 = VTT(SCc(ci, 16), SCR2(2, 8), SMALL[:, 0:8], ALU.mult, tick=True)
                MM(PS6[:, 64:72], UI[:, :], SCc(ci, 16),
                   waits=[("ve", v6), ("se", P.cnt["se"])])
                ct = MM(PS6[0:1, 96:104], ONESC[:, :], SCc(ci, 16), tick=True)
                VCP(SCc(ci, 24), PS6[:, 64:72], waits=[("pe", ct)])
                vtt_ = VCP(TOTR[0:1, c * 8:(c + 1) * 8], PS6[0:1, 96:104], tick=True)
                a1 = ACT(SCc(ci, 32), PS6[:, 64:72], AF.Exp,
                         waits=[("pe", ct)], tick=True)
                VSEP()
                v7 = VCP(TOTRR[0:1, c * 8:(c + 1) * 8], TOTR[0:1, c * 8:(c + 1) * 8],
                         waits=[("ve", vtt_)], tick=True)
                tb8 = MM(PS6[:, 104:112], ONESR_R[0:1, :],
                         TOTRR[0:1, c * 8:(c + 1) * 8],
                         waits=[("ve", v7), ("se", a1)], tick=True)
                v8 = VTT(SCR2(0, 8), PS6[:, 104:112], SCc(ci, 24), ALU.subtract,
                         waits=[("pe", tb8)], tick=True)
                ACT(SCc(ci, 48), SCR2(0, 8), AF.Exp, waits=[("ve", v8)])
                ac1 = ACT(SCc(ci, 56), PS6[:, 104:112], AF.Exp, tick=True)
                VTT(SCc(ci, 40), SCc(ci, 8), SCc(ci, 32), ALU.mult,
                    waits=[("se", ac1)])
                lq = ACT(SCR2(0, 16), SCc(ci, 64, 16), AF.Ln, bias=1e-6)
                eq = ACT(SCR2(0, 16), SCR2(0, 16), AF.Exp, scale=-0.5, tick=True)
                for hh in range(NH):
                    VTS(QNs(ci, hh * 128, 128), QNs(ci, hh * 128, 128),
                        SCR2(0, 16)[:, hh:hh + 1], None, ALU.mult,
                        waits=[("se", eq)] if hh == 0 else ())
                    VTS(KNs(ci, hh * 128, 128), KNs(ci, hh * 128, 128),
                        SCR2(0, 16)[:, 8 + hh:9 + hh], None, ALU.mult,
                        tick=(hh == NH - 1))
                vseq = P.cnt["ve"]

                if OPT_P1_STOP < 3:
                    continue
                ofmv = [0] * NH
                for h in range(NH):
                    v9 = VTS(ABS(24), KNs(ci, h * 128, 128), SCc(ci, 8)[:, h:h + 1],
                             None, ALU.mult, waits=[("ve", vseq)] if h == 0 else (),
                             tick=True)
                    TR(PS7B[:, 0:128], QNs(ci, h * 128, 128), waits=[("ve", v9)])
                    TR(PS7B[:, 128:256], KNs(ci, h * 128, 128))
                    t1 = TR(PS7B[:, 256:384], ABS(24), tick=True)
                    v10 = VCP(ABS(0, 384), PS7B[:, 0:384], waits=[("pe", t1)],
                              tick=True)
                    VTS(SCR2(0), UI[:, :], SCc(ci, 16)[:, h:h + 1], None, ALU.mult,
                        waits=[("se", P.cnt["se"])] if h == 0 else ())
                    v11 = VTS(SCR2(1), MST[:, :], SCc(ci, 16)[:, h:h + 1], None,
                              ALU.mult, tick=True)
                    MM(PS01[:, 0:128], SU[:, :], SCR2(0),
                       waits=[("ve", v11), ("se", P.cnt["se"])])
                    x2 = MM(PS01[:, 512:640], UI[:, :], SCR2(1), tick=True)
                    MM(PS4[:, 128:256], ABS(1), ABS(2),
                       waits=[("ve", v10)])
                    MM(PS4[:, 256:384], ABS(1), ABS(0))
                    kk2 = MM(PS5[:, 128:256], ABS(2), ABS(1), tick=True)
                    xe1 = ACT(XET, PS01[:, 0:128], AF.Exp, waits=[("pe", x2)],
                              tick=True)
                    xe2 = ACT(SCR2(2), PS01[:, 512:640], AF.Exp, tick=True)
                    VTT(SCR2(3), PS4[:, 128:256], XET, ALU.mult,
                        waits=[("se", xe1), ("pe", kk2)])
                    VTT(ABS(3), SCR2(3), MSK[:, :], ALU.mult)
                    VTT(SCR2(3), PS4[:, 256:384], XET, ALU.mult)
                    VTT(ABS(9), SCR2(3), UI[:, :], ALU.mult)
                    VTT(SCR2(3), PS5[:, 128:256], SCR2(2), ALU.mult,
                        waits=[("se", xe2)])
                    VTT(ABS(4), SCR2(3), MST[:, :], ALU.mult)
                    nv0 = VTT(ABS(7), IDB[:, :], ABS(3), ALU.add, tick=True)
                    P.wait("tensor", [("ve", nv0)])
                    qcur, qnxt = 7, 8
                    ncur, tcur, nnxt, tnxt = 3, 4, 5, 6
                    for k in range(6):
                        MM(PS4[:, 0:128], ABS(tcur), ABS(ncur))
                        m1_ = MM(PS5[:, 0:128], ABS(ncur), ABS(tcur), tick=True)
                        ev1 = VCP(ABS(tnxt), PS5[:, 0:128], waits=[("pe", m1_)],
                                  tick=True)
                        m2_ = MM(PS5[:, 128:256], ABS(tnxt), ABS(qcur),
                                 waits=[("ve", ev1)], tick=True)
                        VCP(ABS(nnxt), PS4[:, 0:128])
                        qv = VTT(ABS(qnxt), PS5[:, 128:256], ABS(qcur), ALU.add,
                                 waits=[("pe", m2_)], tick=True)
                        P.wait("tensor", [("ve", qv)])
                        qcur, qnxt = qnxt, qcur
                        ncur, nnxt = nnxt, ncur
                        tcur, tnxt = tnxt, tcur
                    WINV = ABS(qcur)
                    if OPT_P1_STOP < 4:
                        ofmv[h] = P.cnt["ve"]
                        continue
                    sv0 = VCP(ABS(13), S0[:, h * 128:(h + 1) * 128], tick=True)
                    ks0 = MM(PS23[:, 0:128], ABS(1), ABS(13), waits=[("ve", sv0)],
                             tick=True)
                    VTS(ABS(11), VNs(ci, h * 128, 128), SCc(ci, 0)[:, h:h + 1],
                        None, ALU.mult)
                    rv_ = VSTT(ABS(12), PS23[:, 0:128], SCc(ci, 40)[:, h:h + 1],
                               ABS(11), ALU.mult, ALU.add,
                               waits=[("pe", ks0)], tick=True)
                    up = MM(PS23[:, 512:640], WINV, ABS(12), waits=[("ve", rv_)],
                            tick=True)
                    uv = VCP(ABS(10), PS23[:, 512:640], waits=[("pe", up)], tick=True)
                    MM(PS23[:, 128:256], ABS(9), ABS(10), waits=[("ve", uv)])
                    o2 = MM(PS23[:, 256:384], ABS(0), ABS(13), tick=True)
                    VTS(SCR2(3), PS23[:, 256:384], SCc(ci, 32)[:, h:h + 1], None,
                        ALU.mult, waits=[("pe", o2)])
                    ov = VTT(OTM, PS23[:, 128:256], SCR2(3), ALU.add, tick=True)
                    kt_ = VTS(ABS(14), KNs(ci, h * 128, 128),
                              SCc(ci, 48)[:, h:h + 1], None, ALU.mult, tick=True)
                    sp_ = MM(PS23[:, 640:768], ABS(14), ABS(10),
                             waits=[("ve", kt_)], tick=True)
                    s0v = VSTT(S0[:, h * 128:(h + 1) * 128],
                               S0[:, h * 128:(h + 1) * 128],
                               SCc(ci, 56)[:, h:h + 1], PS23[:, 640:768],
                               ALU.mult, ALU.add, waits=[("pe", sp_)], tick=True)
                    if OPT_P1_STOP < 5:
                        ofmv[h] = s0v
                        continue
                    VTT(SCR2(3), OTM, OTM, ALU.mult, waits=[("ve", ov)])
                    VRD(AFa[:, 32:33], SCR2(3), ALU.add)
                    VSEP()
                    vsq = VSEP(tick=True)
                    ACT(AFa[:, 33:34], AFa[:, 32:33], AF.Ln, scale=1.0 / 128,
                        bias=1e-6, waits=[("ve", vsq)])
                    on1 = ACT(AFa[:, 33:34], AFa[:, 33:34], AF.Exp, scale=-0.5,
                              tick=True)
                    VTS(OTM, OTM, AFa[:, 33:34], None, ALU.mult, waits=[("se", on1)])
                    VTT(OTM, OTM, ONB[:, :], ALU.mult)
                    obv = VTT(ABS(15), OTM, GNs(ci, h * 128, 128), ALU.mult,
                              tick=True)
                    tro = TR(PS7B[:, 384:512], ABS(15), waits=[("ve", obv)],
                             tick=True)
                    ofmv[h] = VCP(ABS(16 + h), PS7B[:, 384:512],
                                  waits=[("pe", tro)], tick=True)
                    vseq = ofmv[h]
                if OPT_P1_STOP < 6:
                    continue
                if "OW" in OPT_SER or "O1" in OPT_SER:
                    P.serialize = True
                w0 = c * 128
                rvv = 0
                x1v = [0] * NDJ
                for do in range(NDJ):
                    pd = PS5 if do % 2 == 0 else PS23
                    po = 0 if do % 2 == 0 else 384
                    for h in range(NH):
                        ws = []
                        if h == 0 and do == 0:
                            ws = [("ve", max(ofmv))]
                        elif h == 0 and do >= 2:
                            ws = [("ve", x1v[do - 2])]
                        MM(pd[:, po:po + 128], OWBs(h, do), ABS(16 + h),
                           start=(h == 0), stop=(h == NH - 1), waits=ws,
                           tick=(h == NH - 1))
                    owt = P.cnt["pe"]
                    x1v[do] = VSTT(X1P(do), pd[:, po:po + 128], 0.5, ZR(ci, do),
                                   ALU.mult, ALU.add, waits=[("pe", owt)], tick=True)
                    rs = RES[:, do * T + w0: do * T + w0 + 128]
                    VTT(rs, rs, X1P(do), ALU.add)
                if "O1" in OPT_SER:
                    P.serialize = False
                if "O2" in OPT_SER:
                    P.serialize = True
                if OPT_P1_STOP < 7:
                    continue
                sqv = [0, 0]
                smv2 = [0] * NDJ
                for dj in range(NDJ):
                    sqv[dj % 2] = ACT(SCR2(dj % 2), X1P(dj), AF.Square,
                                      waits=([("pe", smv2[dj - 2])] if dj >= 2
                                             else [("ve", x1v[dj])]), tick=True)
                    smv2[dj] = MM(PS01[0:1, 0:128], ONESC[:, :], SCR2(dj % 2),
                                  start=(dj == 0), stop=(dj == NDJ - 1),
                                  waits=[("se", sqv[dj % 2])], tick=True)
                ACT(RWS[0:1, 384:512], PS01[0:1, 0:128], AF.Ln, scale=1.0 / D,
                    bias=1e-6, waits=[("pe", smv2[NDJ - 1])])
                r2s = ACT(RWS[0:1, 384:512], RWS[0:1, 384:512], AF.Exp, scale=-0.5,
                          tick=True)
                r2v = VCP(RWR[0:1, 384:512], RWS[0:1, 384:512],
                          waits=[("se", r2s)], tick=True)
                rb2 = MM(PS6[:, 0:128], ONESR_R[0:1, :], RWR[0:1, 384:512],
                         waits=[("ve", r2v)], tick=True)
                if OPT_P1_STOP < 8:
                    continue
                for dj in range(NDJ):
                    VTT(SCR2(0), X1P(dj), PS6[:, 0:128], ALU.mult,
                        waits=([("pe", rb2), ("ldg", h2v)] if dj == 0 else ()))
                    VTS(ABS(25 + dj), SCR2(0), CSM[:, 24 + dj:25 + dj], None,
                        ALU.mult, tick=(dj == NDJ - 1))
                if "OW" in OPT_SER:
                    P.serialize = False
                h2t = P.cnt["ve"]
                if not OPT_NO_H2D:
                    for dj in range(NDJ):
                        h2v = P.dma("sync",
                                    h2d[:, dj * T + w0: dj * T + w0 + 128],
                                    AB[:, SCN + (25 + dj) * 128:
                                       SCN + (26 + dj) * 128],
                                    waits=[("ve", h2t)] if dj == 0 else (),
                                    sem="ldg")
                vseq = h2t
        P.serialize = False
        if not OPT_NO_H2D:
            P.wait("sync", [("ldg", h2v)])

    def emit_ffn2():
        P.barrier()
        pwv = stage(WBF, 0, dr["pw"], 32768)
        FFAs = lambda ff: AB[:, A2["FFA"] + ff * P2W: A2["FFA"] + (ff + 1) * P2W]
        H2Ls = lambda dj: AB[:, A2["H2L"] + dj * P2W: A2["H2L"] + (dj + 1) * P2W]
        FCRs = lambda r: AB[:, A2["FCR"] + r * 1024: A2["FCR"] + (r + 1) * 1024]
        PWB = lambda ff, do: WBF[:, ff * 1024 + do * 128: ff * 1024 + (do + 1) * 128]
        fcv = {r: 0 for r in range(4)}
        fcu = {r: 0 for r in range(4)}
        h2ld = 0
        silv = [0, 0]
        evv = [0, 0]
        for tc in range(NP2):
            w0 = tc * P2W
            for dj in range(NDJ):
                h2ld = P.dma("sync",
                             AB[:, A2["H2L"] + dj * P2W: A2["H2L"] + (dj + 1) * P2W],
                             h2d[:, dj * T + w0: dj * T + w0 + P2W],
                             waits=([("ve", evv[0]), ("ve", evv[1])]
                                    if dj == 0 else ()), sem="ldh")
            for ff in range(32):
                r = ff % 4
                par = stg["par"]
                stg["par"] ^= 1
                sem = "sga" if par == 0 else "sgb"
                for dj in range(NDJ):
                    ldv = P.dma("sync", STG[:, par * 1024 + dj * 128:
                                            par * 1024 + (dj + 1) * 128],
                                dr["fcw"][:, dj * 4096 + ff * 128:
                                          dj * 4096 + (ff + 1) * 128],
                                waits=([("ve", stg["cast"][par])]
                                       if dj == 0 else ()), sem=sem)
                cv = VCP(FCRs(r), STG[:, par * 1024: par * 1024 + 1024],
                         waits=[(sem, ldv), ("pe", fcu[r])], tick=True)
                stg["cast"][par] = cv
                fcv[r] = cv
                pd = PS01 if ff % 2 == 0 else PS23
                for dj in range(NDJ):
                    ws = []
                    if dj == 0:
                        ws = [("ve", fcv[r]), ("se", silv[ff % 2]),
                              ("ldh", h2ld)]
                    MM(pd[:, 0:P2W], FCRs(r)[:, dj * 128:(dj + 1) * 128],
                       H2Ls(dj), start=(dj == 0), stop=(dj == NDJ - 1),
                       waits=ws, tick=(dj == NDJ - 1))
                fcu[r] = P.cnt["pe"]
                silv[ff % 2] = ACT(FFAs(ff), pd[:, 0:P2W], AF.Silu,
                                   waits=[("pe", fcu[r])], tick=True)
            sall = max(silv)
            for do in range(NDJ):
                pd = PS4 if do % 2 == 0 else PS5
                for ff in range(32):
                    ws = []
                    if ff == 0:
                        ws = [("se", sall), ("ve", evv[do % 2]),
                              ("ve", pwv)]
                    MM(pd[:, 0:P2W], PWB(ff, do), FFAs(ff),
                       start=(ff == 0), stop=(ff == 31), waits=ws,
                       tick=(ff == 31))
                pwt = P.cnt["pe"]
                VTS(AFa[:, 64:64 + P2W], pd[:, 0:P2W], 0.5, None, ALU.mult,
                    waits=[("pe", pwt)])
                rs = RES[:, do * T + w0: do * T + w0 + P2W]
                evv[do % 2] = VTT(rs, rs, AFa[:, 64:64 + P2W], ALU.add, tick=True)
        P.wait("tensor", [("ve", max(evv))])

    if phases[0]:
        emit_ffn()
    if phases[1]:
        emit_attn()
    if phases[2]:
        emit_ffn2()
    if phases[3]:
        emit_ffn()

    P.barrier()
    stv = 0
    for dj in range(NDJ):
        stv = P.dma("sync", y[:, dj * T:(dj + 1) * T], RES[:, dj * T:(dj + 1) * T],
                    sem="st")
    if dbg:
        dab = nc.dram_tensor("dbgab", [128, 24576], BF16, kind="ExternalOutput").ap()
        dwt = nc.dram_tensor("dbgwt", [128, NTT * 128], F32, kind="ExternalOutput").ap()
        dsc = nc.dram_tensor("dbgsc", [128, GRP * 80 + 16], F32, kind="ExternalOutput").ap()
        ds0 = nc.dram_tensor("dbgs0", [128, NH * 128], F32, kind="ExternalOutput").ap()
        dst_ = nc.dram_tensor("dbgstg", [128, 2048], F32, kind="ExternalOutput").ap()
        dwb = nc.dram_tensor("dbgwbf", [128, 32768], BF16, kind="ExternalOutput").ap()
        daf = nc.dram_tensor("dbgaf", [128, 3904], F32, kind="ExternalOutput").ap()
        dsm = nc.dram_tensor("dbgsm", [128, 96], F32, kind="ExternalOutput").ap()
        for a in range(0, 24576, 8192):
            stv = P.dma("sync", dab[:, a:a + 8192], AB[:, a:a + 8192], sem="st")
        stv = P.dma("sync", dwt, WTT[:, :], sem="st")
        stv = P.dma("sync", dsc, SC8[:, :], sem="st")
        stv = P.dma("sync", ds0, S0[:, :], sem="st")
        stv = P.dma("sync", dst_, STG[:, :], sem="st")
        for a in range(0, 32768, 8192):
            stv = P.dma("sync", dwb[:, a:a + 8192], WBF[:, a:a + 8192], sem="st")
        stv = P.dma("sync", daf, AFa[:, :], sem="st")
        stv = P.dma("sync", dsm[:, 0:16], SMALL[:, :], sem="st")
        stv = P.dma("sync", dsm[0:1, 16:80], ROW8[0:1, 0:64], sem="st")
    P.wait("sync", [("st", stv)])

    with nc.Block() as block:
        P.emit(block)
    return nc


def _fm(w):
    n = w.shape[1]
    return np.ascontiguousarray(w.reshape(8, 128, n).transpose(1, 0, 2)
                                .reshape(128, 8 * n))


def _prep_inputs(inputs, T):
    f32 = np.float32
    g = {k: np.asarray(v, dtype=f32) for k, v in inputs.items()}
    shared = {
        "wm": _fm(g["L0_ffn_Wm"]),
        "w1t": np.ascontiguousarray(g["L0_ffn_W1"].transpose(1, 0, 2)
                                    .reshape(128, 16 * 512)),
        "w2r": np.ascontiguousarray(g["L0_ffn_W2"].reshape(64, 128, 128)
                                    .transpose(1, 0, 2).reshape(128, 64 * 128)),
        "wo": _fm(g["L0_ffn_Wo"]),
        "qw": _fm(g["L0_attn_qw"]),
        "kw": _fm(g["L0_attn_kw"]),
        "vw": _fm(g["L0_attn_vw"]),
        "gw": _fm(g["L0_attn_gw"]),
        "ow": _fm(g["L0_attn_ow"]),
        "fcw": _fm(g["L0_attn_fcw"]),
        "pw": np.ascontiguousarray(g["L0_attn_pw"].reshape(32, 128, 1024)
                                   .transpose(1, 0, 2).reshape(128, 32 * 1024)),
        "bwaw": _fm(np.concatenate([g["L0_attn_bw"], g["L0_attn_aw"]], axis=1)),
        "emb": np.ascontiguousarray(g["L0_ffn_expert_emb"]),
        "b2": np.ascontiguousarray(g["L0_ffn_b2"]),
        "b1c": np.ascontiguousarray(g["L0_ffn_b1"].reshape(16, 4, 128)
                                    .transpose(2, 0, 1).reshape(128, 64)),
        "bmc": np.ascontiguousarray(g["L0_ffn_bm"].reshape(8, 128).T),
        "boc": np.ascontiguousarray(g["L0_ffn_bo"].reshape(8, 128).T),
        "ln1c": np.ascontiguousarray(g["L0_attn_ln1_w"].reshape(8, 128).T),
        "ln2c": np.ascontiguousarray(g["L0_attn_ln2_w"].reshape(8, 128).T),
        "alog": g["L0_attn_A_log"].reshape(1, 8),
        "dtb": g["L0_attn_dt_bias"].reshape(1, 8),
        "onormr": g["L0_attn_onorm_w"].reshape(1, 128),
        "identc": np.eye(128, dtype=f32),
        "suc": np.tril(np.ones((128, 128), f32), -1),
        "uic": np.triu(np.ones((128, 128), f32), 0),
        "msc": np.triu(np.ones((128, 128), f32), 1),
        "mstc": np.tril(np.ones((128, 128), f32), -1),
        "onesc": np.ones((128, 1), f32),
        "onesr": np.ones((128, 128), f32),
        "fconst": np.tile(np.array(FCONST_VALS, f32), (128, 1)),
    }
    shared = {k: np.ascontiguousarray(v, dtype=f32) for k, v in shared.items()}
    x = g["L0_attn_x"]
    in_maps = []
    for i in range(x.shape[0]):
        m = dict(shared)
        m["x"] = _fm(np.ascontiguousarray(x[i].T))
        in_maps.append(m)
    return in_maps


def kernel(**inputs):
    x = np.asarray(inputs["L0_attn_x"])
    B, T, _ = x.shape
    nc = build_program(T=T)
    in_maps = _prep_inputs(inputs, T)
    res = run_bass_kernel_spmd(nc, in_maps, core_ids=list(range(B)))
    outs = []
    for i in range(B):
        yy = res.results[i]["y"]
        full = yy.reshape(128, 8, T).transpose(1, 0, 2).reshape(1024, T)
        outs.append(full.T)
    return np.ascontiguousarray(np.stack(outs, axis=0), dtype=np.float32)
